# version 04
# baseline (speedup 1.0000x reference)
import sys
for _pth in ("/opt/trn_rl_repo",):
    if _pth not in sys.path:
        sys.path.insert(0, _pth)
import math
import contextlib
import numpy as np
import ml_dtypes
import concourse.bass as bass
import concourse.mybir as mybir
from concourse.bass_utils import run_bass_kernel_spmd

F32 = mybir.dt.float32
BF16 = mybir.dt.bfloat16
AF = mybir.ActivationFunctionType
ALU = mybir.AluOpType
BF16NP = ml_dtypes.bfloat16

D = 1024
T = 2048
B = 16
NCORES = 8
ROWS_PER_CORE = B // NCORES
NDT = D // 128

ENGINES = ("sync", "tensor", "vector", "scalar", "gpsimd")


class Ev:
    __slots__ = ("stream", "target")

    def __init__(self, stream, target):
        self.stream, self.target = stream, target


class Prog:

    def __init__(self, serial=False):
        self.nc = bass.Bass("TRN2", target_bir_lowering=False)
        self.es = contextlib.ExitStack()
        self.ops = {e: [] for e in ENGINES}
        self.counts = {}
        self._sems = {}
        self.waited = {e: {} for e in ENGINES}
        self.serial = serial
        self._last_ev = None
        self.tensors = {}

    def sbuf(self, name, shape, dtype):
        es = getattr(self, "_scope", None) or self.es
        if getattr(self, "_scope", None) is not None:
            name = f"{name}_s{self._scope_id}"
        t = es.enter_context(self.nc.sbuf_tensor(name, list(shape), dtype))
        self.tensors[name] = t
        return t

    def psum(self, name, shape, dtype=F32):
        es = getattr(self, "_scope", None) or self.es
        if getattr(self, "_scope", None) is not None:
            name = f"{name}_s{self._scope_id}"
        t = es.enter_context(self.nc.psum_tensor(name, list(shape), dtype))
        self.tensors[name] = t
        return t

    def push_scope(self):
        assert getattr(self, "_scope", None) is None
        self._scope = contextlib.ExitStack()
        self._scope_id = getattr(self, "_scope_id", 0) + 1

    def pop_scope(self):
        self._scope.close()
        self._scope = None

    def dram_in(self, name, shape, dtype=F32):
        return self.nc.dram_tensor(name, list(shape), dtype, kind="ExternalInput").ap()

    def dram_out(self, name, shape, dtype=F32):
        return self.nc.dram_tensor(name, list(shape), dtype, kind="ExternalOutput").ap()

    SEM_CAP = 24000

    def _stream(self, name):
        if name not in self._sems:
            self._sems[name] = self.es.enter_context(self.nc.semaphore(name))
            self.counts[name] = 0
        return self._sems[name]

    def _gen_stream(self, base, n):
        gens = getattr(self, "_gens", None)
        if gens is None:
            gens = self._gens = {}
        g = gens.get(base, 0)
        name = f"{base}@{g}" if g else base
        self._stream(name)
        if self.counts[name] + n > self.SEM_CAP:
            g += 1
            gens[base] = g
            name = f"{base}@{g}"
            self._stream(name)
        return name

    def emit(self, engine, fn, waits=(), sig=False, dma=False,
             force_waits=()):
        assert not getattr(self, "_finalizing", False), \
            "emit() during finalize (op closure called p.emit/constcol?)"
        wl = []
        waits = [w for w in waits if w is not None]
        if self.serial and self._last_ev is not None:
            waits = waits + [self._last_ev]
        for ev in force_waits:
            if ev is None:
                continue
            wl.append((ev.stream, ev.target))
            self.waited[engine][ev.stream] = max(
                ev.target, self.waited[engine].get(ev.stream, -1))
        for ev in waits:
            if ev.stream.split("@")[0] == "s_" + engine:
                continue
            prev = self.waited[engine].get(ev.stream, -1)
            if ev.target <= prev:
                continue
            self.waited[engine][ev.stream] = ev.target
            wl.append((ev.stream, ev.target))
        ev_out = None
        if sig or self.serial:
            n = 16 if dma else 1
            stream = self._gen_stream(("d_" if dma else "s_") + engine, n)
            self.counts[stream] += n
            ev_out = Ev(stream, self.counts[stream])
            self.ops[engine].append((wl, fn, (stream, n)))
            self._last_ev = ev_out
        else:
            self.ops[engine].append((wl, fn, None))
        return ev_out

    def _ensure_consts(self):
        if not hasattr(self, "_cvals"):
            self._cvals = {}
            self._ctensor = self.es.enter_context(
                self.nc.sbuf_tensor("c_consts", [128, 16], F32))
            self.const_evs = []

    def constcol(self, val):
        self._ensure_consts()
        if val not in self._cvals:
            idx = len(self._cvals)
            assert idx < 16
            self._cvals[val] = idx
            ap = self._ctensor[:, idx:idx + 1]
            self.const_evs.append(self.emit(
                "vector", lambda v, ap=ap, val=val: v.memset(ap, float(val)),
                sig=True))
        idx = self._cvals[val]
        return self._ctensor[:, idx:idx + 1]

    def debug_dump(self, name, ap, waits=()):
        shp = list(ap.shape)
        out = self.dram_out("dbg_" + name, shp, ap.dtype)
        return self.dma("scalar", out, ap, waits=waits)

    def dma(self, engine, out, in_, waits=(), **kw):
        return self.emit(engine, lambda e: e.dma_start(out, in_, **kw),
                         waits=waits, sig=True, dma=True)

    def finalize(self, final_waits=()):
        self._finalizing = True
        for ev in final_waits:
            if ev is not None:
                self._stream(ev.stream)
                self.ops["sync"].append(([(ev.stream, ev.target)], None, None))
        block = self.es.enter_context(self.nc.Block())
        for ename in ENGINES:
            ops = self.ops[ename]
            if not ops:
                continue

            def body(eng, ops=ops):
                for wl, fn, inc in ops:
                    for (stream, target) in wl:
                        eng.wait_ge(self._sems[stream], target)
                    if fn is None:
                        continue
                    inst = fn(eng)
                    if inc is not None:
                        stream, n = inc
                        inst.then_inc(self._sems[stream], n)
            getattr(block, ename)(body)
        return self.nc

    def close(self):
        self.es.close()



def host_tables():
    return {
        "c_ident": np.eye(128, dtype=BF16NP),
        "c_ones_col": np.ones((128, 1), dtype=BF16NP),
        "c_ones_row": np.ones((65, 128), dtype=BF16NP),
        "c_onesf": np.ones((1, 128), dtype=np.float32),
    }


def _prep_row_major_w(w, m):
    return np.ascontiguousarray(
        w.reshape(NDT, 128, m).transpose(1, 0, 2).reshape(128, NDT * m))


def _prep_dcol(v):
    return np.ascontiguousarray(v.reshape(NDT, 128).T)


def prep_l0attn(inp):
    return {
        "a_cw": np.ascontiguousarray(
            inp["L0_attn_conv_w"].reshape(3, NDT, 128).transpose(2, 1, 0)
            .reshape(128, NDT * 3)),
        "a_wbc": _prep_row_major_w(inp["L0_attn_W_bc"], 32),
        "a_bbc": np.ascontiguousarray(inp["L0_attn_b_bc"].reshape(32, 1)),
        "a_wdt": _prep_row_major_w(inp["L0_attn_W_dt"], 64),
        "a_bdt": np.ascontiguousarray(inp["L0_attn_b_dt"].reshape(64, 1)),
        "a_wdtp": np.ascontiguousarray(inp["L0_attn_W_dtp"]),
        "a_bdtp": _prep_dcol(inp["L0_attn_b_dtp"]),
        "a_alog": _prep_row_major_w(inp["L0_attn_A_log"], 16),
        "a_dcoef": _prep_dcol(inp["L0_attn_Dcoeff"]),
    }


L0ATTN_WEIGHT_DECLS = [
    ("a_cw", (128, NDT * 3), F32),
    ("a_wbc", (128, NDT * 32), BF16),
    ("a_bbc", (32, 1), F32),
    ("a_wdt", (128, NDT * 64), BF16),
    ("a_bdt", (64, 1), F32),
    ("a_wdtp", (64, 1024), BF16),
    ("a_bdtp", (128, NDT), F32),
    ("a_alog", (128, 128), F32),
    ("a_dcoef", (128, NDT), F32),
]


def declare_weights(p, decls):
    sb = {}
    evs = []
    for name, shape, dt in decls:
        ap = p.dram_in(name, list(shape), F32)
        t = p.sbuf(name + "_sb", list(shape), dt)
        if dt == F32:
            evs.append(p.dma("sync", t[:], ap))
        else:
            evs.append(p.dma("gpsimd", t[:], ap))
        sb[name] = t
    return sb, evs


def declare_tables(p, tabs):
    aps = {}
    evs = []
    if not hasattr(p, "_tab_drams"):
        p._tab_drams = {}
    for k, v in tabs.items():
        dt = BF16 if v.dtype == BF16NP else F32
        if k not in p._tab_drams:
            p._tab_drams[k] = p.dram_in(k, list(v.shape), dt)
        ap = p._tab_drams[k]
        t = p.sbuf(k + "_sb", list(v.shape), dt)
        evs.append(p.dma("sync", t[:], ap))
        aps[k] = t
    return aps, evs



def ln_stats_chunk(p, get_xtile, sqb, xcb, ps1, ps2, rsum, c0, ones_col, waits,
                   is_bf16=False, rsT=T):
    c1 = c0 + 512
    ev = None
    for j in range(NDT):
        xt = get_xtile(j)
        sq = sqb[j % 2]
        if is_bf16:
            xc = xt
            e1 = None
        else:
            xc = xcb[j % 2]
            e1 = p.emit("vector", lambda v, xt=xt, xc=xc: v.tensor_copy(xc[:], xt),
                        waits=waits, sig=True)
        e2 = p.emit("vector", lambda v, xc=xc, sq=sq: v.tensor_mul(
            sq[:], xc[:], xc[:]), waits=list(waits) + [e1], sig=True)
        p.emit("tensor", lambda t, xc=xc, j=j: t.matmul(
            ps1[0:1, 0:512], ones_col[:], xc[:, 0:512],
            start=(j == 0), stop=(j == NDT - 1), skip_group_check=True),
            waits=[e1] if e1 is not None else list(waits), sig=False)
        ev = p.emit("tensor", lambda t, sq=sq, j=j: t.matmul(
            ps2[0:1, 0:512], ones_col[:], sq[:, 0:512],
            start=(j == 0), stop=(j == NDT - 1), skip_group_check=True),
            waits=[e2], sig=(j == NDT - 1))
    e3 = p.emit("vector", lambda v: v.tensor_copy(
        rsum[0:1, c0:c1], ps1[0:1, 0:512]), waits=[ev], sig=True)
    e4 = p.emit("vector", lambda v: v.tensor_copy(
        rsum[0:1, rsT + c0:rsT + c1], ps2[0:1, 0:512]), waits=[ev], sig=True)
    return [e3, e4]


def ln_rows(p, rsum, rw, rowbf, waits, eps=1e-5, dmodel=D, c0=0, c1=T,
            rowf=None, rsT=T, rbc0=None, rbT=T):
    inv = 1.0 / dmodel
    cc_eps = p.constcol(eps)[0:1, 0:1]
    cc_mh = p.constcol(-0.5)[0:1, 0:1]
    evs = []
    if rbc0 is None:
        rbc0 = c0
    for si, s in enumerate(range(c0, c1, 512)):
        sb_ = rbc0 + si * 512
        t0 = rw[0:1, 0:512]
        t1 = rw[0:1, 512:1024]
        t2 = rw[0:1, 1024:1536]
        e1 = p.emit("vector", lambda v, s=s, t0=t0: v.tensor_scalar_mul(
            t0, rsum[0:1, s:s + 512], inv), waits=waits, sig=True)
        e2 = p.emit("vector", lambda v, t0=t0, t1=t1: v.tensor_mul(
            t1, t0, t0), waits=[e1], sig=True)
        e3 = p.emit("vector", lambda v, s=s, t1=t1, t2=t2: v.scalar_tensor_tensor(
            t2, rsum[0:1, rsT + s:rsT + s + 512], inv, t1, ALU.mult,
            ALU.subtract), waits=waits + [e2], sig=True)
        e4 = p.emit("scalar", lambda s_, t1=t1, t2=t2: s_.activation(
            t1, t2, AF.Ln, bias=cc_eps), waits=[e3], sig=True)
        e5 = p.emit("scalar", lambda s_, t1=t1, t2=t2: s_.activation(
            t2, t1, AF.Exp, scale=cc_mh), waits=[e4], sig=True)
        e6 = p.emit("vector", lambda v, t0=t0, t1=t1, t2=t2: v.scalar_tensor_tensor(
            t1, t0, -1.0, t2, ALU.mult, ALU.mult), waits=[e5], sig=True)
        e7 = p.emit("vector", lambda v, sb_=sb_, t2=t2: v.tensor_copy(
            rowbf[0:1, sb_:sb_ + 512], t2), waits=[e5], sig=True)
        e8 = p.emit("vector", lambda v, sb_=sb_, t1=t1: v.tensor_copy(
            rowbf[0:1, rbT + sb_:rbT + sb_ + 512], t1), waits=[e6], sig=True)
        evs += [e7, e8]
        if rowf is not None:
            evs.append(p.emit("vector", lambda v, sb_=sb_, t2=t2: v.tensor_copy(
                rowf[0:1, sb_:sb_ + 512], t2), waits=[e5], sig=True))
            evs.append(p.emit("vector", lambda v, sb_=sb_, t1=t1: v.tensor_copy(
                rowf[0:1, T + sb_:T + sb_ + 512], t1), waits=[e6], sig=True))
    return evs


def bcast_to_psum(p, ps_dst, row_bf, ones_row, waits):
    W_ = row_bf.shape[-1]
    ev = None
    bp = row_bf.base_partition()
    ones = ones_row[bp:bp + 1, :]
    for s0 in range(0, W_, 512):
        s1 = min(s0 + 512, W_)
        ev = p.emit("tensor", lambda t, s0=s0, s1=s1: t.matmul(
            ps_dst[:, s0:s1], ones[:], row_bf[:, s0:s1],
            start=True, stop=True), waits=waits, sig=(s1 == W_))
    return ev


CH = 1024
NCH = T // CH
NST = 16


def emit_l0attn_row(p, bufs, tabs, W, xrow_in, xrow_out):
    xs = bufs["xs"]
    xnb = bufs["xnb"]
    ub = bufs["ub"]
    yacc = bufs["yacc"]
    sqb, xcb = bufs["sqb"], bufs["xcb"]
    t1m = bufs["t1m"]
    dBb = bufs["dBb"]
    btl = bufs["btl"]
    dtb = bufs["dtb"]
    xdtb = bufs["xdtb"]
    bc_sb = bufs["bc_sb"]
    zb = bufs["zb"]
    brow = bufs["brow"]
    carry = bufs["carry"]
    rsum, rw, rowbf = bufs["rsum"], bufs["rw"], bufs["rowbf"]
    ps_a, ps_b = bufs["ps_a"], bufs["ps_b"]
    ps_c, ps_d = bufs["ps_c"], bufs["ps_d"]
    ones_col = tabs["c_ones_col"]
    ones_row = tabs["c_ones_row"]

    def xs_t(j, s0=0, s1=CH):
        return xs[:, j * CH + s0:j * CH + s1]

    def xnb_t(j, c0=0, c1=T):
        return xnb[:, j * T + c0:j * T + c1]

    for ci in range(NCH):
        h0, h1 = ci * CH, (ci + 1) * CH
        eld = p.dma("sync",
                    xs[:].rearrange("p (j t) -> p j t", j=NDT),
                    xrow_in.rearrange("(j p) t -> p j t", p=128)[:, :, h0:h1])
        sevs = []
        for s0 in range(0, CH, 512):
            sevs += ln_stats_chunk(p, lambda j, s0=s0: xs_t(j, s0, s0 + 512),
                                   sqb, xcb, ps_c, ps_d, rsum, h0 + s0,
                                   ones_col, [eld])
        revs = ln_rows(p, rsum, rw, rowbf, sevs, c0=h0, c1=h1)
        eR = bcast_to_psum(p, ps_a[:, 0:CH], rowbf[0:1, h0:h1], ones_row, revs)
        eA = bcast_to_psum(p, ps_b[:, 0:CH], rowbf[0:1, T + h0:T + h1], ones_row, revs)
        for j in range(NDT):
            tmp = t1m[j % 2]
            e1 = p.emit("vector", lambda v, j=j, tmp=tmp: v.tensor_mul(
                tmp[:], xs_t(j), ps_a[:, 0:CH]), waits=[eR, eld], sig=True)
            p.emit("vector", lambda v, j=j, h0=h0, h1=h1, tmp=tmp: v.tensor_add(
                xnb_t(j, h0, h1), tmp[:], ps_b[:, 0:CH]), waits=[e1, eA],
                sig=True)

    for ci in range(NCH):
        h0, h1 = ci * CH, (ci + 1) * CH
        for j in range(NDT):
            w0 = W["a_cw"][:, j * 3 + 0:j * 3 + 1]
            w1 = W["a_cw"][:, j * 3 + 1:j * 3 + 2]
            w2 = W["a_cw"][:, j * 3 + 2:j * 3 + 3]
            cvA, cvB = t1m[0], t1m[1]
            ubj = ub[:, j * CH:(j + 1) * CH]
            e1 = p.emit("vector", lambda v, j=j, cvA=cvA, w1=w1, h0=h0, h1=h1:
                        v.tensor_scalar_mul(cvA[:], xnb_t(j, h0, h1), w1),
                        sig=True)
            if h0 == 0:
                e2 = p.emit("vector", lambda v, j=j, cvB=cvB, cvA=cvA, w0=w0:
                            v.scalar_tensor_tensor(
                                cvB[:, 1:CH], xnb_t(j, 0, CH - 1), w0,
                                cvA[:, 1:CH], ALU.mult, ALU.add),
                            waits=[e1], sig=True)
                e2b = p.emit("vector", lambda v, cvB=cvB, cvA=cvA: v.tensor_copy(
                    cvB[:, 0:1], cvA[:, 0:1]), waits=[e1], sig=True)
            else:
                e2 = p.emit("vector", lambda v, j=j, cvB=cvB, cvA=cvA, w0=w0,
                            h0=h0, h1=h1:
                            v.scalar_tensor_tensor(
                                cvB[:], xnb_t(j, h0 - 1, h1 - 1), w0,
                                cvA[:], ALU.mult, ALU.add),
                            waits=[e1], sig=True)
                e2b = None
            if h1 == T:
                e3 = p.emit("vector", lambda v, j=j, cvB=cvB, cvA=cvA, w2=w2,
                            h0=h0:
                            v.scalar_tensor_tensor(
                                cvA[:, 0:CH - 1], xnb_t(j, h0 + 1, T), w2,
                                cvB[:, 0:CH - 1], ALU.mult, ALU.add),
                            waits=[e2, e2b], sig=True)
                e3b = p.emit("vector", lambda v, cvA=cvA, cvB=cvB: v.tensor_copy(
                    cvA[:, CH - 1:CH], cvB[:, CH - 1:CH]), waits=[e2], sig=True)
            else:
                e3 = p.emit("vector", lambda v, j=j, cvB=cvB, cvA=cvA, w2=w2,
                            h0=h0, h1=h1:
                            v.scalar_tensor_tensor(
                                cvA[:], xnb_t(j, h0 + 1, h1 + 1), w2,
                                cvB[:], ALU.mult, ALU.add),
                            waits=[e2, e2b], sig=True)
                e3b = None
            p.emit("scalar", lambda s, ubj=ubj, cvA=cvA: s.activation(
                ubj[:], cvA[:], AF.Silu), waits=[e3, e3b], sig=True)
        for s0 in range(0, CH, 512):
            mm_bc = None
            mm_z = None
            for j in range(NDT):
                mm_bc = p.emit("tensor", lambda t, j=j, s0=s0: t.matmul(
                    ps_c[0:32, 0:512], W["a_wbc"][:, j * 32:(j + 1) * 32],
                    ub[:, j * CH + s0:j * CH + s0 + 512],
                    start=(j == 0), stop=(j == NDT - 1), skip_group_check=True),
                    sig=(j == NDT - 1))
                mm_z = p.emit("tensor", lambda t, j=j, s0=s0: t.matmul(
                    ps_d[0:64, 0:512], W["a_wdt"][:, j * 64:(j + 1) * 64],
                    ub[:, j * CH + s0:j * CH + s0 + 512],
                    start=(j == 0), stop=(j == NDT - 1), skip_group_check=True),
                    sig=(j == NDT - 1))
            p.emit("vector", lambda v, c0=h0 + s0: v.tensor_scalar_add(
                bc_sb[0:32, c0:c0 + 512], ps_c[0:32, 0:512],
                W["a_bbc"][0:32, 0:1]), waits=[mm_bc], sig=True)
            p.emit("vector", lambda v, c0=h0 + s0: v.tensor_scalar_add(
                zb[0:64, c0:c0 + 512], ps_d[0:64, 0:512],
                W["a_bdt"][0:64, 0:1]), waits=[mm_z], sig=True)

    ecz = p.emit("vector", lambda v: v.memset(carry[:], 0.0), sig=True)

    for ci in range(NCH):
        h0, h1 = ci * CH, (ci + 1) * CH
        for j in range(NDT):
            mm = None
            for s0 in range(0, CH, 512):
                mm = p.emit("tensor", lambda t, j=j, s0=s0, h0=h0: t.matmul(
                    ps_b[:, s0:s0 + 512], W["a_wdtp"][0:64, j * 128:(j + 1) * 128],
                    zb[0:64, h0 + s0:h0 + s0 + 512],
                    start=True, stop=True), sig=(s0 + 512 == CH))
            spA, spB, spC, spD = t1m[0], t1m[1], t1m[2], t1m[3]
            bcol = W["a_bdtp"][:, j:j + 1]
            e1 = p.emit("vector", lambda v, spA=spA, bcol=bcol: v.tensor_scalar_add(
                spA[:], ps_b[:, 0:CH], bcol), waits=[mm], sig=True)
            e2 = p.emit("scalar", lambda s, spA=spA, spB=spB: s.activation(
                spB[:], spA[:], AF.Abs), waits=[e1], sig=True)
            cc_m1 = p.constcol(-1.0)
            cc_p1 = p.constcol(1.0)
            e3 = p.emit("scalar", lambda s, spB=spB, spC=spC, cc_m1=cc_m1:
                        s.activation(spC[:], spB[:], AF.Exp, scale=cc_m1),
                        waits=[e2], sig=True)
            e4 = p.emit("scalar", lambda s, spC=spC, spB=spB, cc_p1=cc_p1:
                        s.activation(spB[:], spC[:], AF.Ln, bias=cc_p1),
                        waits=[e3], sig=True)
            e5 = p.emit("vector", lambda v, spA=spA, spD=spD: v.tensor_scalar_max(
                spD[:], spA[:], 0.0), waits=[e1], sig=True)
            e6 = p.emit("vector", lambda v, j=j, spD=spD, spB=spB: v.tensor_add(
                dtb[:, j * CH:(j + 1) * CH], spD[:], spB[:]), waits=[e4, e5],
                sig=True)
            p.emit("vector", lambda v, j=j, h0=h0, h1=h1: v.tensor_mul(
                xdtb[:, j * CH:(j + 1) * CH], xnb_t(j, h0, h1),
                dtb[:, j * CH:(j + 1) * CH]), waits=[e6], sig=True)
        for n in range(NST):
            ebr0 = p.dma("gpsimd", brow[0:1, 0:CH], bc_sb[n:n + 1, h0:h1])
            ebr1 = p.dma("gpsimd", brow[0:1, CH:2 * CH],
                         bc_sb[16 + n:17 + n, h0:h1])
            eB = bcast_to_psum(p, ps_a[:, 0:CH], brow[0:1, 0:CH], ones_row,
                               [ebr0])
            eC = bcast_to_psum(p, ps_a[:, CH:2 * CH], brow[0:1, CH:2 * CH],
                               ones_row, [ebr1])
            eBs = p.emit("scalar", lambda s: s.copy(btl[0][:], ps_a[:, 0:CH]),
                         waits=[eB], sig=True)
            eCs = p.emit("scalar", lambda s: s.copy(btl[1][:], ps_a[:, CH:2 * CH]),
                         waits=[eC], sig=True)
            for j in range(NDT):
                acol = W["a_alog"][:, j * 16 + n:j * 16 + n + 1]
                hbuf = t1m[j % 2]
                dAyt = t1m[2 + (j % 2)]
                dB = dBb[j % 2]
                eA_ = p.emit("scalar", lambda s, j=j, dAyt=dAyt, acol=acol:
                             s.activation(dAyt[:], dtb[:, j * CH:(j + 1) * CH],
                                          AF.Exp, scale=acol), sig=True)
                eB_ = p.emit("vector", lambda v, j=j, dB=dB: v.tensor_mul(
                    dB[:], xdtb[:, j * CH:(j + 1) * CH], btl[0][:]),
                    waits=[eBs], sig=True)
                col = n * 8 + j
                eS = p.emit("vector", lambda v, dAyt=dAyt, dB=dB, hbuf=hbuf,
                            col=col: v.tensor_tensor_scan(
                                hbuf[:], dAyt[:], dB[:], carry[:, col:col + 1],
                                ALU.mult, ALU.add),
                            waits=[eA_, eB_, ecz], sig=True)
                if ci == 0 and NCH > 1:
                    p.emit("scalar", lambda s, hbuf=hbuf, col=col: s.copy(
                        carry[:, col:col + 1], hbuf[:, CH - 1:CH]),
                        waits=[eS], sig=True)
                eM = p.emit("vector", lambda v, hbuf=hbuf, dAyt=dAyt: v.tensor_mul(
                    dAyt[:], hbuf[:], btl[1][:]), waits=[eS, eCs], sig=True)
                if n == 0:
                    p.emit("gpsimd", lambda g, j=j, dAyt=dAyt: g.tensor_copy(
                        yacc[:, j * CH:(j + 1) * CH], dAyt[:]), waits=[eM],
                        sig=True)
                else:
                    p.emit("gpsimd", lambda g, j=j, dAyt=dAyt: g.tensor_add(
                        yacc[:, j * CH:(j + 1) * CH],
                        yacc[:, j * CH:(j + 1) * CH], dAyt[:]),
                        waits=[eM], sig=True)
        for j in range(NDT):
            p.emit("vector", lambda v, j=j, h0=h0, h1=h1: v.scalar_tensor_tensor(
                yacc[:, j * CH:(j + 1) * CH], xnb_t(j, h0, h1),
                W["a_dcoef"][:, j:j + 1], yacc[:, j * CH:(j + 1) * CH],
                ALU.mult, ALU.add), sig=True)
        sevs = []
        for s0 in range(0, CH, 512):
            sevs += ln_stats_chunk(
                p, lambda j, s0=s0: yacc[:, j * CH + s0:j * CH + s0 + 512],
                sqb, xcb, ps_c, ps_d, rsum, h0 + s0, ones_col, [])
        revs = ln_rows(p, rsum, rw, rowbf, sevs, c0=h0, c1=h1)
        eld = p.dma("sync",
                    xs[:].rearrange("p (j t) -> p j t", j=NDT),
                    xrow_in.rearrange("(j p) t -> p j t", p=128)[:, :, h0:h1])
        eR = bcast_to_psum(p, ps_b[:, 0:CH], rowbf[0:1, h0:h1], ones_row, revs)
        eA = bcast_to_psum(p, ps_a[:, 0:CH], rowbf[0:1, T + h0:T + h1],
                           ones_row, revs)
        st_evs = []
        for j in range(NDT):
            tmp = t1m[j % 2]
            e1 = p.emit("vector", lambda v, j=j, tmp=tmp: v.tensor_mul(
                tmp[:], yacc[:, j * CH:(j + 1) * CH], ps_b[:, 0:CH]),
                waits=[eR], sig=True)
            e2 = p.emit("vector", lambda v, tmp=tmp: v.tensor_add(
                tmp[:], tmp[:], ps_a[:, 0:CH]), waits=[e1, eA], sig=True)
            e3 = p.emit("vector", lambda v, j=j, tmp=tmp: v.tensor_add(
                xs_t(j), xs_t(j), tmp[:]), waits=[e2, eld], sig=True)
            st_evs.append(p.dma(
                "scalar",
                xrow_out.rearrange("(j p) t -> p j t", p=128)[:, j, h0:h1],
                xs_t(j), waits=[e3]))
    return st_evs


def alloc_l0attn_bufs(p):
    bufs = {}
    bufs["xs"] = p.sbuf("xs", [128, NDT * CH], F32)
    bufs["xnb"] = p.sbuf("xnb", [128, NDT * T], BF16)
    bufs["yacc"] = p.sbuf("yacc", [128, NDT * CH], F32)
    bufs["ub"] = bufs["yacc"][:, 0:NDT * CH // 2].bitcast(BF16)
    bufs["dBb"] = [p.sbuf(f"dBb{i}", [128, CH], BF16) for i in range(2)]
    bufs["sqb"] = [bufs["dBb"][0][:, 0:512], bufs["dBb"][0][:, 512:1024]]
    bufs["xcb"] = [bufs["dBb"][1][:, 0:512], bufs["dBb"][1][:, 512:1024]]
    bufs["t1m"] = [p.sbuf(f"t1m{i}", [128, CH], F32) for i in range(4)]
    bufs["btl"] = [p.sbuf(f"btl{i}", [128, CH], BF16) for i in range(2)]
    bufs["dtb"] = p.sbuf("dtb", [128, NDT * CH], BF16)
    bufs["xdtb"] = p.sbuf("xdtb", [128, NDT * CH], BF16)
    bufs["bc_sb"] = p.sbuf("bc_sb", [32, T], BF16)
    bufs["zb"] = p.sbuf("zb", [64, T], BF16)
    bufs["brow"] = p.sbuf("brow", [1, 2 * CH], BF16)
    bufs["carry"] = p.sbuf("carry", [128, 128], F32)
    bufs["rsum"] = p.sbuf("rsum", [1, 2 * T], F32)
    bufs["rw"] = p.sbuf("rw", [1, 2048], F32)
    bufs["rowbf"] = p.sbuf("rowbf", [1, 2 * T], BF16)
    bufs["ps_a"] = p.psum("ps_a", [128, 2048])
    bufs["ps_b"] = p.psum("ps_b", [128, 1024])
    bufs["ps_c"] = p.psum("ps_c", [128, 512])
    bufs["ps_d"] = p.psum("ps_d", [128, 512])
    return bufs


def emit_block_l0attn(p, xin, xout):
    tabs, tev = declare_tables(p, host_tables())
    Wsb, wev = declare_weights(p, L0ATTN_WEIGHT_DECLS)
    bufs = alloc_l0attn_bufs(p)
    aneg = p.sbuf("aneg", [128, 128], F32)
    e1 = p.emit("scalar", lambda s: s.activation(aneg[:], Wsb["a_alog"][:], AF.Exp),
                waits=wev + tev, sig=True)
    p.emit("vector", lambda v: v.tensor_scalar_mul(aneg[:], aneg[:], -1.0),
           waits=[e1], sig=True)
    Wd = dict(Wsb)
    Wd["a_alog"] = aneg
    store_evs = []
    for r in range(ROWS_PER_CORE):
        store_evs += emit_l0attn_row(p, bufs, tabs, Wd, xin[r], xout[r])
    return store_evs


def build_l0attn_program(serial=True):
    p = Prog(serial=serial)
    xin = p.dram_in("xT", [ROWS_PER_CORE, D, T], F32)
    xout = p.dram_out("yT", [ROWS_PER_CORE, D, T], F32)
    store_evs = emit_block_l0attn(p, xin, xout)
    nc = p.finalize(final_waits=store_evs)
    return p, nc


NEXP = 8
TCW = 512
NTC = T // TCW
NFT = 4096 // 128


def prep_l0ffn(inp):
    return {
        "f_gw": _prep_row_major_w(
            np.ascontiguousarray(inp["L0_ffn_gate_w"].T), NEXP),
    }


L0FFN_WEIGHT_DECLS = [
    ("f_gw", (128, NDT * NEXP), F32),
]


def emit_l0ffn_gating_row(p, bufs, tabs, W, xrow_in, r):
    xs = bufs["xs"]
    xnb = bufs["xnb"]
    sqb, xcb = bufs["sqb"], bufs["xcb"]
    tmf = bufs["tmf"]
    rsum, rw, rowf, rowbf = bufs["rsum"], bufs["rw"], bufs["rowf"], bufs["rowbf"]
    rawT = bufs["rawT"]
    bis = bufs["bis"]
    crow = bufs["crow"]
    wtb = bufs["wtb"]
    wtok = bufs["wtok"][r]
    ps_a, ps_b, ps_c, ps_t = bufs["ps_a"], bufs["ps_b"], bufs["ps_c"], bufs["ps_t"]
    ones_col = tabs["c_ones_col"]
    ones_row = tabs["c_ones_row"]
    onesf = tabs["c_onesf"]
    gw = W["f_gw"]
    c1f = p.constcol(1.0)

    def xs_t(j, s0=0, s1=CH):
        return xs[:, j * CH + s0:j * CH + s1]

    for ci in range(NCH):
        h0, h1 = ci * CH, (ci + 1) * CH
        eld = p.dma("sync",
                    xs[:].rearrange("p (j t) -> p j t", j=NDT),
                    xrow_in.rearrange("(j p) t -> p j t", p=128)[:, :, h0:h1])
        sevs = []
        for s0 in range(0, CH, 512):
            sevs += ln_stats_chunk(p, lambda j, s0=s0: xs_t(j, s0, s0 + 512),
                                   sqb, xcb, ps_b[0:1, 0:512],
                                   ps_b[0:1, 512:1024], rsum, h0 + s0,
                                   ones_col, [eld])
            mm = None
            for j in range(NDT):
                mm = p.emit("tensor", lambda t, j=j, s0=s0: t.matmul(
                    ps_a[0:8, 0:512], gw[:, j * NEXP:(j + 1) * NEXP],
                    xs_t(j, s0, s0 + 512), start=(j == 0), stop=(j == NDT - 1),
                    skip_group_check=True), waits=[eld], sig=(j == NDT - 1))
            p.emit("vector", lambda v, c0=h0 + s0: v.tensor_copy(
                rawT[0:8, c0:c0 + 512], ps_a[0:8, 0:512]), waits=[mm], sig=True)
        revs = ln_rows(p, rsum, rw, rowbf, sevs, c0=h0, c1=h1, rowf=rowf)
        eR = bcast_to_psum(p, ps_a[:, 0:CH], rowbf[0:1, h0:h1], ones_row, revs)
        eA = bcast_to_psum(p, ps_b[:, 0:CH], rowbf[0:1, T + h0:T + h1],
                           ones_row, revs)
        for j in range(NDT):
            tmp = tmf[j % 2]
            e1 = p.emit("vector", lambda v, j=j, tmp=tmp: v.tensor_mul(
                tmp[:], xs_t(j), ps_a[:, 0:CH]), waits=[eR], sig=True)
            p.emit("vector", lambda v, j=j, h0=h0, h1=h1, tmp=tmp: v.tensor_add(
                xnb[:, j * T + h0:j * T + h1], tmp[:], ps_b[:, 0:CH]),
                waits=[e1, eA], sig=True)

    tmp8 = xs[0:8, 0:T]
    expT = xs[0:8, T:2 * T]
    probsT = xs[0:8, 2 * T:3 * T]
    cmpT = xs[0:8, 3 * T:4 * T]
    mm = None
    for j in range(NDT):
        mm = p.emit("tensor", lambda t, j=j: t.matmul(
            ps_c[0:8, 0:1], gw[:, j * NEXP:(j + 1) * NEXP], c1f,
            start=(j == 0), stop=(j == NDT - 1), skip_group_check=True),
            sig=(j == NDT - 1))
    e_g8 = p.emit("vector", lambda v: v.tensor_copy(
        bis[0:8, 7:8], ps_c[0:8, 0:1]), waits=[mm], sig=True)
    evR = None
    for s0 in range(0, T, 512):
        evR = p.emit("tensor", lambda t, s0=s0: t.matmul(
            ps_a[0:8, s0:s0 + 512], onesf[0:1, 0:8], rowf[0:1, s0:s0 + 512],
            start=True, stop=True), sig=(s0 + 512 == T))
    e1 = p.emit("vector", lambda v: v.tensor_mul(
        tmp8, rawT[0:8, :], ps_a[0:8, 0:T]), waits=[evR], sig=True)
    evA = None
    for s0 in range(0, T, 512):
        evA = p.emit("tensor", lambda t, s0=s0: t.matmul(
            ps_a[0:8, s0:s0 + 512], onesf[0:1, 0:8],
            rowf[0:1, T + s0:T + s0 + 512],
            start=True, stop=True), waits=[e1], sig=(s0 + 512 == T))
    e2 = p.emit("vector", lambda v: v.scalar_tensor_tensor(
        rawT[0:8, :], ps_a[0:8, 0:T], bis[0:8, 7:8], tmp8,
        ALU.mult, ALU.add), waits=[evA, e_g8], sig=True)
    e3 = p.emit("vector", lambda v: v.reduce_max(
        bis[0:8, 5:6], rawT[0:8, :], axis=mybir.AxisListType.X),
        waits=[e2], sig=True)
    e4 = p.dma("gpsimd", crow[0:1, 0:8], bis[0:8, 5:6], waits=[e3])
    e5 = p.emit("vector", lambda v: v.reduce_max(
        crow[0:1, 16:17], crow[0:1, 0:8], axis=mybir.AxisListType.X),
        waits=[e4], sig=True)
    e6 = p.emit("tensor", lambda t: t.matmul(
        ps_c[0:8, 0:1], onesf[0:1, 0:8], crow[0:1, 16:17],
        start=True, stop=True), waits=[e5], sig=True)
    e7 = p.emit("vector", lambda v: v.tensor_scalar_mul(
        bis[0:8, 6:7], ps_c[0:8, 0:1], -1.0), waits=[e6], sig=True)
    e8 = p.emit("scalar", lambda s: s.activation(
        expT, rawT[0:8, :], AF.Exp, bias=bis[0:8, 6:7]), waits=[e7], sig=True)
    evS = None
    for s0 in range(0, T, 512):
        evS = p.emit("tensor", lambda t, s0=s0: t.matmul(
            ps_b[0:1, 0:512] if s0 % 1024 == 0 else ps_b[0:1, 512:1024],
            c1f[0:8, 0:1], expT[:, s0:s0 + 512],
            start=True, stop=True), waits=[e8], sig=True)
        p.emit("vector", lambda v, s0=s0, evS=evS: v.tensor_copy(
            rw[0:1, s0:s0 + 512],
            ps_b[0:1, 0:512] if s0 % 1024 == 0 else ps_b[0:1, 512:1024]),
            waits=[evS], sig=True)
    if getattr(p, "dbg", False) and r == 0:
        p.debug_dump("rawT", rawT[0:8, :])
        p.debug_dump("expT", expT)
        p.debug_dump("sums", rw[0:1, 0:T])
    cc_m1g = p.constcol(-1.0)[0:1, 0:1]
    e9 = p.emit("scalar", lambda s: s.activation(
        rowf[0:1, 0:T], rw[0:1, 0:T], AF.Ln), sig=True)
    e10 = p.emit("scalar", lambda s: s.activation(
        rw[0:1, 0:T], rowf[0:1, 0:T], AF.Exp, scale=cc_m1g),
        waits=[e9], sig=True)
    evR2 = None
    for s0 in range(0, T, 512):
        evR2 = p.emit("tensor", lambda t, s0=s0: t.matmul(
            ps_a[0:8, s0:s0 + 512], onesf[0:1, 0:8], rw[0:1, s0:s0 + 512],
            start=True, stop=True), waits=[e10], sig=(s0 + 512 == T))
    e11 = p.emit("vector", lambda v: v.tensor_mul(
        probsT, expT, ps_a[0:8, 0:T]), waits=[evR2], sig=True)
    if getattr(p, "dbg", False) and r == 0:
        p.debug_dump("recipb", rw[0:1, 0:T])
    cc_half = p.constcol(0.5)
    cc_nh = p.constcol(-511.5)
    p.emit("gpsimd", lambda g: g.memset(bis[0:8, 0:1], 0.0), sig=True)
    ez = p.emit("gpsimd", lambda g: g.memset(bis[0:8, 1:2], 2.0), sig=True)
    for it in range(30):
        em = p.emit("gpsimd", lambda g: g.tensor_add(
            bis[0:8, 2:3], bis[0:8, 0:1], bis[0:8, 1:2]), waits=[ez], sig=True)
        em2 = p.emit("gpsimd", lambda g: g.tensor_mul(
            bis[0:8, 2:3], bis[0:8, 2:3], cc_half[0:8, 0:1]), waits=[em],
            sig=True)
        ec = p.emit("vector", lambda v: v.tensor_scalar(
            cmpT, probsT, bis[0:8, 2:3], None, ALU.is_ge),
            waits=[em2, e11], sig=True)
        er = p.emit("vector", lambda v: v.reduce_sum(
            bis[0:8, 3:4], cmpT, axis=mybir.AxisListType.X), waits=[ec], sig=True)
        eg1 = p.emit("scalar", lambda s: s.activation(
            bis[0:8, 4:5], bis[0:8, 3:4], AF.Sign, bias=cc_nh[0:8, 0:1]),
            waits=[er], sig=True)
        eg2 = p.emit("gpsimd", lambda g: g.tensor_mul(
            bis[0:8, 4:5], bis[0:8, 4:5], cc_half[0:8, 0:1]), waits=[eg1],
            sig=True)
        es_ = p.emit("gpsimd", lambda g: g.tensor_add(
            bis[0:8, 4:5], bis[0:8, 4:5], cc_half[0:8, 0:1]), waits=[eg2],
            sig=True)
        ed1 = p.emit("gpsimd", lambda g: g.tensor_sub(
            bis[0:8, 5:6], bis[0:8, 2:3], bis[0:8, 0:1]), waits=[es_], sig=True)
        ed2 = p.emit("gpsimd", lambda g: g.tensor_sub(
            bis[0:8, 6:7], bis[0:8, 1:2], bis[0:8, 2:3]), waits=[es_], sig=True)
        em3 = p.emit("gpsimd", lambda g: g.tensor_mul(
            bis[0:8, 5:6], bis[0:8, 5:6], bis[0:8, 4:5]), waits=[ed1], sig=True)
        em4 = p.emit("gpsimd", lambda g: g.tensor_mul(
            bis[0:8, 6:7], bis[0:8, 6:7], bis[0:8, 4:5]), waits=[ed2], sig=True)
        en1 = p.emit("gpsimd", lambda g: g.tensor_add(
            bis[0:8, 0:1], bis[0:8, 0:1], bis[0:8, 5:6]), waits=[em3], sig=True)
        ez = p.emit("gpsimd", lambda g: g.tensor_add(
            bis[0:8, 1:2], bis[0:8, 2:3], bis[0:8, 6:7]), waits=[em4, en1],
            sig=True)
    ew1 = p.emit("vector", lambda v: v.tensor_scalar(
        cmpT, probsT, bis[0:8, 0:1], None, ALU.is_ge), waits=[ez], sig=True)
    ew2 = p.emit("vector", lambda v: v.tensor_mul(
        tmp8, probsT, cmpT), waits=[ew1], sig=True)
    ew3 = p.emit("vector", lambda v: v.tensor_copy(wtb[0:8, :], tmp8),
                 waits=[ew2], sig=True)
    for tt in range(16):
        et = p.emit("tensor", lambda t, tt=tt: t.transpose(
            ps_t[:, 0:8], wtb[0:8, tt * 128:(tt + 1) * 128],
            tabs["c_ident"][0:8, 0:8]), waits=[ew3], sig=True)
        p.emit("vector", lambda v, tt=tt: v.tensor_copy(
            wtok[:, tt * 8:(tt + 1) * 8], ps_t[:, 0:8]), waits=[et], sig=True)
    if getattr(p, "dbg", False) and r == 0:
        p.debug_dump("probsT", probsT)
        p.debug_dump("w8", tmp8)
        p.debug_dump("thr", bis[0:8, 0:2])
        p.debug_dump("wtok", wtok[:, :])
        p.debug_dump("xnb", xnb[:, :])


def emit_l0ffn_experts_row(p, bufs, tabs, W, w1_ap, w2_ap, yscr_row, r):
    xnb = bufs["xnb"]
    hgel = bufs["hgel"]
    outacc = bufs["outacc"]
    ocb = bufs["ocb"]
    ytc = bufs["ytc"]
    w1c, w2c = bufs["w1c"], bufs["w2c"]
    wtok = bufs["wtok"][r]
    ps_a, ps_c, ps_t = bufs["ps_a"], bufs["ps_c"], bufs["ps_t"]
    ident = tabs["c_ident"]
    NTT_C = TCW // 128
    for tc in range(NTC):
        for e in range(NEXP):
            for dh in range(2):
                for f in range(NFT):
                    buf = f % 2
                    if dh == 0:
                        ew1 = p.dma("gpsimd",
                                    w1c[buf][:].rearrange(
                                        "p (j f) -> p j f", j=NDT),
                                    w1_ap[e].rearrange(
                                        "(j p) f -> p j f", p=128)[:, :, f * 128:(f + 1) * 128])
                    else:
                        ew1 = None
                    ew2 = p.dma("gpsimd", w2c[buf][:],
                                w2_ap[e][f * 128:(f + 1) * 128, :])
                    if dh == 0:
                        mm = None
                        for j in range(NDT):
                            mm = p.emit("tensor", lambda t, j=j, f=f, tc=tc, buf=buf: t.matmul(
                                ps_c[:, 0:TCW], w1c[buf][:, j * 128:(j + 1) * 128],
                                xnb[:, j * T + tc * TCW:j * T + (tc + 1) * TCW],
                                start=(j == 0), stop=(j == NDT - 1)),
                                waits=[ew1], sig=(j == NDT - 1))
                        eg = p.emit("scalar", lambda s, f=f: s.activation(
                            hgel[:, f * TCW:(f + 1) * TCW], ps_c[:, 0:TCW],
                            AF.Gelu), waits=[mm], sig=True)
                    else:
                        eg = None
                    for tt in range(NTT_C):
                        p.emit("tensor", lambda t, f=f, tt=tt, dh=dh, buf=buf: t.matmul(
                            ps_a[:, tt * 512:(tt + 1) * 512],
                            hgel[:, f * TCW + tt * 128:f * TCW + (tt + 1) * 128],
                            w2c[buf][:, dh * 512:(dh + 1) * 512],
                            start=(f == 0), stop=(f == NFT - 1),
                            skip_group_check=True),
                            waits=[eg, ew2], sig=(f == NFT - 1 and tt == NTT_C - 1))
                for tt in range(NTT_C):
                    wcol = wtok[:, (tc * NTT_C + tt) * 8 + e:(tc * NTT_C + tt) * 8 + e + 1]
                    dst = outacc[:, tt * 1024 + dh * 512:tt * 1024 + (dh + 1) * 512]
                    if e == 0:
                        p.emit("vector", lambda v, tt=tt, dst=dst, wcol=wcol:
                               v.tensor_scalar_mul(
                                   dst, ps_a[:, tt * 512:(tt + 1) * 512], wcol),
                               sig=True)
                    else:
                        p.emit("vector", lambda v, tt=tt, dst=dst, wcol=wcol:
                               v.scalar_tensor_tensor(
                                   dst, ps_a[:, tt * 512:(tt + 1) * 512], wcol,
                                   dst, ALU.mult, ALU.add), sig=True)
        for hh in range(2):
            p.emit("vector", lambda v, hh=hh: v.tensor_copy(
                ocb[:, hh * 2048:(hh + 1) * 2048],
                outacc[:, hh * 2048:(hh + 1) * 2048]), sig=True)
        for tt in range(NTT_C):
            for dj in range(NDT):
                et = p.emit("tensor", lambda t, tt=tt, dj=dj: t.transpose(
                    ps_t[:, 0:128],
                    ocb[:, tt * 1024 + dj * 128:tt * 1024 + (dj + 1) * 128],
                    ident[:]), sig=True)
                p.emit("vector", lambda v, tt=tt, dj=dj: v.tensor_copy(
                    ytc[:, dj * TCW + tt * 128:dj * TCW + (tt + 1) * 128],
                    ps_t[:, 0:128]), waits=[et], sig=True)
        p.dma("scalar",
              yscr_row.rearrange("(j p) t -> p j t", p=128)[
                  :, :, tc * TCW:(tc + 1) * TCW],
              ytc[:].rearrange("p (j t) -> p j t", j=NDT))
        if getattr(p, "dbg", False) and r == 0 and tc == 0:
            p.debug_dump("outacc", outacc[:, :])


def emit_l0ffn_lnout_row(p, bufs, tabs, xrow_in, xrow_out, yscr_row):
    xs = bufs["xs"]
    ybt = bufs["xnb"][:, 0:NDT * CH]
    sqb, xcb = bufs["sqb"], bufs["xcb"]
    tmf = bufs["tmf"]
    rsum, rw, rowbf = bufs["rsum"], bufs["rw"], bufs["rowbf"]
    ps_a, ps_b = bufs["ps_a"], bufs["ps_b"]
    ones_col = tabs["c_ones_col"]
    ones_row = tabs["c_ones_row"]
    st_evs = []
    for ci in range(NCH):
        h0, h1 = ci * CH, (ci + 1) * CH
        eldy = p.dma("sync",
                     ybt[:].rearrange("p (j t) -> p j t", j=NDT),
                     yscr_row.rearrange("(j p) t -> p j t", p=128)[:, :, h0:h1])
        eldx = p.dma("gpsimd",
                     xs[:].rearrange("p (j t) -> p j t", j=NDT),
                     xrow_in.rearrange("(j p) t -> p j t", p=128)[:, :, h0:h1])
        sevs = []
        for s0 in range(0, CH, 512):
            sevs += ln_stats_chunk(
                p, lambda j, s0=s0: ybt[:, j * CH + s0:j * CH + s0 + 512],
                sqb, xcb, ps_b[0:1, 0:512], ps_b[0:1, 512:1024], rsum,
                h0 + s0, ones_col, [eldy], is_bf16=True)
        revs = ln_rows(p, rsum, rw, rowbf, sevs, c0=h0, c1=h1)
        eR = bcast_to_psum(p, ps_b[:, 0:CH], rowbf[0:1, h0:h1], ones_row, revs)
        eA = bcast_to_psum(p, ps_a[:, 0:CH], rowbf[0:1, T + h0:T + h1],
                           ones_row, revs)
        for j in range(NDT):
            tmp = tmf[j % 2]
            e1 = p.emit("vector", lambda v, j=j, tmp=tmp: v.tensor_mul(
                tmp[:], ybt[:, j * CH:(j + 1) * CH], ps_b[:, 0:CH]),
                waits=[eR], sig=True)
            e2 = p.emit("vector", lambda v, tmp=tmp: v.tensor_add(
                tmp[:], tmp[:], ps_a[:, 0:CH]), waits=[e1, eA], sig=True)
            e3 = p.emit("vector", lambda v, j=j, tmp=tmp: v.tensor_add(
                xs[:, j * CH:(j + 1) * CH], xs[:, j * CH:(j + 1) * CH], tmp[:]),
                waits=[e2, eldx], sig=True)
            st_evs.append(p.dma(
                "scalar",
                xrow_out.rearrange("(j p) t -> p j t", p=128)[:, j, h0:h1],
                xs[:, j * CH:(j + 1) * CH], waits=[e3]))
    return st_evs


def alloc_l0ffn_bufs(p):
    bufs = {}
    bufs["xs"] = p.sbuf("xs", [128, NDT * CH], F32)
    bufs["xnb"] = p.sbuf("xnb", [128, NDT * T], BF16)
    bufs["hgel"] = p.sbuf("hgel", [128, NFT * TCW], BF16)
    bufs["ocb"] = bufs["hgel"][:, 0:4 * 1024]
    bufs["outacc"] = p.sbuf("outacc", [128, 4 * 1024], F32)
    bufs["ytc"] = p.sbuf("ytc", [128, NDT * TCW], BF16)
    bufs["w1c"] = [p.sbuf(f"w1c{i}", [128, NDT * 128], BF16) for i in range(2)]
    bufs["w2c"] = [p.sbuf(f"w2c{i}", [128, 1024], BF16) for i in range(2)]
    bufs["sqb"] = [p.sbuf(f"sqb{i}", [128, 512], BF16) for i in range(2)]
    bufs["xcb"] = [p.sbuf(f"xcb{i}", [128, 512], BF16) for i in range(2)]
    bufs["tmf"] = [p.sbuf(f"tmf{i}", [128, CH], F32) for i in range(2)]
    bufs["rawT"] = p.sbuf("rawT", [8, T], F32)
    bufs["wtb"] = p.sbuf("wtb", [8, T], BF16)
    bufs["wtok"] = [p.sbuf(f"wtok{i}", [128, 128], F32) for i in range(2)]
    bufs["bis"] = p.sbuf("bis", [8, 8], F32)
    bufs["crow"] = p.sbuf("crow", [1, 32], F32)
    bufs["rsum"] = p.sbuf("rsum", [1, 2 * T], F32)
    bufs["rowf"] = bufs["rsum"]
    bufs["rw"] = p.sbuf("rw", [1, 2048], F32)
    bufs["rowbf"] = p.sbuf("rowbf", [1, 2 * T], BF16)
    bufs["ps_a"] = p.psum("ps_a", [128, 2048])
    bufs["ps_b"] = p.psum("ps_b", [128, 1024])
    bufs["ps_c"] = p.psum("ps_c", [128, 512])
    bufs["ps_t"] = p.psum("ps_t", [128, 512], BF16)
    return bufs


def emit_block_l0ffn(p, xin, xout):
    w1 = p.dram_in("f_w1", [NEXP, D, 4096], F32)
    w2 = p.dram_in("f_w2", [NEXP, 4096, D], F32)
    yscr = p.nc.dram_tensor("yscr", [ROWS_PER_CORE, D, T], BF16,
                            kind="Internal").ap()
    tabs, tev = declare_tables(p, host_tables())
    Wsb, wev = declare_weights(p, L0FFN_WEIGHT_DECLS)
    bufs = alloc_l0ffn_bufs(p)
    st_evs = []
    for r in range(ROWS_PER_CORE):
        emit_l0ffn_gating_row(p, bufs, tabs, Wsb, xin[r], r)
        emit_l0ffn_experts_row(p, bufs, tabs, Wsb, w1, w2, yscr[r], r)
        st_evs += emit_l0ffn_lnout_row(p, bufs, tabs, xin[r], xout[r], yscr[r])
    return st_evs


def build_l0ffn_program(serial=True, dbg=False):
    p = Prog(serial=serial)
    p.dbg = dbg
    xin = p.dram_in("xT", [ROWS_PER_CORE, D, T], F32)
    xout = p.dram_out("yT", [ROWS_PER_CORE, D, T], F32)
    st_evs = emit_block_l0ffn(p, xin, xout)
    nc = p.finalize(final_waits=st_evs)
    return p, nc


def run_l0ffn(x, inputs, serial=True):
    shards = _shard_x(x)
    tabs = host_tables()
    wprep = prep_l0ffn(inputs)
    p, nc = build_l0ffn_program(serial=serial)
    in_maps = []
    for c in range(NCORES):
        m = {"xT": shards[c],
             "f_w1": inputs["L0_ffn_w1"], "f_w2": inputs["L0_ffn_w2"]}
        m.update(tabs)
        m.update({k: np.ascontiguousarray(v, dtype=np.float32)
                  for k, v in wprep.items()})
        in_maps.append(m)
    res = run_bass_kernel_spmd(nc, in_maps, core_ids=list(range(NCORES)))
    p.close()
    out = np.concatenate([r["yT"] for r in res.results], axis=0)
    return np.ascontiguousarray(np.transpose(out, (0, 2, 1)))


NH = 16
DK = 64
QC = 512
NQC = T // QC


def _l1_perm():
    perm = []
    for g in range(4):
        for k in range(4):
            h = 4 * g + k
            perm += list(range(h * 64, h * 64 + 32))
    for g in range(4):
        for k in range(4):
            h = 4 * g + k
            perm += list(range(h * 64 + 32, h * 64 + 64))
    return np.array(perm)


def _prep2(w):
    return np.ascontiguousarray(
        w.reshape(8, 128, 8, 128).transpose(1, 2, 0, 3).reshape(128, 8192))


def prep_l1attn(inp):
    Wq = inp["L1_attn_Wq"]
    perm = _l1_perm()
    u = inp["L1_attn_u"].reshape(NH, DK)
    v = inp["L1_attn_v"].reshape(NH, DK)
    uprep = np.zeros((128, 8), np.float32)
    for m in range(8):
        for pp in range(128):
            uprep[pp, m] = u[2 * m + pp // 64, pp % 64]
    vflat = v.reshape(1024)[perm]
    vperm = np.ascontiguousarray(vflat.reshape(8, 128).T)
    return {
        "t_wq": _prep2(Wq),
        "t_wqp": _prep2(np.ascontiguousarray(Wq[:, perm])),
        "t_wk": _prep2(inp["L1_attn_Wk"]),
        "t_wv": np.ascontiguousarray(inp["L1_attn_Wv"]),
        "t_wo": _prep2(inp["L1_attn_Wo"]),
        "t_u": uprep,
        "t_v": vperm,
        "t_bo": _prep_dcol(inp["L1_attn_bo"]),
    }


L1ATTN_WEIGHT_DECLS = [
    ("t_u", (128, 8), F32),
    ("t_v", (128, 8), F32),
    ("t_bo", (128, 8), F32),
]


def l1attn_tables():
    inv = (1.0 / 10000.0 ** (np.arange(0, DK, 2, dtype=np.float64) / DK))
    q = np.arange(T, dtype=np.float64)
    k = np.arange(T, dtype=np.float64)
    sinq = np.sin(q[None, :] * inv[:, None])
    cosq = np.cos(q[None, :] * inv[:, None])
    sA = np.sin((2049.0 + q[None, :]) * inv[:, None])
    cA = np.cos((2049.0 + q[None, :]) * inv[:, None])
    cosk = np.cos(k[None, :] * inv[:, None])
    sink = np.sin(k[None, :] * inv[:, None])
    t32 = lambda a: np.ascontiguousarray(np.tile(a, (4, 1)).astype(BF16NP))
    htab = np.zeros((128, T), np.float64)
    htab[0:32] = cosk
    htab[32:64] = sink
    htab[64:96] = cosk
    htab[96:128] = sink
    kk = np.arange(128)[:, None]
    qq = np.arange(128)[None, :]
    tabs = {
        "c_tsin": t32(sinq), "c_tcos": t32(cosq),
        "c_tsa": t32(sA), "c_tca": t32(cA),
        "c_htab": np.ascontiguousarray(htab.astype(BF16NP)),
        "c_mlow": np.ascontiguousarray((kk <= qq).astype(BF16NP)),
        "c_md1": np.ascontiguousarray((kk == qq + 1).astype(BF16NP)),
        "c_mup": np.ascontiguousarray((kk >= qq + 2).astype(BF16NP)),
    }
    return tabs


def emit_l1attn_row(p, bufs, tabs, W, wq_ap, wqp_ap, wk_ap, wv_ap, wo_ap,
                    xrow_in, xrow_out):
    import os
    PH = int(os.environ.get("L1_PHASES", "9"))
    xs = bufs["xs"]
    xnb = bufs["xnb"]
    ktb = bufs["ktb"]
    qvpc = bufs["qvpc"]
    vaug = bufs["vaug"]
    scr16 = bufs["scr16"]
    wbf = bufs["wbf"]
    qg = bufs["qg"]
    prods = bufs["prods"]
    pbuf = bufs["pbuf"]
    pwb = bufs["pwb"]
    pk128 = bufs["pk128"]
    pk2 = bufs["pk2"]
    pcorn = bufs["pcorn"]
    recb = bufs["recb"]
    rrow = bufs["rrow"]
    sqb, xcb = bufs["sqb"], bufs["xcb"]
    tmf = bufs["tmf"]
    rsum, rw, rowbf = bufs["rsum"], bufs["rw"], bufs["rowbf"]
    ps_a, ps_b = bufs["ps_a"], bufs["ps_b"]
    ps_w, ps_d = bufs["ps_w"], bufs["ps_d"]
    ps_av = bufs["ps_av"]
    ps_m = bufs["ps_m"]
    ones_col = tabs["c_ones_col"]
    ones_row = tabs["c_ones_row"]

    quc = xs[:, 4096:6144].bitcast(BF16)
    ut = xs[:, 6144:8192].bitcast(BF16)
    ych = xs[:, 0:4096]
    xrel = xs[:, 4096:8192]

    def xnb_t(j, c0=0, c1=T):
        return xnb[:, j * T + c0:j * T + c1]

    for ci in range(NCH):
        h0, h1 = ci * CH, (ci + 1) * CH
        eld = p.dma("sync",
                    xs[:].rearrange("p (j t) -> p j t", j=NDT),
                    xrow_in.rearrange("(j p) t -> p j t", p=128)[:, :, h0:h1])
        revs = []
        for s0 in range(0, CH, 512):
            sevs = ln_stats_chunk(
                p, lambda j, s0=s0: xs[:, j * CH + s0:j * CH + s0 + 512],
                sqb, xcb, ps_a[0:1, 0:512], ps_b[0:1, 0:512], rsum, 0,
                ones_col, [eld], rsT=512)
            revs += ln_rows(p, rsum, rw, rowbf, sevs, c0=0, c1=512, rsT=512,
                            rbc0=s0, rbT=1024)
        eR = bcast_to_psum(p, ps_a[:, 0:512], rowbf[0:1, 0:512],
                           ones_row, revs)
        eR2 = bcast_to_psum(p, ps_w[:, 0:512], rowbf[0:1, 512:1024],
                            ones_row, revs)
        eA = bcast_to_psum(p, ps_b[:, 0:512], rowbf[0:1, 1024:1536],
                           ones_row, revs)
        eA2 = bcast_to_psum(p, ps_m[:, 0:512], rowbf[0:1, 1536:2048],
                            ones_row, revs)
        for j in range(NDT):
            for (sh, eRx, eAx, psR, psA) in ((0, eR, eA, ps_a, ps_b),
                                             (512, eR2, eA2, ps_w, ps_m)):
                tmp = tmf[j % 2]
                e1 = p.emit("vector", lambda v, j=j, sh=sh, tmp=tmp, psR=psR:
                            v.tensor_mul(tmp[:, 0:512],
                                         xs[:, j * CH + sh:j * CH + sh + 512],
                                         psR[:, 0:512]), waits=[eRx], sig=True)
                p.emit("vector", lambda v, j=j, sh=sh, tmp=tmp, psA=psA,
                       h0=h0: v.tensor_add(
                           xnb_t(j, h0 + sh, h0 + sh + 512), tmp[:, 0:512],
                           psA[:, 0:512]), waits=[e1, eAx], sig=True)

    if PH < 2:
        ev = p.dma("scalar", xrow_out.rearrange("(j p) t -> p j t", p=128)[:, :, 0:CH],
                   xs[:].rearrange("p (j t) -> p j t", j=NDT))
        return [ev]
    for m in range(8):
        ewk = p.dma("gpsimd", wbf[m % 2][:],
                    wk_ap[:, m * 1024:(m + 1) * 1024])
        for c0 in range(0, T, 512):
            mm = None
            for j in range(NDT):
                mm = p.emit("tensor", lambda t, j=j, m=m, c0=c0: t.matmul(
                    ps_a[:, 0:512], wbf[m % 2][:, j * 128:(j + 1) * 128],
                    xnb_t(j, c0, c0 + 512), start=(j == 0), stop=(j == NDT - 1)),
                    waits=[ewk], sig=(j == NDT - 1))
            p.emit("scalar", lambda s, m=m, c0=c0: s.copy(
                ktb[:, m * T + c0:m * T + c0 + 512], ps_a[:, 0:512]),
                waits=[mm], sig=True)
    vr = vaug[:].rearrange("p (tt h c) -> p tt h c", tt=16, h=16, c=65)
    p.emit("vector", lambda v: v.memset(vr[:, :, :, 64:65], 1.0), sig=True)
    ewv = p.dma("gpsimd", scr16[:].rearrange("p (j c) -> p j c", j=NDT),
                wv_ap.rearrange("(j p) c -> p j c", p=128))
    for tt in range(16):
        for half in range(2):
            mm = None
            for j in range(NDT):
                mm = p.emit("tensor", lambda t, j=j, tt=tt, half=half: t.matmul(
                    ps_a[:, 0:512],
                    xnb[:, j * T + tt * 128:j * T + (tt + 1) * 128],
                    scr16[:, j * 1024 + half * 512:j * 1024 + (half + 1) * 512],
                    start=(j == 0), stop=(j == NDT - 1)),
                    waits=[ewv], sig=(j == NDT - 1))
            p.emit("vector", lambda v, tt=tt, half=half: v.tensor_copy(
                vr[:, tt, half * 8:(half + 1) * 8, 0:64], ps_a[:, 0:512]),
                waits=[mm], sig=True)

    if PH < 3:
        ev = p.dma("scalar", xrow_out.rearrange("(j p) t -> p j t", p=128)[:, :, 0:CH],
                   xs[:].rearrange("p (j t) -> p j t", j=NDT))
        return [ev]
    st_evs = []
    for qc in range(NQC):
        q0 = qc * QC
        for m in range(8):
            ewq = p.dma("gpsimd", wbf[m % 2][:],
                        wqp_ap[:, m * 1024:(m + 1) * 1024])
            mm = None
            for j in range(NDT):
                mm = p.emit("tensor", lambda t, j=j, m=m, q0=q0: t.matmul(
                    ps_w[:, 0:512], wbf[m % 2][:, j * 128:(j + 1) * 128],
                    xnb_t(j, q0, q0 + QC), start=(j == 0), stop=(j == NDT - 1)),
                    waits=[ewq], sig=(j == NDT - 1))
            p.emit("vector", lambda v, m=m: v.tensor_scalar_add(
                qvpc[:, m * 514:m * 514 + 512], ps_w[:, 0:512],
                W["t_v"][:, m:m + 1]), waits=[mm], sig=True)
            if q0 + QC < T:
                mm2 = None
                for j in range(NDT):
                    mm2 = p.emit("tensor", lambda t, j=j, m=m, q0=q0: t.matmul(
                        ps_d[:, 0:1], wbf[m % 2][:, j * 128:(j + 1) * 128],
                        xnb_t(j, q0 + QC, q0 + QC + 1),
                        start=(j == 0), stop=(j == NDT - 1)),
                        waits=[ewq], sig=(j == NDT - 1))
                p.emit("vector", lambda v, m=m: v.tensor_scalar_add(
                    qvpc[:, m * 514 + 512:m * 514 + 513], ps_d[:, 0:1],
                    W["t_v"][:, m:m + 1]), waits=[mm2], sig=True)
            else:
                p.emit("vector", lambda v, m=m: v.memset(
                    qvpc[:, m * 514 + 512:m * 514 + 513], 0.0), sig=True)
        for m in range(8):
            ewq = p.dma("gpsimd", wbf[m % 2][:],
                        wq_ap[:, m * 1024:(m + 1) * 1024])
            mm = None
            for j in range(NDT):
                mm = p.emit("tensor", lambda t, j=j, m=m, q0=q0: t.matmul(
                    ps_a[:, 0:512], wbf[m % 2][:, j * 128:(j + 1) * 128],
                    xnb_t(j, q0, q0 + QC), start=(j == 0), stop=(j == NDT - 1)),
                    waits=[ewq], sig=(j == NDT - 1))
            p.emit("vector", lambda v, m=m: v.tensor_scalar_add(
                quc[:, m * QC:(m + 1) * QC], ps_a[:, 0:512],
                W["t_u"][:, m:m + 1]), waits=[mm], sig=True)
        for g in range(4):
            sft = qvpc[:, g * 514:g * 514 + 512]
            cft = qvpc[:, (4 + g) * 514:(4 + g) * 514 + 512]
            sfs = qvpc[:, g * 514 + 1:g * 514 + 513]
            cfs = qvpc[:, (4 + g) * 514 + 1:(4 + g) * 514 + 513]
            e1 = p.emit("vector", lambda v, sft=sft, q0=q0: v.tensor_mul(
                prods[0][:], sft, tabs["c_tsin"][:, q0:q0 + QC]), sig=True)
            e2 = p.emit("vector", lambda v, cft=cft, q0=q0: v.tensor_mul(
                prods[1][:], cft, tabs["c_tcos"][:, q0:q0 + QC]), sig=True)
            e3 = p.emit("vector", lambda v, cft=cft, q0=q0: v.tensor_mul(
                prods[2][:], cft, tabs["c_tsin"][:, q0:q0 + QC]), sig=True)
            e4 = p.emit("vector", lambda v, sft=sft, q0=q0: v.tensor_mul(
                prods[3][:], sft, tabs["c_tcos"][:, q0:q0 + QC]), sig=True)
            for k in range(4):
                h = 4 * g + k
                b1 = 64 if h % 2 == 0 else 0
                b2 = 96 if h % 2 == 0 else 32
                sl = slice(32 * k, 32 * k + 32)
                p.emit("vector", lambda v, h=h, b1=b1, sl=sl: v.tensor_add(
                    qg[h % 4][b1:b1 + 32, :], prods[0][sl, :], prods[1][sl, :]),
                    waits=[e1, e2], sig=True)
                p.emit("vector", lambda v, h=h, b2=b2, sl=sl: v.tensor_sub(
                    qg[h % 4][b2:b2 + 32, :], prods[2][sl, :], prods[3][sl, :]),
                    waits=[e3, e4], sig=True)
            e5 = p.emit("vector", lambda v, sfs=sfs, q0=q0: v.tensor_mul(
                prods[0][:, 0:sfs.shape[-1]], sfs,
                tabs["c_tsa"][:, q0:q0 + sfs.shape[-1]]), sig=True)
            e6 = p.emit("vector", lambda v, cfs=cfs, q0=q0: v.tensor_mul(
                prods[1][:, 0:cfs.shape[-1]], cfs,
                tabs["c_tca"][:, q0:q0 + cfs.shape[-1]]), sig=True)
            e7 = p.emit("vector", lambda v, cfs=cfs, q0=q0: v.tensor_mul(
                prods[2][:, 0:cfs.shape[-1]], cfs,
                tabs["c_tsa"][:, q0:q0 + cfs.shape[-1]]), sig=True)
            e8 = p.emit("vector", lambda v, sfs=sfs, q0=q0: v.tensor_mul(
                prods[3][:, 0:sfs.shape[-1]], sfs,
                tabs["c_tca"][:, q0:q0 + sfs.shape[-1]]), sig=True)
            wq_ = sfs.shape[-1]
            for k in range(4):
                h = 4 * g + k
                b1 = 64 if h % 2 == 0 else 0
                b2 = 96 if h % 2 == 0 else 32
                sl = slice(32 * k, 32 * k + 32)
                p.emit("vector", lambda v, h=h, b1=b1, sl=sl, wq_=wq_:
                       v.tensor_add(qg[4 + h % 4][b1:b1 + 32, 0:wq_],
                                    prods[0][sl, 0:wq_], prods[1][sl, 0:wq_]),
                       waits=[e5, e6], sig=True)
                p.emit("vector", lambda v, h=h, b2=b2, sl=sl, wq_=wq_:
                       v.tensor_sub(qg[4 + h % 4][b2:b2 + 32, 0:wq_],
                                    prods[2][sl, 0:wq_], prods[3][sl, 0:wq_]),
                       waits=[e7, e8], sig=True)
            if PH >= 5:
                for k in range(4):
                    h = 4 * g + k
                    emit_l1attn_head(p, bufs, tabs, W, h, qc, qg[h % 4],
                                     qg[4 + h % 4], quc, ut)
        if PH < 6:
            continue
        for ct in range(8):
            ewo = p.dma("gpsimd", wbf[ct % 2][:],
                        wo_ap[:, ct * 1024:(ct + 1) * 1024])
            mm = None
            for hp in range(8):
                mm = p.emit("tensor", lambda t, hp=hp, ct=ct: t.matmul(
                    ps_a[:, 0:512], wbf[ct % 2][:, hp * 128:(hp + 1) * 128],
                    ut[:, hp * QC:(hp + 1) * QC], start=(hp == 0),
                    stop=(hp == 7)), waits=[ewo], sig=(hp == 7))
            p.emit("vector", lambda v, ct=ct: v.tensor_scalar_add(
                ych[:, ct * 512:(ct + 1) * 512], ps_a[:, 0:512],
                W["t_bo"][:, ct:ct + 1]), waits=[mm], sig=True)
        sevs = ln_stats_chunk(
            p, lambda j: ych[:, j * 512:(j + 1) * 512],
            sqb, xcb, ps_a[0:1, 0:512], ps_b[0:1, 0:512], rsum, 0,
            ones_col, [], rsT=512)
        revs = ln_rows(p, rsum, rw, rowbf, sevs, c0=0, c1=QC, rsT=512,
                       rbc0=0, rbT=1024)
        eldx = p.dma("sync", xrel[:].rearrange("p (j t) -> p j t", j=NDT),
                     xrow_in.rearrange("(j p) t -> p j t", p=128)[
                         :, :, q0:q0 + QC])
        eR = bcast_to_psum(p, ps_a[:, 0:512], rowbf[0:1, 0:QC],
                           ones_row, revs)
        eA = bcast_to_psum(p, ps_b[:, 0:512], rowbf[0:1, 1024:1024 + QC],
                           ones_row, revs)
        for j in range(NDT):
            tmp = tmf[j % 2]
            e1 = p.emit("vector", lambda v, j=j, tmp=tmp: v.tensor_mul(
                tmp[:, 0:512], ych[:, j * 512:(j + 1) * 512], ps_a[:, 0:512]),
                waits=[eR], sig=True)
            e2 = p.emit("vector", lambda v, tmp=tmp: v.tensor_add(
                tmp[:, 0:512], tmp[:, 0:512], ps_b[:, 0:512]),
                waits=[e1, eA], sig=True)
            e3 = p.emit("vector", lambda v, j=j, tmp=tmp: v.tensor_add(
                xrel[:, j * 512:(j + 1) * 512], xrel[:, j * 512:(j + 1) * 512],
                tmp[:, 0:512]), waits=[e2, eldx], sig=True)
            st_evs.append(p.dma(
                "scalar",
                xrow_out.rearrange("(j p) t -> p j t", p=128)[:, j, q0:q0 + QC],
                xrel[:, j * 512:(j + 1) * 512], waits=[e3]))
    return st_evs


def emit_l1attn_head(p, bufs, tabs, W, h, qc, qgh, qgph, quc, ut):
    import os
    HD = int(os.environ.get("L1_HEAD", "9"))
    ktb = bufs["ktb"]
    vaug = bufs["vaug"]
    pbuf = bufs["pbuf"]
    pwb = bufs["pwb"]
    pk128 = bufs["pk128"]
    pk2 = bufs["pk2"]
    pcorn_s = bufs["pcorn"]
    recb = bufs["recb"]
    rrow = bufs["rrow"]
    ps_S = bufs["ps_a"] if h % 2 == 0 else bufs["ps_b"]
    ps_w, ps_d = bufs["ps_w"], bufs["ps_d"]
    ps_av = bufs["ps_av"][h % 2]
    ps_m = bufs["ps_m"]
    htab = tabs["c_htab"]
    q0 = qc * QC
    hp = h // 2
    bA = 0 if h % 2 == 0 else 64
    b1 = 64 if h % 2 == 0 else 0
    b2 = 96 if h % 2 == 0 else 32
    cc8 = p.constcol(0.125)
    ktr2 = ktb[:].rearrange("p (m kt c) -> p m kt c", m=8, kt=16, c=128)
    qur2 = quc[:].rearrange("p (m qt c) -> p m qt c", m=8, qt=4, c=128)
    vav = vaug[:].rearrange("p (tt hh c) -> p tt hh c", tt=16, hh=16, c=65)

    kts = [kt for kt in range(qc * 4 + 1, qc * 4 + 5) if kt < 16]
    ncor = len(kts)
    ec2 = None
    if ncor and HD >= 3:
        ec1 = None
        for i, kt_ in enumerate(kts):
            kap = ktr2[bA:bA + 64, hp, kt_, 0:1]
            qap = qur2[bA:bA + 64, hp, kt_ - qc * 4 - 1, 127:128]
            ec1 = p.emit("tensor", lambda t, i=i, kap=kap, qap=qap: t.matmul(
                ps_d[0:1, i:i + 1], kap, qap, start=True, stop=True),
                sig=(i == ncor - 1))
        ec2 = p.emit("scalar", lambda s: s.activation(
            pcorn_s[0:1, 0:ncor], ps_d[0:1, 0:ncor], AF.Exp,
            scale=cc8[0:1, 0:1]), waits=[ec1], sig=True)
    for kt in range(16):
        buf = pbuf[kt % 2]
        lo_t, hi_t = qc * 4, qc * 4 + 3
        kind = "main" if kt < lo_t else ("wrap" if kt > hi_t else "mix")
        if HD < 4 and kind == "mix":
            kind = "main"
        if HD < 2 and kind == "wrap":
            kind = "main"
        if kind == "main":
            eS = emit_score_matmuls(p, ps_S, ktb, htab, qgh, quc, hp, kt,
                                    bA, b1, b2)
            eP = p.emit("scalar", lambda s, buf=buf: s.activation(
                buf[:], ps_S[:, 0:512], AF.Exp, scale=cc8), waits=[eS],
                sig=True)
        elif kind == "wrap":
            eS = emit_score_matmuls(p, ps_S, ktb, htab, qgph, quc, hp, kt,
                                    bA, b1, b2)
            eP = p.emit("scalar", lambda s, buf=buf: s.activation(
                buf[:], ps_S[:, 0:512], AF.Exp, scale=cc8), waits=[eS],
                sig=True)
        else:
            eS = emit_score_matmuls(p, ps_S, ktb, htab, qgh, quc, hp, kt,
                                    bA, b1, b2)
            eP = p.emit("scalar", lambda s, buf=buf: s.activation(
                buf[:], ps_S[:, 0:512], AF.Exp, scale=cc8), waits=[eS],
                sig=True)
            eSw = emit_score_matmuls(p, ps_w, ktb, htab, qgph, quc, hp, kt,
                                     bA, b1, b2)
            dcol = (kt - lo_t) * 128
            if dcol > 0:
                eP = p.emit("scalar", lambda s, buf=buf, dcol=dcol:
                            s.activation(buf[:, 0:dcol], ps_w[:, 0:dcol],
                                         AF.Exp, scale=cc8),
                            waits=[eSw, eP], sig=True)
            eK = p.emit("tensor", lambda t, kt=kt, dcol=dcol: t.matmul(
                ps_d[:, 0:128],
                ktr2[bA:bA + 64, hp, kt, :],
                qur2[bA:bA + 64, hp, (q0 + dcol) // 128 - qc * 4, :],
                start=True, stop=True), waits=[eP], sig=True)
            eK2 = p.emit("scalar", lambda s: s.activation(
                pk128[:], ps_d[:, 0:128], AF.Exp, scale=cc8),
                waits=[eK], sig=True)
            e1 = p.emit("vector", lambda v: v.tensor_mul(
                pk2[:], pk128[:], tabs["c_md1"][:]), waits=[eK2], sig=True)
            eW1 = p.emit("scalar", lambda s, dcol=dcol: s.activation(
                pwb[:, 0:128], ps_w[:, dcol:dcol + 128], AF.Exp, scale=cc8),
                waits=[eSw], sig=True)
            eW2 = p.emit("vector", lambda v: v.tensor_mul(
                pwb[:, 128:256], pwb[:, 0:128], tabs["c_mup"][:]),
                waits=[eW1], sig=True)
            e2 = p.emit("vector", lambda v, buf=buf, dcol=dcol: v.tensor_mul(
                buf[:, dcol:dcol + 128], buf[:, dcol:dcol + 128],
                tabs["c_mlow"][:]), waits=[eP], sig=True)
            e4 = p.emit("vector", lambda v, buf=buf, dcol=dcol: v.tensor_add(
                buf[:, dcol:dcol + 128], buf[:, dcol:dcol + 128], pk2[:]),
                waits=[e1, e2], sig=True)
            eP = p.emit("vector", lambda v, buf=buf, dcol=dcol: v.tensor_add(
                buf[:, dcol:dcol + 128], buf[:, dcol:dcol + 128],
                pwb[:, 128:256]), waits=[e4, eW2], sig=True)
        if kt in kts and HD >= 3:
            ci = kt - (qc * 4 + 1)
            cloc = (kt - 1 - qc * 4) * 128 + 127
            eP = p.emit("vector", lambda v, buf=buf, ci=ci, cloc=cloc:
                        v.tensor_copy(buf[0:1, cloc:cloc + 1],
                                      pcorn_s[0:1, ci:ci + 1]),
                        waits=[eP, ec2], sig=True)
        if HD >= 5:
            p.emit("tensor", lambda t, kt=kt, buf=buf: t.matmul(
                ps_av[0:65, 0:512], vav[:, kt, h, :], buf[:],
                start=(kt == 0), stop=(kt == 15)), waits=[eP], sig=True)
    if HD < 6:
        return
    e1 = p.emit("vector", lambda v: v.tensor_copy(
        rrow[0:1, 0:512], ps_av[64:65, 0:512]), sig=True)
    e2 = p.emit("scalar", lambda s: s.activation(
        rrow[0:1, 512:1024], rrow[0:1, 0:512], AF.Ln), waits=[e1], sig=True)
    cc_m1h = p.constcol(-1.0)[0:1, 0:1]
    e3 = p.emit("scalar", lambda s: s.activation(
        rrow[0:1, 0:512], rrow[0:1, 512:1024], AF.Exp,
        scale=cc_m1h), waits=[e2], sig=True)
    e4 = p.emit("vector", lambda v: v.tensor_copy(
        rrow[0:1, 512:1024].bitcast(BF16)[0:1, 0:512], rrow[0:1, 0:512]),
        waits=[e3], sig=True)
    e5 = bcast_to_psum(p, ps_m[:, 0:512],
                       rrow[0:1, 512:1024].bitcast(BF16)[0:1, 0:512],
                       tabs["c_ones_row"], [e4])
    e6 = p.emit("scalar", lambda s: s.copy(recb[:], ps_m[:, 0:512]),
                waits=[e5], sig=True)
    p.emit("vector", lambda v: v.tensor_mul(
        ut[bA:bA + 64, hp * QC:(hp + 1) * QC], ps_av[0:64, 0:512],
        recb[bA:bA + 64, :]), waits=[e6], sig=True)


def emit_score_matmuls(p, ps, ktb, htab, qgx, quc, hp, kt, bA, b1, b2):
    bG = 64 - bA
    mmA = lambda t, start, stop: t.matmul(
        ps[:, 0:512], ktb[bA:bA + 64, hp * T + kt * 128:hp * T + (kt + 1) * 128],
        quc[bA:bA + 64, hp * QC:(hp + 1) * QC], start=start, stop=stop,
        skip_group_check=True)
    mmG = lambda t, start, stop: t.matmul(
        ps[:, 0:512], htab[bG:bG + 64, kt * 128:(kt + 1) * 128],
        qgx[bG:bG + 64, :], start=start, stop=stop, skip_group_check=True)
    if bA == 0:
        ei = p.emit("tensor", lambda t: mmA(t, True, False), sig=True)
        return p.emit("tensor", lambda t: mmG(t, False, True), sig=True,
                      force_waits=[ei])
    else:
        ei = p.emit("tensor", lambda t: mmG(t, True, False), sig=True)
        return p.emit("tensor", lambda t: mmA(t, False, True), sig=True,
                      force_waits=[ei])


def alloc_l1attn_bufs(p):
    bufs = {}
    bufs["xs"] = p.sbuf("xs", [128, NDT * CH], F32)
    bufs["xnb"] = p.sbuf("xnb", [128, NDT * T], BF16)
    bufs["ktb"] = p.sbuf("ktb", [128, NDT * T], BF16)
    bufs["qvpc"] = p.sbuf("qvpc", [128, 8 * 514], BF16)
    bufs["vaug"] = p.sbuf("vaug", [128, 16 * 16 * 65], BF16)
    bufs["scr16"] = p.sbuf("scr16", [128, 8192], BF16)
    s16 = bufs["scr16"]
    bufs["qg"] = [s16[:, i * 512:(i + 1) * 512] for i in range(8)]
    bufs["prods"] = [s16[:, 4096 + i * 512:4096 + (i + 1) * 512]
                     for i in range(4)]
    bufs["pbuf"] = [s16[:, 6144 + i * 512:6144 + (i + 1) * 512]
                    for i in range(2)]
    bufs["pwb"] = s16[:, 7168:7424]
    bufs["pk128"] = s16[:, 7424:7552]
    bufs["pk2"] = s16[:, 7552:7680]
    bufs["recb"] = s16[:, 7680:8192]
    bufs["pcorn"] = p.sbuf("pcorn", [1, 4], F32)
    _wbf = p.sbuf("wbf", [128, 1024], BF16)
    bufs["wbf"] = [_wbf, _wbf]
    bufs["sqb"] = [s16[:, 2048 + i * 512:2048 + (i + 1) * 512]
                   for i in range(2)]
    bufs["xcb"] = [s16[:, 3072 + i * 512:3072 + (i + 1) * 512]
                   for i in range(2)]
    _t32 = s16[:, 0:2048].bitcast(F32)
    bufs["tmf"] = [_t32[:, 0:512], _t32[:, 512:1024]]
    bufs["rsum"] = p.sbuf("rsum", [1, 1024], F32)
    bufs["rw"] = s16[:, 4096:7168].bitcast(F32)[0:1, 0:1536]
    bufs["rrow"] = p.sbuf("rrow", [1, 1024], F32)
    bufs["rowbf"] = p.sbuf("rowbf", [1, 2048], BF16)
    bufs["ps_a"] = p.psum("ps_a", [128, 512])
    bufs["ps_b"] = p.psum("ps_b", [128, 512])
    bufs["ps_w"] = p.psum("ps_w", [128, 512])
    bufs["ps_d"] = p.psum("ps_d", [128, 128])
    bufs["ps_av"] = [p.psum(f"ps_av{i}", [128, 512]) for i in range(2)]
    bufs["ps_m"] = p.psum("ps_m", [128, 512])
    return bufs


def emit_block_l1attn(p, xin, xout):
    wq = p.dram_in("t_wq", [128, 8192], F32)
    wqp = p.dram_in("t_wqp", [128, 8192], F32)
    wk = p.dram_in("t_wk", [128, 8192], F32)
    wv = p.dram_in("t_wv", [1024, 1024], F32)
    wo = p.dram_in("t_wo", [128, 8192], F32)
    tabs_np = host_tables()
    del tabs_np["c_ident"], tabs_np["c_onesf"]
    tabs_np.update(l1attn_tables())
    tabs, tev = declare_tables(p, tabs_np)
    Wsb, wev = declare_weights(p, L1ATTN_WEIGHT_DECLS)
    bufs = alloc_l1attn_bufs(p)
    st_evs = []
    for r in range(ROWS_PER_CORE):
        st_evs += emit_l1attn_row(p, bufs, tabs, Wsb, wq, wqp, wk, wv, wo,
                                  xin[r], xout[r])
    return st_evs


def build_l1attn_program(serial=True, dbg=False):
    p = Prog(serial=serial)
    p.dbg = dbg
    xin = p.dram_in("xT", [ROWS_PER_CORE, D, T], F32)
    xout = p.dram_out("yT", [ROWS_PER_CORE, D, T], F32)
    st_evs = emit_block_l1attn(p, xin, xout)
    nc = p.finalize(final_waits=st_evs)
    return p, nc


def run_l1attn(x, inputs, serial=True):
    shards = _shard_x(x)
    tabs = host_tables()
    del tabs["c_ident"], tabs["c_onesf"]
    tabs.update(l1attn_tables())
    wprep = prep_l1attn(inputs)
    p, nc = build_l1attn_program(serial=serial)
    in_maps = []
    for c in range(NCORES):
        m = {"xT": shards[c]}
        m.update(tabs)
        m.update({k: np.ascontiguousarray(v, dtype=np.float32)
                  for k, v in wprep.items()})
        in_maps.append(m)
    res = run_bass_kernel_spmd(nc, in_maps, core_ids=list(range(NCORES)))
    p.close()
    out = np.concatenate([r["yT"] for r in res.results], axis=0)
    return np.ascontiguousarray(np.transpose(out, (0, 2, 1)))


DC = 256
HID = 2048
FEPS = 1e-6


def prep_l1ffn(inp):
    perm66 = None
    We = inp["L1_ffn_We_gate"]
    Wg = inp["L1_ffn_Wg_gate"]
    gate = np.zeros((1024, 66), np.float32)
    gate[:, 0:4] = We[:, 0:4]
    gate[:, 32:36] = We[:, 4:8]
    gate[:, 64:66] = Wg
    gbias = np.zeros((66, 1), np.float32)
    gbias[0:4, 0] = inp["L1_ffn_expert_bias"][0:4]
    gbias[32:36, 0] = inp["L1_ffn_expert_bias"][4:8]
    gbias[64:66, 0] = inp["L1_ffn_group_bias"]
    return {
        "m_wq": _prep2(inp["L1_ffn_Wq"]),
        "m_wkc": np.ascontiguousarray(
            inp["L1_ffn_Wk_c"].reshape(8, 128, 2, 128).transpose(1, 2, 0, 3)
            .reshape(128, 2048)),
        "m_wvc": np.ascontiguousarray(
            inp["L1_ffn_Wv_c"].reshape(8, 128, 2, 128).transpose(1, 2, 0, 3)
            .reshape(128, 2048)),
        "m_wk": np.ascontiguousarray(
            inp["L1_ffn_Wk"].reshape(2, 128, 8, 128).transpose(1, 2, 0, 3)
            .reshape(128, 2048)),
        "m_wv": np.ascontiguousarray(inp["L1_ffn_Wv"]),
        "m_wo": _prep2(inp["L1_ffn_Wo"]),
        "m_wsi": np.ascontiguousarray(
            inp["L1_ffn_W_shared_in"].reshape(8, 128, 32, 128)
            .transpose(1, 2, 0, 3).reshape(128, 32 * 1024)),
        "m_w1s": np.ascontiguousarray(
            inp["L1_ffn_W1_shared"].reshape(8, 128, 32, 128)
            .transpose(1, 2, 0, 3).reshape(128, 32 * 1024)),
        "m_wso": np.ascontiguousarray(inp["L1_ffn_W_shared_out"]),
        "m_gate": np.ascontiguousarray(
            gate.reshape(8, 128, 66).transpose(1, 0, 2).reshape(128, 528)),
        "m_gbias": gbias,
        "m_n1w": _prep_dcol(inp["L1_ffn_norm1_w"]),
        "m_n2w": _prep_dcol(inp["L1_ffn_norm2_w"]),
    }


L1FFN_WEIGHT_DECLS = [
    ("m_wk", (128, 2048), BF16),
    ("m_gate", (128, 528), BF16),
    ("m_gbias", (66, 1), F32),
    ("m_n1w", (128, 8), F32),
    ("m_n2w", (128, 8), F32),
]


def l1ffn_tables():
    inv = (1.0 / 10000.0 ** (np.arange(0, 64, 2, dtype=np.float64) / 64))
    t = np.arange(T, dtype=np.float64)
    sin = np.sin(t[None, :] * inv[:, None])
    cos = np.cos(t[None, :] * inv[:, None])
    T1 = np.zeros((64, T)); T2 = np.zeros((64, T))
    T1[0::2] = cos; T1[1::2] = cos
    T2[0::2] = -sin; T2[1::2] = sin
    return {
        "c_rt1": np.ascontiguousarray(np.tile(T1, (2, 1)).astype(BF16NP)),
        "c_rt2": np.ascontiguousarray(np.tile(T2, (2, 1)).astype(BF16NP)),
        "c_identf": np.eye(128, dtype=np.float32),
    }


def emit_rope(p, dst, srcap, tabs, shuf, waits=()):
    pass


def emit_l1ffn_row(p, bufs, tabs, W, waps, xrow_in, xrow_out):
    xs = bufs["xs"]
    xnb = bufs["xnb"]
    x1b = bufs["x1b"]
    ktb = bufs["ktb"]
    vaug = bufs["vaug"]
    cb = bufs["cb"]
    scr16 = bufs["scr16"]
    pbuf = bufs["pbuf"]
    recb = bufs["recb"]
    rrow = bufs["rrow"]
    wtok = bufs["wtok"]
    lgsb = bufs["lgsb"]
    gtt = bufs["gtt"]
    wbf = bufs["wbf"]
    sqb, xcb = bufs["sqb"], bufs["xcb"]
    tmf = bufs["tmf"]
    rsum, rw, rowbf = bufs["rsum"], bufs["rw"], bufs["rowbf"]
    ps_a, ps_b = bufs["ps_a"], bufs["ps_b"]
    ps_w, ps_d = bufs["ps_w"], bufs["ps_d"]
    ps_av = bufs["ps_av"]
    ps_m = bufs["ps_m"]
    ones_col = tabs["c_ones_col"]
    ones_row = tabs["c_ones_row"]
    rt1, rt2 = tabs["c_rt1"], tabs["c_rt2"]
    swapmask = []
    for i in range(16):
        swapmask += [2 * i + 1, 2 * i]
    quc = xs[:, 4096:6144].bitcast(BF16)
    ut = xs[:, 6144:8192].bitcast(BF16)
    ych = xs[:, 0:4096]
    xrel = xs[:, 4096:8192]

    def xnb_t(j, c0=0, c1=T):
        return xnb[:, j * T + c0:j * T + c1]

    for ci in range(NCH):
        h0, h1 = ci * CH, (ci + 1) * CH
        eld = p.dma("sync",
                    xs[:].rearrange("p (j t) -> p j t", j=NDT),
                    xrow_in.rearrange("(j p) t -> p j t", p=128)[:, :, h0:h1])
        revs = []
        for s0 in range(0, CH, 512):
            sevs = ln_stats_chunk(
                p, lambda j, s0=s0: xs[:, j * CH + s0:j * CH + s0 + 512],
                sqb, xcb, ps_a[0:1, 0:512], ps_b[0:1, 0:512], rsum, 0,
                ones_col, [eld], rsT=512)
            revs += ln_rows(p, rsum, rw, rowbf, sevs, c0=0, c1=512, rsT=512,
                            rbc0=s0, rbT=1024)
        eR = bcast_to_psum(p, ps_a[:, 0:512], rowbf[0:1, 0:512],
                           ones_row, revs)
        eR2 = bcast_to_psum(p, ps_w[:, 0:512], rowbf[0:1, 512:1024],
                            ones_row, revs)
        eA = bcast_to_psum(p, ps_b[:, 0:512], rowbf[0:1, 1024:1536],
                           ones_row, revs)
        eA2 = bcast_to_psum(p, ps_m[:, 0:512], rowbf[0:1, 1536:2048],
                            ones_row, revs)
        for j in range(NDT):
            for (sh, eRx, eAx, psR, psA) in ((0, eR, eA, ps_a, ps_b),
                                             (512, eR2, eA2, ps_w, ps_m)):
                tmp = tmf[j % 2]
                e1 = p.emit("vector", lambda v, j=j, sh=sh, tmp=tmp, psR=psR,
                            ci=ci: v.tensor_mul(
                                tmp[:, 0:512],
                                xs[:, j * CH + sh:j * CH + sh + 512],
                                psR[:, 0:512]), waits=[eRx], sig=True)
                p.emit("vector", lambda v, j=j, sh=sh, tmp=tmp, psA=psA,
                       h0=h0: v.tensor_add(
                           xnb_t(j, h0 + sh, h0 + sh + 512), tmp[:, 0:512],
                           psA[:, 0:512]), waits=[e1, eAx], sig=True)

    for lowrank, wname in ((0, "m_wvc"), (1, "m_wkc")):
        for m in range(2):
            ewc = p.dma("gpsimd", bufs["wbf2"][:],
                        waps[wname][:, m * 1024:(m + 1) * 1024])
            for c0 in range(0, T, 512):
                mm = None
                for j in range(NDT):
                    mm = p.emit("tensor", lambda t, j=j, m=m, c0=c0:
                                t.matmul(
                                    ps_a[:, 0:512],
                                    bufs["wbf2"][:, j * 128:(j + 1) * 128],
                                    xnb_t(j, c0, c0 + 512),
                                    start=(j == 0), stop=(j == NDT - 1)),
                                waits=[ewc], sig=(j == NDT - 1))
                p.emit("scalar", lambda s, m=m, c0=c0: s.copy(
                    cb[:, m * T + c0:m * T + c0 + 512], ps_a[:, 0:512]),
                    waits=[mm], sig=True)
        if lowrank == 0:
            vr = vaug[:].rearrange("p (tt hh c) -> p tt hh c", tt=16, hh=16,
                                   c=65)
            p.emit("vector", lambda v: v.memset(vr[:, :, :, 64:65], 1.0),
                   sig=True)
            ewv = p.dma("gpsimd", scr16[:, 0:2048].rearrange(
                "p (m c) -> p m c", m=2),
                waps["m_wv"].rearrange("(m p) c -> p m c", p=128))
            for tt in range(16):
                for half in range(2):
                    mm = None
                    for m in range(2):
                        mm = p.emit("tensor", lambda t, m=m, tt=tt, half=half:
                                    t.matmul(
                                        ps_a[:, 0:512],
                                        cb[:, m * T + tt * 128:
                                           m * T + (tt + 1) * 128],
                                        scr16[:, m * 1024 + half * 512:
                                              m * 1024 + (half + 1) * 512],
                                        start=(m == 0), stop=(m == 1)),
                                    waits=[ewv], sig=(m == 1))
                    p.emit("vector", lambda v, tt=tt, half=half: v.tensor_copy(
                        vr[:, tt, half * 8:(half + 1) * 8, 0:64],
                        ps_a[:, 0:512]), waits=[mm], sig=True)
    for m in range(8):
        for c0 in range(0, T, 512):
            mm = None
            for dc in range(2):
                mm = p.emit("tensor", lambda t, dc=dc, m=m, c0=c0: t.matmul(
                    ps_a[:, 0:512],
                    W["m_wk"][:, (m * 2 + dc) * 128:(m * 2 + dc) * 128 + 128],
                    cb[:, dc * T + c0:dc * T + c0 + 512],
                    start=(dc == 0), stop=(dc == 1)), sig=(dc == 1))
            ek = p.emit("scalar", lambda s, m=m, c0=c0: s.copy(
                ktb[:, m * T + c0:m * T + c0 + 512], ps_a[:, 0:512]),
                waits=[mm], sig=True)
    for m in range(8):
        for c0 in range(0, T, 512):
            kt_ = ktb[:, m * T + c0:m * T + c0 + 512]
            e1 = p.emit("vector", lambda v, kt_=kt_: v.stream_shuffle(
                bufs["rsc"][:], kt_, swapmask), sig=True)
            e2 = p.emit("vector", lambda v, kt_=kt_, c0=c0: v.tensor_mul(
                bufs["rsc2"][:], kt_, rt1[:, c0:c0 + 512]), sig=True)
            e3 = p.emit("vector", lambda v, c0=c0: v.tensor_mul(
                bufs["rsc"][:], bufs["rsc"][:], rt2[:, c0:c0 + 512]),
                waits=[e1], sig=True)
            p.emit("vector", lambda v, kt_=kt_: v.tensor_add(
                kt_, bufs["rsc2"][:], bufs["rsc"][:]), waits=[e2, e3],
                sig=True)

    for qc in range(NQC):
        q0 = qc * QC
        for m in range(8):
            ewq = p.dma("gpsimd", wbf[0][:],
                        waps["m_wq"][:, m * 1024:(m + 1) * 1024])
            mm = None
            for j in range(NDT):
                mm = p.emit("tensor", lambda t, j=j, m=m, q0=q0: t.matmul(
                    ps_a[:, 0:512], wbf[0][:, j * 128:(j + 1) * 128],
                    xnb_t(j, q0, q0 + QC), start=(j == 0), stop=(j == NDT - 1)),
                    waits=[ewq], sig=(j == NDT - 1))
            eq = p.emit("scalar", lambda s, m=m: s.copy(
                quc[:, m * QC:(m + 1) * QC], ps_a[:, 0:512]), waits=[mm],
                sig=True)
            qt_ = quc[:, m * QC:(m + 1) * QC]
            e1 = p.emit("vector", lambda v, qt_=qt_: v.stream_shuffle(
                bufs["rsc"][:], qt_, swapmask), waits=[eq], sig=True)
            e2 = p.emit("vector", lambda v, qt_=qt_, q0=q0: v.tensor_mul(
                bufs["rsc2"][:], qt_, rt1[:, q0:q0 + QC]), waits=[eq], sig=True)
            e3 = p.emit("vector", lambda v, q0=q0: v.tensor_mul(
                bufs["rsc"][:], bufs["rsc"][:], rt2[:, q0:q0 + QC]),
                waits=[e1], sig=True)
            p.emit("vector", lambda v, qt_=qt_: v.tensor_add(
                qt_, bufs["rsc2"][:], bufs["rsc"][:]), waits=[e2, e3],
                sig=True)
        cc8 = p.constcol(0.125)
        cc_m1h = p.constcol(-1.0)[0:1, 0:1]
        vav = vaug[:].rearrange("p (tt hh c) -> p tt hh c", tt=16, hh=16, c=65)
        for h in range(NH):
            hp = h // 2
            bA = 0 if h % 2 == 0 else 64
            ps_S = ps_a if h % 2 == 0 else ps_b
            ps_avh = ps_av[h % 2]
            for kt in range(16):
                buf = pbuf[kt % 2]
                eS = p.emit("tensor", lambda t, hp=hp, kt=kt, bA=bA, ps_S=ps_S:
                            t.matmul(
                                ps_S[:, 0:512],
                                ktb[bA:bA + 64, hp * T + kt * 128:
                                    hp * T + (kt + 1) * 128],
                                quc[bA:bA + 64, hp * QC:(hp + 1) * QC],
                                start=True, stop=True), sig=True)
                eP = p.emit("scalar", lambda s, buf=buf, ps_S=ps_S:
                            s.activation(buf[:], ps_S[:, 0:512], AF.Exp,
                                         scale=cc8), waits=[eS], sig=True)
                p.emit("tensor", lambda t, kt=kt, buf=buf, h=h, ps_avh=ps_avh:
                       t.matmul(ps_avh[0:65, 0:512], vav[:, kt, h, :], buf[:],
                                start=(kt == 0), stop=(kt == 15)),
                       waits=[eP], sig=True)
            e1 = p.emit("vector", lambda v, ps_avh=ps_avh: v.tensor_copy(
                rrow[0:1, 0:512], ps_avh[64:65, 0:512]), sig=True)
            e2 = p.emit("scalar", lambda s: s.activation(
                rrow[0:1, 512:1024], rrow[0:1, 0:512], AF.Ln), waits=[e1],
                sig=True)
            e3 = p.emit("scalar", lambda s: s.activation(
                rrow[0:1, 0:512], rrow[0:1, 512:1024], AF.Exp,
                scale=cc_m1h), waits=[e2], sig=True)
            e4 = p.emit("vector", lambda v: v.tensor_copy(
                rrow[0:1, 512:1024].bitcast(BF16)[0:1, 0:512],
                rrow[0:1, 0:512]), waits=[e3], sig=True)
            e5 = bcast_to_psum(p, ps_m[:, 0:512],
                               rrow[0:1, 512:1024].bitcast(BF16)[0:1, 0:512],
                               ones_row, [e4])
            e6 = p.emit("scalar", lambda s: s.copy(recb[:], ps_m[:, 0:512]),
                        waits=[e5], sig=True)
            p.emit("vector", lambda v, bA=bA, hp=hp, ps_avh=ps_avh:
                   v.tensor_mul(ut[bA:bA + 64, hp * QC:(hp + 1) * QC],
                                ps_avh[0:64, 0:512], recb[bA:bA + 64, :]),
                   waits=[e6], sig=True)
        for ct in range(8):
            ewo = p.dma("gpsimd", wbf[0][:],
                        waps["m_wo"][:, ct * 1024:(ct + 1) * 1024])
            mm = None
            for hp in range(8):
                mm = p.emit("tensor", lambda t, hp=hp, ct=ct: t.matmul(
                    ps_a[:, 0:512], wbf[0][:, hp * 128:(hp + 1) * 128],
                    ut[:, hp * QC:(hp + 1) * QC], start=(hp == 0),
                    stop=(hp == 7)), waits=[ewo], sig=(hp == 7))
            p.emit("vector", lambda v, ct=ct, q0=q0: v.tensor_add(
                ych[:, ct * 512:(ct + 1) * 512], ps_a[:, 0:512],
                xnb_t(ct, q0, q0 + QC)), waits=[mm], sig=True)
        sevs = ln_stats_chunk(
            p, lambda j: ych[:, j * 512:(j + 1) * 512],
            sqb, xcb, ps_a[0:1, 0:512], ps_b[0:1, 0:512], rsum, 0,
            ones_col, [], rsT=512)
        cc_eps = p.constcol(FEPS)[0:1, 0:1]
        cc_mh = p.constcol(-0.5)[0:1, 0:1]
        e1 = p.emit("vector", lambda v: v.tensor_scalar_mul(
            rw[0:1, 0:512], rsum[0:1, 512:1024], 1.0 / D), waits=sevs,
            sig=True)
        e2 = p.emit("scalar", lambda s: s.activation(
            rw[0:1, 512:1024], rw[0:1, 0:512], AF.Ln, bias=cc_eps),
            waits=[e1], sig=True)
        e3 = p.emit("scalar", lambda s: s.activation(
            rw[0:1, 0:512], rw[0:1, 512:1024], AF.Exp, scale=cc_mh),
            waits=[e2], sig=True)
        e4 = p.emit("vector", lambda v: v.tensor_copy(
            rowbf[0:1, 0:512], rw[0:1, 0:512]), waits=[e3], sig=True)
        eR = bcast_to_psum(p, ps_b[:, 0:512], rowbf[0:1, 0:512], ones_row,
                           [e4])
        for j in range(NDT):
            p.emit("vector", lambda v, j=j, q0=q0: v.scalar_tensor_tensor(
                x1b[:, j * T + q0:j * T + q0 + QC],
                ych[:, j * 512:(j + 1) * 512], W["m_n1w"][:, j:j + 1],
                ps_b[:, 0:512], ALU.mult, ALU.mult), waits=[eR], sig=True)
    return emit_l1ffn_moe(p, bufs, tabs, W, waps, xrow_in, xrow_out)
def emit_l1ffn_moe(p, bufs, tabs, W, waps, xrow_in, xrow_out):
    xs = bufs["xs"]
    xnb = bufs["xnb"]
    x1b = bufs["x1b"]
    scr16 = bufs["scr16"]
    wtok = bufs["wtok"]
    lgsb = bufs["lgsb"]
    gtt = bufs["gtt"]
    wbf = bufs["wbf"]
    sqb, xcb = bufs["sqb"], bufs["xcb"]
    tmf = bufs["tmf"]
    rsum, rw, rowbf = bufs["rsum"], bufs["rw"], bufs["rowbf"]
    ps_a, ps_b = bufs["ps_a"], bufs["ps_b"]
    ps_w, ps_d = bufs["ps_w"], bufs["ps_d"]
    ps_av = bufs["ps_av"]
    ps_m = bufs["ps_m"]
    ones_col = tabs["c_ones_col"]
    ones_row = tabs["c_ones_row"]
    identf = tabs["c_identf"]
    rsc, rsc2 = bufs["rsc"], bufs["rsc2"]
    swb = xnb[:, 0:8192]
    hb = xnb[:, 8192:16384]
    ych = xs[:, 0:4096]
    xrel = xs[:, 4096:8192]
    c1f = p.constcol(1.0)
    cc_half = p.constcol(0.5)
    cc_epsf = p.constcol(FEPS)[0:1, 0:1]
    cc_eps5 = p.constcol(1e-5)[0:1, 0:1]
    cc_mh = p.constcol(-0.5)[0:1, 0:1]

    for c0 in range(0, T, 512):
        mm = None
        for j in range(NDT):
            for (r0, r1) in ((0, 4), (32, 36), (64, 66)):
                gs = {0: 0, 32: 4, 64: 8}[r0]
                mm = p.emit("tensor", lambda t, j=j, c0=c0, r0=r0, r1=r1,
                            gs=gs: t.matmul(
                                ps_d[r0:r1, 0:512] if False else
                                ps_a[r0:r1, 0:512],
                                W["m_gate"][:, j * 66 + gs:j * 66 + gs +
                                            (r1 - r0)],
                                x1b[:, j * T + c0:j * T + c0 + 512],
                                start=(j == 0), stop=(j == NDT - 1),
                                skip_group_check=True),
                            sig=(j == NDT - 1 and r0 == 64))
        p.emit("vector", lambda v, c0=c0: v.tensor_scalar_add(
            lgsb[0:66, c0:c0 + 512], ps_a[0:66, 0:512],
            W["m_gbias"][0:66, 0:1]), waits=[mm], sig=True)
    for tt in range(16):
        et = p.emit("tensor", lambda t, tt=tt: t.transpose(
            ps_d[:, 0:66], lgsb[0:66, tt * 128:(tt + 1) * 128],
            identf[0:66, 0:66]), sig=True)
        ec1 = p.emit("vector", lambda v: v.tensor_copy(
            gtt[:, 0:4], ps_d[:, 0:4]), waits=[et], sig=True)
        ec2 = p.emit("vector", lambda v: v.tensor_copy(
            gtt[:, 4:8], ps_d[:, 32:36]), waits=[et], sig=True)
        ec3 = p.emit("vector", lambda v: v.tensor_copy(
            gtt[:, 8:10], ps_d[:, 64:66]), waits=[et], sig=True)
        ee = p.emit("scalar", lambda s: s.activation(
            gtt[:, 10:20], gtt[:, 0:10], AF.Exp), waits=[ec1, ec2, ec3],
            sig=True)
        ex = lambda i: gtt[:, 10 + i:11 + i]
        g = "gpsimd"
        e1 = p.emit(g, lambda gg: gg.tensor_add(gtt[:, 20:21], ex(0), ex(1)),
                    waits=[ee], sig=True)
        e1 = p.emit(g, lambda gg: gg.tensor_add(gtt[:, 20:21], gtt[:, 20:21],
                                                ex(2)), waits=[e1], sig=True)
        e1 = p.emit(g, lambda gg: gg.tensor_add(gtt[:, 20:21], gtt[:, 20:21],
                                                ex(3)), waits=[e1], sig=True)
        e2 = p.emit(g, lambda gg: gg.tensor_add(gtt[:, 21:22], ex(4), ex(5)),
                    waits=[ee], sig=True)
        e2 = p.emit(g, lambda gg: gg.tensor_add(gtt[:, 21:22], gtt[:, 21:22],
                                                ex(6)), waits=[e2], sig=True)
        e2 = p.emit(g, lambda gg: gg.tensor_add(gtt[:, 21:22], gtt[:, 21:22],
                                                ex(7)), waits=[e2], sig=True)
        e3 = p.emit(g, lambda gg: gg.tensor_sub(gtt[:, 22:23], gtt[:, 9:10],
                                                gtt[:, 8:9]), sig=True)
        e4 = p.emit("scalar", lambda s: s.sign(gtt[:, 23:24], gtt[:, 22:23]),
                    waits=[e3], sig=True)
        e5 = p.emit(g, lambda gg: gg.tensor_mul(gtt[:, 23:24], gtt[:, 23:24],
                                                cc_half[:, 0:1]), waits=[e4],
                    sig=True)
        e5 = p.emit(g, lambda gg: gg.tensor_add(gtt[:, 23:24], gtt[:, 23:24],
                                                cc_half[:, 0:1]), waits=[e5],
                    sig=True)
        e6 = p.emit(g, lambda gg: gg.tensor_sub(gtt[:, 24:25], gtt[:, 21:22],
                                                gtt[:, 20:21]), waits=[e1, e2],
                    sig=True)
        e6 = p.emit(g, lambda gg: gg.tensor_mul(gtt[:, 24:25], gtt[:, 24:25],
                                                gtt[:, 23:24]), waits=[e6, e5],
                    sig=True)
        e6 = p.emit(g, lambda gg: gg.tensor_add(gtt[:, 24:25], gtt[:, 24:25],
                                                gtt[:, 20:21]), waits=[e6],
                    sig=True)
        e7 = p.emit(g, lambda gg: gg.tensor_sub(gtt[:, 25:26], ex(9 - 1),
                                                ex(8 - 0)), waits=[ee],
                    sig=True)
        e7 = p.emit(g, lambda gg: gg.tensor_mul(gtt[:, 25:26], gtt[:, 25:26],
                                                gtt[:, 23:24]), waits=[e7, e5],
                    sig=True)
        e7 = p.emit(g, lambda gg: gg.tensor_add(gtt[:, 25:26], gtt[:, 25:26],
                                                ex(8)), waits=[e7], sig=True)
        e8 = p.emit(g, lambda gg: gg.tensor_add(gtt[:, 26:27], ex(8), ex(9)),
                    waits=[ee], sig=True)
        e9 = p.emit(g, lambda gg: gg.tensor_mul(gtt[:, 27:28], gtt[:, 24:25],
                                                gtt[:, 26:27]), waits=[e6, e8],
                    sig=True)
        e10 = p.emit("vector", lambda v: v.reciprocal(
            gtt[:, 28:29], gtt[:, 27:28]), waits=[e9], sig=True)
        e11 = p.emit(g, lambda gg: gg.tensor_mul(gtt[:, 29:30], gtt[:, 25:26],
                                                 gtt[:, 28:29]),
                     waits=[e10, e7], sig=True)
        e12 = p.emit(g, lambda gg: gg.tensor_sub(gtt[:, 30:31], c1f[:, 0:1],
                                                 gtt[:, 23:24]), waits=[e5],
                     sig=True)
        e12 = p.emit(g, lambda gg: gg.tensor_mul(gtt[:, 30:31], gtt[:, 30:31],
                                                 gtt[:, 29:30]),
                     waits=[e12, e11], sig=True)
        e13 = p.emit(g, lambda gg: gg.tensor_mul(gtt[:, 31:32], gtt[:, 23:24],
                                                 gtt[:, 29:30]),
                     waits=[e11], sig=True)
        e14 = p.emit("vector", lambda v: v.tensor_scalar_mul(
            gtt[:, 0:4], gtt[:, 10:14], gtt[:, 30:31]), waits=[e12], sig=True)
        e15 = p.emit("vector", lambda v: v.tensor_scalar_mul(
            gtt[:, 4:8], gtt[:, 14:18], gtt[:, 31:32]), waits=[e13], sig=True)
        e16 = p.emit("vector", lambda v: v.max(
            rsc[:, 0:8].bitcast(F32)[:, 0:4] if False else gtt[:, 10:18],
            gtt[:, 0:8]), waits=[e14, e15], sig=True)
        e17 = p.emit(g, lambda gg: gg.tensor_copy(gtt[:, 18:19],
                                                  gtt[:, 11:12]),
                     waits=[e16], sig=True)
        e18 = p.emit("vector", lambda v: v.tensor_scalar(
            gtt[:, 20:28], gtt[:, 0:8], gtt[:, 18:19], None, ALU.is_ge),
            waits=[e17], sig=True)
        e19 = p.emit(g, lambda gg: gg.tensor_mul(gtt[:, 0:8], gtt[:, 0:8],
                                                 gtt[:, 20:28]), waits=[e18],
                     sig=True)
        p.emit(g, lambda gg, tt=tt: gg.tensor_copy(
            wtok[:, tt * 8:(tt + 1) * 8], gtt[:, 0:8]), waits=[e19], sig=True)

    st_evs = []
    for tc in range(4):
        t0 = tc * 512
        for wname, dest in (("m_wsi", swb), ("m_w1s", hb)):
            for i in range(16):
                ew1 = p.dma("gpsimd", wbf[0][:],
                            waps[wname][:, i * 1024:(i + 1) * 1024])
                ew2 = p.dma("gpsimd", bufs["wbf2"][:],
                            waps[wname][:, (i + 16) * 1024:(i + 17) * 1024])
                mma = None
                mmb = None
                for j in range(NDT):
                    mma = p.emit("tensor", lambda t, j=j, t0=t0: t.matmul(
                        ps_a[:, 0:512], wbf[0][:, j * 128:(j + 1) * 128],
                        x1b[:, j * T + t0:j * T + t0 + 512],
                        start=(j == 0), stop=(j == NDT - 1)),
                        waits=[ew1], sig=(j == NDT - 1))
                for j in range(NDT):
                    mmb = p.emit("tensor", lambda t, j=j, t0=t0: t.matmul(
                        ps_w[:, 0:512], bufs["wbf2"][:, j * 128:(j + 1) * 128],
                        x1b[:, j * T + t0:j * T + t0 + 512],
                        start=(j == 0), stop=(j == NDT - 1)),
                        waits=[ew2], sig=(j == NDT - 1))
                es = p.emit("scalar", lambda s: s.activation(
                    rsc[:], ps_a[:, 0:512], AF.Silu), waits=[mma], sig=True)
                p.emit("vector", lambda v, i=i, dest=dest: v.tensor_mul(
                    dest[:, i * 512:(i + 1) * 512], rsc[:], ps_w[:, 0:512]),
                    waits=[es, mmb], sig=True)
        pso = [ps_a, ps_b, ps_w, ps_m]
        for ei in range(9):
            hsrc = swb if ei == 0 else hb
            for dh in range(2):
                mm = None
                for f in range(16):
                    if ei == 0:
                        ewo = p.dma("gpsimd", wbf[0][:, 0:1024],
                                    waps["m_wso"][f * 128:(f + 1) * 128, :])
                    else:
                        ewo = p.dma("gpsimd", wbf[0][:, 0:1024],
                                    waps["m_w2e"][ei - 1,
                                                  f * 128:(f + 1) * 128, :])
                    for ttl in range(4):
                        mm = p.emit("tensor", lambda t, f=f, ttl=ttl, dh=dh,
                                    hsrc=hsrc: t.matmul(
                                        pso[ttl][:, 0:512],
                                        hsrc[:, f * 512 + ttl * 128:
                                             f * 512 + (ttl + 1) * 128],
                                        wbf[0][:, dh * 512:(dh + 1) * 512],
                                        start=(f == 0), stop=(f == 15),
                                        skip_group_check=True),
                                    waits=[ewo],
                                    sig=(f == 15))
                for ttl in range(4):
                    dst = ych[:, ttl * 1024 + dh * 512:
                              ttl * 1024 + (dh + 1) * 512]
                    if ei == 0:
                        p.emit("vector", lambda v, ttl=ttl, dst=dst:
                               v.tensor_copy(dst, pso[ttl][:, 0:512]),
                               waits=[mm], sig=True)
                    else:
                        wcol = wtok[:, (tc * 4 + ttl) * 8 + ei - 1:
                                    (tc * 4 + ttl) * 8 + ei]
                        p.emit("vector", lambda v, ttl=ttl, dst=dst,
                               wcol=wcol: v.scalar_tensor_tensor(
                                   dst, pso[ttl][:, 0:512], wcol, dst,
                                   ALU.mult, ALU.add), waits=[mm], sig=True)
        ffnT = bufs["ffnT"]
        for ttl in range(4):
            for dj in range(NDT):
                et = p.emit("tensor", lambda t, ttl=ttl, dj=dj: t.transpose(
                    ps_d[:, 0:128],
                    ych[:, ttl * 1024 + dj * 128:ttl * 1024 + (dj + 1) * 128],
                    tabs["c_identf"][:]), sig=True)
                e1 = p.emit("vector", lambda v, ttl=ttl, dj=dj: v.tensor_copy(
                    ffnT[:, dj * 512 + ttl * 128:dj * 512 + (ttl + 1) * 128],
                    ps_d[:, 0:128]), waits=[et], sig=True)
        for j in range(NDT):
            p.emit("vector", lambda v, j=j, t0=t0: v.tensor_add(
                xrel[:, j * 512:(j + 1) * 512],
                ffnT[:, j * 512:(j + 1) * 512],
                x1b[:, j * T + t0:j * T + t0 + 512]), sig=True)
        sevs = ln_stats_chunk(
            p, lambda j: xrel[:, j * 512:(j + 1) * 512],
            sqb, xcb, ps_a[0:1, 0:512], ps_b[0:1, 0:512], rsum, 0,
            ones_col, [], rsT=512)
        e1 = p.emit("vector", lambda v: v.tensor_scalar_mul(
            rw[0:1, 0:512], rsum[0:1, 512:1024], 1.0 / D), waits=sevs,
            sig=True)
        e2 = p.emit("scalar", lambda s: s.activation(
            rw[0:1, 512:1024], rw[0:1, 0:512], AF.Ln, bias=cc_epsf),
            waits=[e1], sig=True)
        e3 = p.emit("scalar", lambda s: s.activation(
            rw[0:1, 0:512], rw[0:1, 512:1024], AF.Exp, scale=cc_mh),
            waits=[e2], sig=True)
        e4 = p.emit("vector", lambda v: v.tensor_copy(
            rowbf[0:1, 0:512], rw[0:1, 0:512]), waits=[e3], sig=True)
        eR = bcast_to_psum(p, ps_b[:, 0:512], rowbf[0:1, 0:512], ones_row,
                           [e4])
        for j in range(NDT):
            p.emit("vector", lambda v, j=j: v.scalar_tensor_tensor(
                ych[:, j * 512:(j + 1) * 512], xrel[:, j * 512:(j + 1) * 512],
                W["m_n2w"][:, j:j + 1], ps_b[:, 0:512], ALU.mult, ALU.mult),
                waits=[eR], sig=True)
        sevs = ln_stats_chunk(
            p, lambda j: ych[:, j * 512:(j + 1) * 512],
            sqb, xcb, ps_a[0:1, 0:512], ps_b[0:1, 0:512], rsum, 0,
            ones_col, [], rsT=512)
        revs = ln_rows(p, rsum, rw, rowbf, sevs, c0=0, c1=512, rsT=512,
                       rbc0=0, rbT=1024)
        eldx = p.dma("sync", xrel[:].rearrange("p (j t) -> p j t", j=NDT),
                     xrow_in.rearrange("(j p) t -> p j t", p=128)[
                         :, :, t0:t0 + 512])
        eR = bcast_to_psum(p, ps_a[:, 0:512], rowbf[0:1, 0:512],
                           ones_row, revs)
        eA = bcast_to_psum(p, ps_b[:, 0:512], rowbf[0:1, 1024:1536],
                           ones_row, revs)
        for j in range(NDT):
            tmp = tmf[j % 2]
            e1 = p.emit("vector", lambda v, j=j, tmp=tmp: v.tensor_mul(
                tmp[:, 0:512], ych[:, j * 512:(j + 1) * 512], ps_a[:, 0:512]),
                waits=[eR], sig=True)
            e2 = p.emit("vector", lambda v, tmp=tmp: v.tensor_add(
                tmp[:, 0:512], tmp[:, 0:512], ps_b[:, 0:512]),
                waits=[e1, eA], sig=True)
            e3 = p.emit("vector", lambda v, j=j, tmp=tmp: v.tensor_add(
                xrel[:, j * 512:(j + 1) * 512], xrel[:, j * 512:(j + 1) * 512],
                tmp[:, 0:512]), waits=[e2, eldx], sig=True)
            st_evs.append(p.dma(
                "scalar",
                xrow_out.rearrange("(j p) t -> p j t", p=128)[
                    :, j, t0:t0 + 512],
                xrel[:, j * 512:(j + 1) * 512], waits=[e3]))
    return st_evs


def alloc_l1ffn_bufs(p):
    bufs = {}
    bufs["xs"] = p.sbuf("xs", [128, NDT * CH], F32)
    bufs["xnb"] = p.sbuf("xnb", [128, NDT * T], BF16)
    bufs["x1b"] = p.sbuf("x1b", [128, NDT * T], BF16)
    bufs["ktb"] = p.sbuf("ktb", [128, NDT * T], BF16)
    bufs["vaug"] = p.sbuf("vaug", [128, 16 * 16 * 65], BF16)
    bufs["cb"] = p.sbuf("cb", [128, 2 * T], BF16)
    bufs["lgsb"] = bufs["cb"][:].bitcast(F32)[0:66, 0:2048]
    bufs["scr16"] = p.sbuf("scr16", [128, 8192], BF16)
    s16 = bufs["scr16"]
    bufs["rsc"] = s16[:, 0:512]
    bufs["rsc2"] = s16[:, 512:1024]
    bufs["pbuf"] = [s16[:, 1024 + i * 512:1024 + (i + 1) * 512]
                    for i in range(2)]
    bufs["recb"] = s16[:, 2048:2560]
    bufs["ffnT"] = s16[:, 0:4096]
    bufs["sqb"] = [s16[:, 4096 + i * 512:4096 + (i + 1) * 512]
                   for i in range(2)]
    bufs["xcb"] = [s16[:, 5120 + i * 512:5120 + (i + 1) * 512]
                   for i in range(2)]
    _t32 = s16[:, 6144:8192].bitcast(F32)
    bufs["tmf"] = [_t32[:, 0:512], _t32[:, 512:1024]]
    bufs["wtok"] = p.sbuf("wtok", [128, 128], F32)
    bufs["gtt"] = p.sbuf("gtt", [128, 32], F32)
    _wbf = p.sbuf("wbf", [128, 1024], BF16)
    bufs["wbf"] = [_wbf, _wbf]
    bufs["wbf2"] = p.sbuf("wbf2", [128, 1024], BF16)
    bufs["rsum"] = p.sbuf("rsum", [1, 1024], F32)
    _cbf = bufs["cb"][:].bitcast(F32)
    bufs["rw"] = _cbf[0:1, 0:1536]
    bufs["rrow"] = _cbf[32:33, 0:1024]
    bufs["rowbf"] = bufs["cb"][64:65, 0:2048]
    bufs["ps_a"] = p.psum("ps_a", [128, 512])
    bufs["ps_b"] = p.psum("ps_b", [128, 512])
    bufs["ps_w"] = p.psum("ps_w", [128, 512])
    bufs["ps_d"] = p.psum("ps_d", [128, 128])
    bufs["ps_av"] = [p.psum(f"ps_av{i}", [128, 512]) for i in range(2)]
    bufs["ps_m"] = p.psum("ps_m", [128, 512])
    return bufs


def emit_block_l1ffn(p, xin, xout):
    waps = {
        "m_wq": p.dram_in("m_wq", [128, 8192], F32),
        "m_wkc": p.dram_in("m_wkc", [128, 2048], F32),
        "m_wvc": p.dram_in("m_wvc", [128, 2048], F32),
        "m_wo": p.dram_in("m_wo", [128, 8192], F32),
        "m_wv": p.dram_in("m_wv", [256, 1024], F32),
        "m_wsi": p.dram_in("m_wsi", [128, 32 * 1024], F32),
        "m_w1s": p.dram_in("m_w1s", [128, 32 * 1024], F32),
        "m_wso": p.dram_in("m_wso", [2048, 1024], F32),
        "m_w2e": p.dram_in("m_w2e", [8, 2048, 1024], F32),
    }
    tabs_np = host_tables()
    del tabs_np["c_onesf"]
    tabs_np.update(l1ffn_tables())
    tabs, tev = declare_tables(p, tabs_np)
    Wsb, wev = declare_weights(p, L1FFN_WEIGHT_DECLS)
    bufs = alloc_l1ffn_bufs(p)
    st_evs = []
    for r in range(ROWS_PER_CORE):
        st_evs += emit_l1ffn_row(p, bufs, tabs, Wsb, waps, xin[r], xout[r])
    return st_evs


def build_l1ffn_program(serial=True, dbg=False):
    p = Prog(serial=serial)
    p.dbg = dbg
    xin = p.dram_in("xT", [ROWS_PER_CORE, D, T], F32)
    xout = p.dram_out("yT", [ROWS_PER_CORE, D, T], F32)
    st_evs = emit_block_l1ffn(p, xin, xout)
    nc = p.finalize(final_waits=st_evs)
    return p, nc


def run_l1ffn(x, inputs, serial=True):
    shards = _shard_x(x)
    tabs = host_tables()
    del tabs["c_onesf"]
    tabs.update(l1ffn_tables())
    wprep = prep_l1ffn(inputs)
    p, nc = build_l1ffn_program(serial=serial)
    in_maps = []
    for c in range(NCORES):
        m = {"xT": shards[c], "m_w2e": inputs["L1_ffn_W2_experts"]}
        m.update(tabs)
        m.update({k: np.ascontiguousarray(v, dtype=np.float32)
                  for k, v in wprep.items()})
        in_maps.append(m)
    res = run_bass_kernel_spmd(nc, in_maps, core_ids=list(range(NCORES)))
    p.close()
    out = np.concatenate([r["yT"] for r in res.results], axis=0)
    return np.ascontiguousarray(np.transpose(out, (0, 2, 1)))



def _shard_x(x):
    xT = np.ascontiguousarray(np.transpose(x, (0, 2, 1)))
    return [xT[c * ROWS_PER_CORE:(c + 1) * ROWS_PER_CORE] for c in range(NCORES)]


def run_l0attn(x, inputs, serial=True):
    shards = _shard_x(x)
    tabs = host_tables()
    wprep = prep_l0attn(inputs)
    p, nc = build_l0attn_program(serial=serial)
    in_maps = []
    for c in range(NCORES):
        m = {"xT": shards[c]}
        m.update(tabs)
        m.update({k: np.ascontiguousarray(v, dtype=np.float32)
                  for k, v in wprep.items()})
        in_maps.append(m)
    res = run_bass_kernel_spmd(nc, in_maps, core_ids=list(range(NCORES)))
    p.close()
    out = np.concatenate([r["yT"] for r in res.results], axis=0)
    return np.ascontiguousarray(np.transpose(out, (0, 2, 1)))


def build_fused_program(serial=True):
    p = Prog(serial=serial)
    xin = p.dram_in("xT", [ROWS_PER_CORE, D, T], F32)
    xout = p.dram_out("yT", [ROWS_PER_CORE, D, T], F32)
    xm = [p.nc.dram_tensor(f"xmid{i}", [ROWS_PER_CORE, D, T], F32,
                           kind="Internal").ap() for i in range(3)]
    p._ensure_consts()
    plan = [(emit_block_l0attn, xin, xm[0]),
            (emit_block_l0ffn, xm[0], xm[1]),
            (emit_block_l1attn, xm[1], xm[2]),
            (emit_block_l1ffn, xm[2], xout)]
    st_evs = []
    for fn, a, b in plan:
        p.push_scope()
        st_evs = fn(p, a, b)
        p.pop_scope()
    nc = p.finalize(final_waits=st_evs)
    return p, nc


def fused_in_map(inputs, shard):
    m = {"xT": shard,
         "f_w1": inputs["L0_ffn_w1"], "f_w2": inputs["L0_ffn_w2"],
         "m_w2e": inputs["L1_ffn_W2_experts"]}
    tabs = host_tables()
    tabs.update(l1attn_tables())
    tabs.update(l1ffn_tables())
    m.update(tabs)
    for prep in (prep_l0attn, prep_l0ffn, prep_l1attn, prep_l1ffn):
        m.update({k: np.ascontiguousarray(v, dtype=np.float32)
                  for k, v in prep(inputs).items()})
    return m


def kernel(**inputs):
    x = inputs["L0_attn_x"]
    shards = _shard_x(x)
    p, nc = build_fused_program(serial=True)
    in_maps = [fused_in_map(inputs, shards[c]) for c in range(NCORES)]
    res = run_bass_kernel_spmd(nc, in_maps, core_ids=list(range(NCORES)))
    p.close()
    out = np.concatenate([r["yT"] for r in res.results], axis=0)
    return np.ascontiguousarray(np.transpose(out, (0, 2, 1)))


# version 05 (final)
# speedup vs baseline: 1.7192x; 1.7192x over previous
import sys
for _pth in ("/opt/trn_rl_repo",):
    if _pth not in sys.path:
        sys.path.insert(0, _pth)
import math
import contextlib
import numpy as np
import ml_dtypes
import concourse.bass as bass
import concourse.mybir as mybir
from concourse.bass_utils import run_bass_kernel_spmd

F32 = mybir.dt.float32
BF16 = mybir.dt.bfloat16
AF = mybir.ActivationFunctionType
ALU = mybir.AluOpType
BF16NP = ml_dtypes.bfloat16

D = 1024
T = 2048
B = 16
NCORES = 8
ROWS_PER_CORE = B // NCORES
NDT = D // 128

ENGINES = ("sync", "tensor", "vector", "scalar", "gpsimd")


class Ev:
    __slots__ = ("stream", "target")

    def __init__(self, stream, target):
        self.stream, self.target = stream, target


class Prog:

    def __init__(self, serial=False):
        self.nc = bass.Bass("TRN2", target_bir_lowering=False)
        self.es = contextlib.ExitStack()
        self.ops = {e: [] for e in ENGINES}
        self.counts = {}
        self._sems = {}
        self.waited = {e: {} for e in ENGINES}
        self.serial = serial
        self._last_ev = None
        self.tensors = {}

    def sbuf(self, name, shape, dtype):
        es = getattr(self, "_scope", None) or self.es
        if getattr(self, "_scope", None) is not None:
            name = f"{name}_s{self._scope_id}"
        t = es.enter_context(self.nc.sbuf_tensor(name, list(shape), dtype))
        self.tensors[name] = t
        return t

    def psum(self, name, shape, dtype=F32):
        es = getattr(self, "_scope", None) or self.es
        if getattr(self, "_scope", None) is not None:
            name = f"{name}_s{self._scope_id}"
        t = es.enter_context(self.nc.psum_tensor(name, list(shape), dtype))
        self.tensors[name] = t
        return t

    def push_scope(self):
        assert getattr(self, "_scope", None) is None
        self._scope = contextlib.ExitStack()
        self._scope_id = getattr(self, "_scope_id", 0) + 1

    def pop_scope(self):
        self._scope.close()
        self._scope = None

    def dram_in(self, name, shape, dtype=F32):
        return self.nc.dram_tensor(name, list(shape), dtype, kind="ExternalInput").ap()

    def dram_out(self, name, shape, dtype=F32):
        return self.nc.dram_tensor(name, list(shape), dtype, kind="ExternalOutput").ap()

    SEM_CAP = 24000

    def _stream(self, name):
        if name not in self._sems:
            self._sems[name] = self.es.enter_context(self.nc.semaphore(name))
            self.counts[name] = 0
        return self._sems[name]

    def _gen_stream(self, base, n):
        gens = getattr(self, "_gens", None)
        if gens is None:
            gens = self._gens = {}
        g = gens.get(base, 0)
        name = f"{base}@{g}" if g else base
        self._stream(name)
        if self.counts[name] + n > self.SEM_CAP:
            g += 1
            gens[base] = g
            name = f"{base}@{g}"
            self._stream(name)
        return name

    def emit(self, engine, fn, waits=(), sig=False, dma=False,
             force_waits=(), nochain=False):
        assert not getattr(self, "_finalizing", False), \
            "emit() during finalize (op closure called p.emit/constcol?)"
        wl = []
        waits = [w for w in waits if w is not None]
        bar = getattr(self, "_bar", None)
        if bar:
            waits = list(bar) + waits
        if self.serial and self._last_ev is not None and not nochain:
            waits = waits + [self._last_ev]
        for ev in force_waits:
            if ev is None:
                continue
            wl.append((ev.stream, ev.target))
            self.waited[engine][ev.stream] = max(
                ev.target, self.waited[engine].get(ev.stream, -1))
        for ev in waits:
            if ev.stream.split("@")[0] == "s_" + engine:
                continue
            prev = self.waited[engine].get(ev.stream, -1)
            if ev.target <= prev:
                continue
            self.waited[engine][ev.stream] = ev.target
            wl.append((ev.stream, ev.target))
        ev_out = None
        if sig or self.serial:
            n = 16 if dma else 1
            stream = self._gen_stream(("d_" if dma else "s_") + engine, n)
            self.counts[stream] += n
            ev_out = Ev(stream, self.counts[stream])
            self.ops[engine].append((wl, fn, (stream, n)))
            if not nochain:
                self._last_ev = ev_out
        else:
            self.ops[engine].append((wl, fn, None))
        return ev_out

    def _ensure_consts(self):
        if not hasattr(self, "_cvals"):
            self._cvals = {}
            self._ctensor = self.es.enter_context(
                self.nc.sbuf_tensor("c_consts", [128, 16], F32))
            self.const_evs = []

    def constcol(self, val):
        self._ensure_consts()
        if val not in self._cvals:
            idx = len(self._cvals)
            assert idx < 16
            self._cvals[val] = idx
            ap = self._ctensor[:, idx:idx + 1]
            self.const_evs.append(self.emit(
                "vector", lambda v, ap=ap, val=val: v.memset(ap, float(val)),
                sig=True))
        idx = self._cvals[val]
        return self._ctensor[:, idx:idx + 1]

    def debug_dump(self, name, ap, waits=()):
        shp = list(ap.shape)
        out = self.dram_out("dbg_" + name, shp, ap.dtype)
        return self.dma("scalar", out, ap, waits=waits)

    def dma(self, engine, out, in_, waits=(), nochain=False, **kw):
        return self.emit(engine, lambda e: e.dma_start(out, in_, **kw),
                         waits=waits, sig=True, dma=True, nochain=nochain)

    def fence(self, engine, ev):
        self._ensure_consts()
        cc = self._ctensor
        if engine == "scalar":
            return self.emit("scalar", lambda s: s.copy(
                cc[0:1, 15:16], cc[0:1, 14:15]), waits=[ev], sig=True)
        return self.emit(engine, lambda v: v.memset(cc[0:1, 15:16], 0.0),
                         waits=[ev], sig=True)

    def barrier(self):
        evs = [Ev(s, c) for s, c in self.counts.items() if c > 0]
        self._bar = evs
        self._last_ev = None if not self.serial else self._last_ev

    @contextlib.contextmanager
    def relaxed(self, tag=""):
        import os
        if tag and tag in os.environ.get("NORELAX", "l0f,l0a"):
            yield
            return
        self.barrier()
        old = self.serial
        self.serial = False
        try:
            yield
        finally:
            self.serial = old
            self.barrier()

    def finalize(self, final_waits=()):
        self._finalizing = True
        for ev in final_waits:
            if ev is not None:
                self._stream(ev.stream)
                self.ops["sync"].append(([(ev.stream, ev.target)], None, None))
        block = self.es.enter_context(self.nc.Block())
        for ename in ENGINES:
            ops = self.ops[ename]
            if not ops:
                continue

            def body(eng, ops=ops):
                for wl, fn, inc in ops:
                    for (stream, target) in wl:
                        eng.wait_ge(self._sems[stream], target)
                    if fn is None:
                        continue
                    inst = fn(eng)
                    if inc is not None:
                        stream, n = inc
                        inst.then_inc(self._sems[stream], n)
            getattr(block, ename)(body)
        return self.nc

    def close(self):
        self.es.close()



def host_tables():
    return {
        "c_ident": np.eye(128, dtype=BF16NP),
        "c_ones_col": np.ones((128, 1), dtype=BF16NP),
        "c_ones_row": np.ones((65, 128), dtype=BF16NP),
        "c_onesf": np.ones((1, 128), dtype=np.float32),
    }


def _prep_row_major_w(w, m):
    return np.ascontiguousarray(
        w.reshape(NDT, 128, m).transpose(1, 0, 2).reshape(128, NDT * m))


def _prep_dcol(v):
    return np.ascontiguousarray(v.reshape(NDT, 128).T)


def prep_l0attn(inp):
    return {
        "a_cw": np.ascontiguousarray(
            inp["L0_attn_conv_w"].reshape(3, NDT, 128).transpose(2, 1, 0)
            .reshape(128, NDT * 3)),
        "a_wbc": _prep_row_major_w(inp["L0_attn_W_bc"], 32),
        "a_bbc": np.ascontiguousarray(inp["L0_attn_b_bc"].reshape(32, 1)),
        "a_wdt": _prep_row_major_w(inp["L0_attn_W_dt"], 64),
        "a_bdt": np.ascontiguousarray(inp["L0_attn_b_dt"].reshape(64, 1)),
        "a_wdtp": np.ascontiguousarray(inp["L0_attn_W_dtp"]),
        "a_bdtp": _prep_dcol(inp["L0_attn_b_dtp"]),
        "a_alog": _prep_row_major_w(inp["L0_attn_A_log"], 16),
        "a_dcoef": _prep_dcol(inp["L0_attn_Dcoeff"]),
    }


L0ATTN_WEIGHT_DECLS = [
    ("a_cw", (128, NDT * 3), F32),
    ("a_wbc", (128, NDT * 32), BF16),
    ("a_bbc", (32, 1), F32),
    ("a_wdt", (128, NDT * 64), BF16),
    ("a_bdt", (64, 1), F32),
    ("a_wdtp", (64, 1024), BF16),
    ("a_bdtp", (128, NDT), F32),
    ("a_alog", (128, 128), F32),
    ("a_dcoef", (128, NDT), F32),
]


def declare_weights(p, decls):
    sb = {}
    evs = []
    for name, shape, dt in decls:
        ap = p.dram_in(name, list(shape), F32)
        t = p.sbuf(name + "_sb", list(shape), dt)
        if dt == F32:
            evs.append(p.dma("sync", t[:], ap))
        else:
            evs.append(p.dma("gpsimd", t[:], ap))
        sb[name] = t
    return sb, evs


def declare_tables(p, tabs):
    aps = {}
    evs = []
    if not hasattr(p, "_tab_drams"):
        p._tab_drams = {}
    for k, v in tabs.items():
        dt = BF16 if v.dtype == BF16NP else F32
        if k not in p._tab_drams:
            p._tab_drams[k] = p.dram_in(k, list(v.shape), dt)
        ap = p._tab_drams[k]
        t = p.sbuf(k + "_sb", list(v.shape), dt)
        evs.append(p.dma("sync", t[:], ap))
        aps[k] = t
    return aps, evs



def ln_stats_chunk(p, get_xtile, sqb, xcb, ps1, ps2, rsum, c0, ones_col, waits,
                   is_bf16=False, rsT=T):
    c1 = c0 + 512
    ev = None
    for j in range(NDT):
        xt = get_xtile(j)
        sq = sqb[j % 2]
        if is_bf16:
            xc = xt
            e1 = None
        else:
            xc = xcb[j % 2]
            e1 = p.emit("vector", lambda v, xt=xt, xc=xc: v.tensor_copy(xc[:], xt),
                        waits=waits, sig=True)
        e2 = p.emit("vector", lambda v, xc=xc, sq=sq: v.tensor_mul(
            sq[:], xc[:], xc[:]), waits=list(waits) + [e1], sig=True)
        p.emit("tensor", lambda t, xc=xc, j=j: t.matmul(
            ps1[0:1, 0:512], ones_col[:], xc[:, 0:512],
            start=(j == 0), stop=(j == NDT - 1), skip_group_check=True),
            waits=[e1] if e1 is not None else list(waits), sig=False)
        ev = p.emit("tensor", lambda t, sq=sq, j=j: t.matmul(
            ps2[0:1, 0:512], ones_col[:], sq[:, 0:512],
            start=(j == 0), stop=(j == NDT - 1), skip_group_check=True),
            waits=[e2], sig=(j == NDT - 1))
    e3 = p.emit("vector", lambda v: v.tensor_copy(
        rsum[0:1, c0:c1], ps1[0:1, 0:512]), waits=[ev], sig=True)
    e4 = p.emit("vector", lambda v: v.tensor_copy(
        rsum[0:1, rsT + c0:rsT + c1], ps2[0:1, 0:512]), waits=[ev], sig=True)
    return [e3, e4]


def ln_rows(p, rsum, rw, rowbf, waits, eps=1e-5, dmodel=D, c0=0, c1=T,
            rowf=None, rsT=T, rbc0=None, rbT=T):
    inv = 1.0 / dmodel
    cc_eps = p.constcol(eps)[0:1, 0:1]
    cc_mh = p.constcol(-0.5)[0:1, 0:1]
    evs = []
    if rbc0 is None:
        rbc0 = c0
    for si, s in enumerate(range(c0, c1, 512)):
        sb_ = rbc0 + si * 512
        t0 = rw[0:1, 0:512]
        t1 = rw[0:1, 512:1024]
        t2 = rw[0:1, 1024:1536]
        e1 = p.emit("vector", lambda v, s=s, t0=t0: v.tensor_scalar_mul(
            t0, rsum[0:1, s:s + 512], inv), waits=waits, sig=True)
        e2 = p.emit("vector", lambda v, t0=t0, t1=t1: v.tensor_mul(
            t1, t0, t0), waits=[e1], sig=True)
        e3 = p.emit("vector", lambda v, s=s, t1=t1, t2=t2: v.scalar_tensor_tensor(
            t2, rsum[0:1, rsT + s:rsT + s + 512], inv, t1, ALU.mult,
            ALU.subtract), waits=waits + [e2], sig=True)
        e4 = p.emit("scalar", lambda s_, t1=t1, t2=t2: s_.activation(
            t1, t2, AF.Ln, bias=cc_eps), waits=[e3], sig=True)
        e5 = p.emit("scalar", lambda s_, t1=t1, t2=t2: s_.activation(
            t2, t1, AF.Exp, scale=cc_mh), waits=[e4], sig=True)
        e6 = p.emit("vector", lambda v, t0=t0, t1=t1, t2=t2: v.scalar_tensor_tensor(
            t1, t0, -1.0, t2, ALU.mult, ALU.mult), waits=[e5], sig=True)
        e7 = p.emit("vector", lambda v, sb_=sb_, t2=t2: v.tensor_copy(
            rowbf[0:1, sb_:sb_ + 512], t2), waits=[e5], sig=True)
        e8 = p.emit("vector", lambda v, sb_=sb_, t1=t1: v.tensor_copy(
            rowbf[0:1, rbT + sb_:rbT + sb_ + 512], t1), waits=[e6], sig=True)
        evs += [e7, e8]
        if rowf is not None:
            evs.append(p.emit("vector", lambda v, sb_=sb_, t2=t2: v.tensor_copy(
                rowf[0:1, sb_:sb_ + 512], t2), waits=[e5], sig=True))
            evs.append(p.emit("vector", lambda v, sb_=sb_, t1=t1: v.tensor_copy(
                rowf[0:1, T + sb_:T + sb_ + 512], t1), waits=[e6], sig=True))
    return evs


def bcast_to_psum(p, ps_dst, row_bf, ones_row, waits):
    W_ = row_bf.shape[-1]
    ev = None
    bp = row_bf.base_partition()
    ones = ones_row[bp:bp + 1, :]
    for s0 in range(0, W_, 512):
        s1 = min(s0 + 512, W_)
        ev = p.emit("tensor", lambda t, s0=s0, s1=s1: t.matmul(
            ps_dst[:, s0:s1], ones[:], row_bf[:, s0:s1],
            start=True, stop=True), waits=waits, sig=(s1 == W_))
    return ev


CH = 1024
NCH = T // CH
NST = 16


def emit_l0attn_row(p, bufs, tabs, W, xrow_in, xrow_out):
    xs = bufs["xs"]
    xnb = bufs["xnb"]
    ub = bufs["ub"]
    yacc = bufs["yacc"]
    sqb, xcb = bufs["sqb"], bufs["xcb"]
    t1m = bufs["t1m"]
    dBb = bufs["dBb"]
    btl = bufs["btl"]
    dtb = bufs["dtb"]
    xdtb = bufs["xdtb"]
    bc_sb = bufs["bc_sb"]
    zb = bufs["zb"]
    brow = bufs["brow"]
    carry = bufs["carry"]
    rsum, rw, rowbf = bufs["rsum"], bufs["rw"], bufs["rowbf"]
    ps_a, ps_b = bufs["ps_a"], bufs["ps_b"]
    ps_c, ps_d = bufs["ps_c"], bufs["ps_d"]
    ones_col = tabs["c_ones_col"]
    ones_row = tabs["c_ones_row"]

    def xs_t(j, s0=0, s1=CH):
        return xs[:, j * CH + s0:j * CH + s1]

    def xnb_t(j, c0=0, c1=T):
        return xnb[:, j * T + c0:j * T + c1]

    for ci in range(NCH):
        h0, h1 = ci * CH, (ci + 1) * CH
        eld = p.dma("sync",
                    xs[:].rearrange("p (j t) -> p j t", j=NDT),
                    xrow_in.rearrange("(j p) t -> p j t", p=128)[:, :, h0:h1])
        sevs = []
        for s0 in range(0, CH, 512):
            sevs += ln_stats_chunk(p, lambda j, s0=s0: xs_t(j, s0, s0 + 512),
                                   sqb, xcb, ps_c, ps_d, rsum, h0 + s0,
                                   ones_col, [eld])
        revs = ln_rows(p, rsum, rw, rowbf, sevs, c0=h0, c1=h1)
        eR = bcast_to_psum(p, ps_a[:, 0:CH], rowbf[0:1, h0:h1], ones_row, revs)
        eA = bcast_to_psum(p, ps_b[:, 0:CH], rowbf[0:1, T + h0:T + h1], ones_row, revs)
        for j in range(NDT):
            tmp = t1m[j % 2]
            e1 = p.emit("vector", lambda v, j=j, tmp=tmp: v.tensor_mul(
                tmp[:], xs_t(j), ps_a[:, 0:CH]), waits=[eR, eld], sig=True)
            p.emit("vector", lambda v, j=j, h0=h0, h1=h1, tmp=tmp: v.tensor_add(
                xnb_t(j, h0, h1), tmp[:], ps_b[:, 0:CH]), waits=[e1, eA],
                sig=True)

    for ci in range(NCH):
        h0, h1 = ci * CH, (ci + 1) * CH
        for j in range(NDT):
            w0 = W["a_cw"][:, j * 3 + 0:j * 3 + 1]
            w1 = W["a_cw"][:, j * 3 + 1:j * 3 + 2]
            w2 = W["a_cw"][:, j * 3 + 2:j * 3 + 3]
            cvA, cvB = t1m[0], t1m[1]
            ubj = ub[:, j * CH:(j + 1) * CH]
            e1 = p.emit("vector", lambda v, j=j, cvA=cvA, w1=w1, h0=h0, h1=h1:
                        v.tensor_scalar_mul(cvA[:], xnb_t(j, h0, h1), w1),
                        sig=True)
            if h0 == 0:
                e2 = p.emit("vector", lambda v, j=j, cvB=cvB, cvA=cvA, w0=w0:
                            v.scalar_tensor_tensor(
                                cvB[:, 1:CH], xnb_t(j, 0, CH - 1), w0,
                                cvA[:, 1:CH], ALU.mult, ALU.add),
                            waits=[e1], sig=True)
                e2b = p.emit("vector", lambda v, cvB=cvB, cvA=cvA: v.tensor_copy(
                    cvB[:, 0:1], cvA[:, 0:1]), waits=[e1], sig=True)
            else:
                e2 = p.emit("vector", lambda v, j=j, cvB=cvB, cvA=cvA, w0=w0,
                            h0=h0, h1=h1:
                            v.scalar_tensor_tensor(
                                cvB[:], xnb_t(j, h0 - 1, h1 - 1), w0,
                                cvA[:], ALU.mult, ALU.add),
                            waits=[e1], sig=True)
                e2b = None
            if h1 == T:
                e3 = p.emit("vector", lambda v, j=j, cvB=cvB, cvA=cvA, w2=w2,
                            h0=h0:
                            v.scalar_tensor_tensor(
                                cvA[:, 0:CH - 1], xnb_t(j, h0 + 1, T), w2,
                                cvB[:, 0:CH - 1], ALU.mult, ALU.add),
                            waits=[e2, e2b], sig=True)
                e3b = p.emit("vector", lambda v, cvA=cvA, cvB=cvB: v.tensor_copy(
                    cvA[:, CH - 1:CH], cvB[:, CH - 1:CH]), waits=[e2], sig=True)
            else:
                e3 = p.emit("vector", lambda v, j=j, cvB=cvB, cvA=cvA, w2=w2,
                            h0=h0, h1=h1:
                            v.scalar_tensor_tensor(
                                cvA[:], xnb_t(j, h0 + 1, h1 + 1), w2,
                                cvB[:], ALU.mult, ALU.add),
                            waits=[e2, e2b], sig=True)
                e3b = None
            p.emit("scalar", lambda s, ubj=ubj, cvA=cvA: s.activation(
                ubj[:], cvA[:], AF.Silu), waits=[e3, e3b], sig=True)
        for s0 in range(0, CH, 512):
            mm_bc = None
            mm_z = None
            for j in range(NDT):
                mm_bc = p.emit("tensor", lambda t, j=j, s0=s0: t.matmul(
                    ps_c[0:32, 0:512], W["a_wbc"][:, j * 32:(j + 1) * 32],
                    ub[:, j * CH + s0:j * CH + s0 + 512],
                    start=(j == 0), stop=(j == NDT - 1), skip_group_check=True),
                    sig=(j == NDT - 1))
                mm_z = p.emit("tensor", lambda t, j=j, s0=s0: t.matmul(
                    ps_d[0:64, 0:512], W["a_wdt"][:, j * 64:(j + 1) * 64],
                    ub[:, j * CH + s0:j * CH + s0 + 512],
                    start=(j == 0), stop=(j == NDT - 1), skip_group_check=True),
                    sig=(j == NDT - 1))
            p.emit("vector", lambda v, c0=h0 + s0: v.tensor_scalar_add(
                bc_sb[0:32, c0:c0 + 512], ps_c[0:32, 0:512],
                W["a_bbc"][0:32, 0:1]), waits=[mm_bc], sig=True)
            p.emit("vector", lambda v, c0=h0 + s0: v.tensor_scalar_add(
                zb[0:64, c0:c0 + 512], ps_d[0:64, 0:512],
                W["a_bdt"][0:64, 0:1]), waits=[mm_z], sig=True)

    ecz = p.emit("vector", lambda v: v.memset(carry[:], 0.0), sig=True)

    for ci in range(NCH):
        h0, h1 = ci * CH, (ci + 1) * CH
        for j in range(NDT):
            mm = None
            for s0 in range(0, CH, 512):
                mm = p.emit("tensor", lambda t, j=j, s0=s0, h0=h0: t.matmul(
                    ps_b[:, s0:s0 + 512], W["a_wdtp"][0:64, j * 128:(j + 1) * 128],
                    zb[0:64, h0 + s0:h0 + s0 + 512],
                    start=True, stop=True), sig=(s0 + 512 == CH))
            spA, spB, spC, spD = t1m[0], t1m[1], t1m[2], t1m[3]
            bcol = W["a_bdtp"][:, j:j + 1]
            e1 = p.emit("vector", lambda v, spA=spA, bcol=bcol: v.tensor_scalar_add(
                spA[:], ps_b[:, 0:CH], bcol), waits=[mm], sig=True)
            e2 = p.emit("scalar", lambda s, spA=spA, spB=spB: s.activation(
                spB[:], spA[:], AF.Abs), waits=[e1], sig=True)
            cc_m1 = p.constcol(-1.0)
            cc_p1 = p.constcol(1.0)
            e3 = p.emit("scalar", lambda s, spB=spB, spC=spC, cc_m1=cc_m1:
                        s.activation(spC[:], spB[:], AF.Exp, scale=cc_m1),
                        waits=[e2], sig=True)
            e4 = p.emit("scalar", lambda s, spC=spC, spB=spB, cc_p1=cc_p1:
                        s.activation(spB[:], spC[:], AF.Ln, bias=cc_p1),
                        waits=[e3], sig=True)
            e5 = p.emit("vector", lambda v, spA=spA, spD=spD: v.tensor_scalar_max(
                spD[:], spA[:], 0.0), waits=[e1], sig=True)
            e6 = p.emit("vector", lambda v, j=j, spD=spD, spB=spB: v.tensor_add(
                dtb[:, j * CH:(j + 1) * CH], spD[:], spB[:]), waits=[e4, e5],
                sig=True)
            p.emit("vector", lambda v, j=j, h0=h0, h1=h1: v.tensor_mul(
                xdtb[:, j * CH:(j + 1) * CH], xnb_t(j, h0, h1),
                dtb[:, j * CH:(j + 1) * CH]), waits=[e6], sig=True)
        dB_evs = [None] * NST
        ym_evs = [None] * NST
        slot_rd = {}
        with p.relaxed('l0a'):
          for n in range(NST):
            ebr0 = p.dma("gpsimd", brow[0:1, 0:CH], bc_sb[n:n + 1, h0:h1],
                         waits=[dB_evs[n - 1] if n else None])
            ebr1 = p.dma("gpsimd", brow[0:1, CH:2 * CH],
                         bc_sb[16 + n:17 + n, h0:h1],
                         waits=[ym_evs[n - 1] if n else None])
            eB = bcast_to_psum(p, ps_a[:, 0:CH], brow[0:1, 0:CH], ones_row,
                               [ebr0, slot_rd.get("psB")])
            eC = bcast_to_psum(p, ps_a[:, CH:2 * CH], brow[0:1, CH:2 * CH],
                               ones_row, [ebr1, slot_rd.get("psC")])
            eBs = p.emit("scalar", lambda s: s.copy(btl[0][:], ps_a[:, 0:CH]),
                         waits=[eB, dB_evs[n - 1] if n else None], sig=True)
            eCs = p.emit("scalar", lambda s: s.copy(btl[1][:], ps_a[:, CH:2 * CH]),
                         waits=[eC, ym_evs[n - 1] if n else None], sig=True)
            slot_rd["psB"] = eBs
            slot_rd["psC"] = eCs
            last_dB = None
            last_ym = None
            for j in range(NDT):
                acol = W["a_alog"][:, j * 16 + n:j * 16 + n + 1]
                hbuf = t1m[j % 2]
                dAyt = t1m[2 + (j % 2)]
                dB = dBb[j % 2]
                eA_ = p.emit("scalar", lambda s, j=j, dAyt=dAyt, acol=acol:
                             s.activation(dAyt[:], dtb[:, j * CH:(j + 1) * CH],
                                          AF.Exp, scale=acol),
                             waits=[slot_rd.get(("ya", j % 2))], sig=True)
                eB_ = p.emit("vector", lambda v, j=j, dB=dB: v.tensor_mul(
                    dB[:], xdtb[:, j * CH:(j + 1) * CH], btl[0][:]),
                    waits=[eBs, slot_rd.get(("sc", j % 2))], sig=True)
                last_dB = eB_
                col = n * 8 + j
                eS = p.emit("vector", lambda v, dAyt=dAyt, dB=dB, hbuf=hbuf,
                            col=col: v.tensor_tensor_scan(
                                hbuf[:], dAyt[:], dB[:], carry[:, col:col + 1],
                                ALU.mult, ALU.add),
                            waits=[eA_, eB_, ecz, slot_rd.get(("ym", j % 2)),
                                   slot_rd.get(("cp", j % 2))], sig=True)
                slot_rd[("sc", j % 2)] = eS
                if ci == 0 and NCH > 1:
                    ecp = p.emit("scalar", lambda s, hbuf=hbuf, col=col: s.copy(
                        carry[:, col:col + 1], hbuf[:, CH - 1:CH]),
                        waits=[eS], sig=True)
                else:
                    ecp = None
                eM = p.emit("vector", lambda v, hbuf=hbuf, dAyt=dAyt:
                            v.tensor_mul(dAyt[:], hbuf[:], btl[1][:]),
                            waits=[eS, eCs], sig=True)
                slot_rd[("ym", j % 2)] = eM
                slot_rd[("cp", j % 2)] = ecp
                last_ym = eM
                if n == 0:
                    ey = p.emit("gpsimd", lambda g, j=j, dAyt=dAyt:
                                g.tensor_copy(yacc[:, j * CH:(j + 1) * CH],
                                              dAyt[:]), waits=[eM], sig=True)
                else:
                    ey = p.emit("gpsimd", lambda g, j=j, dAyt=dAyt:
                                g.tensor_add(yacc[:, j * CH:(j + 1) * CH],
                                             yacc[:, j * CH:(j + 1) * CH],
                                             dAyt[:]), waits=[eM], sig=True)
                slot_rd[("ya", j % 2)] = ey
            dB_evs[n] = last_dB
            ym_evs[n] = last_ym
        for j in range(NDT):
            p.emit("vector", lambda v, j=j, h0=h0, h1=h1: v.scalar_tensor_tensor(
                yacc[:, j * CH:(j + 1) * CH], xnb_t(j, h0, h1),
                W["a_dcoef"][:, j:j + 1], yacc[:, j * CH:(j + 1) * CH],
                ALU.mult, ALU.add), sig=True)
        sevs = []
        for s0 in range(0, CH, 512):
            sevs += ln_stats_chunk(
                p, lambda j, s0=s0: yacc[:, j * CH + s0:j * CH + s0 + 512],
                sqb, xcb, ps_c, ps_d, rsum, h0 + s0, ones_col, [])
        revs = ln_rows(p, rsum, rw, rowbf, sevs, c0=h0, c1=h1)
        eld = p.dma("sync",
                    xs[:].rearrange("p (j t) -> p j t", j=NDT),
                    xrow_in.rearrange("(j p) t -> p j t", p=128)[:, :, h0:h1])
        eR = bcast_to_psum(p, ps_b[:, 0:CH], rowbf[0:1, h0:h1], ones_row, revs)
        eA = bcast_to_psum(p, ps_a[:, 0:CH], rowbf[0:1, T + h0:T + h1],
                           ones_row, revs)
        st_evs = []
        for j in range(NDT):
            tmp = t1m[j % 2]
            e1 = p.emit("vector", lambda v, j=j, tmp=tmp: v.tensor_mul(
                tmp[:], yacc[:, j * CH:(j + 1) * CH], ps_b[:, 0:CH]),
                waits=[eR], sig=True)
            e2 = p.emit("vector", lambda v, tmp=tmp: v.tensor_add(
                tmp[:], tmp[:], ps_a[:, 0:CH]), waits=[e1, eA], sig=True)
            e3 = p.emit("vector", lambda v, j=j, tmp=tmp: v.tensor_add(
                xs_t(j), xs_t(j), tmp[:]), waits=[e2, eld], sig=True)
            st_evs.append(p.dma(
                "scalar",
                xrow_out.rearrange("(j p) t -> p j t", p=128)[:, j, h0:h1],
                xs_t(j), waits=[e3]))
    return st_evs


def alloc_l0attn_bufs(p):
    bufs = {}
    bufs["xs"] = p.sbuf("xs", [128, NDT * CH], F32)
    bufs["xnb"] = p.sbuf("xnb", [128, NDT * T], BF16)
    bufs["yacc"] = p.sbuf("yacc", [128, NDT * CH], F32)
    bufs["ub"] = bufs["yacc"][:, 0:NDT * CH // 2].bitcast(BF16)
    bufs["dBb"] = [p.sbuf(f"dBb{i}", [128, CH], BF16) for i in range(2)]
    bufs["sqb"] = [bufs["dBb"][0][:, 0:512], bufs["dBb"][0][:, 512:1024]]
    bufs["xcb"] = [bufs["dBb"][1][:, 0:512], bufs["dBb"][1][:, 512:1024]]
    bufs["t1m"] = [p.sbuf(f"t1m{i}", [128, CH], F32) for i in range(4)]
    bufs["btl"] = [p.sbuf(f"btl{i}", [128, CH], BF16) for i in range(2)]
    bufs["dtb"] = p.sbuf("dtb", [128, NDT * CH], BF16)
    bufs["xdtb"] = p.sbuf("xdtb", [128, NDT * CH], BF16)
    bufs["bc_sb"] = p.sbuf("bc_sb", [32, T], BF16)
    bufs["zb"] = p.sbuf("zb", [64, T], BF16)
    bufs["brow"] = p.sbuf("brow", [1, 2 * CH], BF16)
    bufs["carry"] = p.sbuf("carry", [128, 128], F32)
    bufs["rsum"] = p.sbuf("rsum", [1, 2 * T], F32)
    bufs["rw"] = p.sbuf("rw", [1, 2048], F32)
    bufs["rowbf"] = p.sbuf("rowbf", [1, 2 * T], BF16)
    bufs["ps_a"] = p.psum("ps_a", [128, 2048])
    bufs["ps_b"] = p.psum("ps_b", [128, 1024])
    bufs["ps_c"] = p.psum("ps_c", [128, 512])
    bufs["ps_d"] = p.psum("ps_d", [128, 512])
    return bufs


def emit_block_l0attn(p, xin, xout):
    tabs, tev = declare_tables(p, host_tables())
    Wsb, wev = declare_weights(p, L0ATTN_WEIGHT_DECLS)
    bufs = alloc_l0attn_bufs(p)
    aneg = p.sbuf("aneg", [128, 128], F32)
    e1 = p.emit("scalar", lambda s: s.activation(aneg[:], Wsb["a_alog"][:], AF.Exp),
                waits=wev + tev, sig=True)
    p.emit("vector", lambda v: v.tensor_scalar_mul(aneg[:], aneg[:], -1.0),
           waits=[e1], sig=True)
    Wd = dict(Wsb)
    Wd["a_alog"] = aneg
    store_evs = []
    for r in range(ROWS_PER_CORE):
        store_evs += emit_l0attn_row(p, bufs, tabs, Wd, xin[r], xout[r])
    return store_evs


def build_l0attn_program(serial=True):
    p = Prog(serial=serial)
    xin = p.dram_in("xT", [ROWS_PER_CORE, D, T], F32)
    xout = p.dram_out("yT", [ROWS_PER_CORE, D, T], F32)
    store_evs = emit_block_l0attn(p, xin, xout)
    nc = p.finalize(final_waits=store_evs)
    return p, nc


NEXP = 8
TCW = 512
NTC = T // TCW
NFT = 4096 // 128


def prep_l0ffn(inp):
    return {
        "f_gw": _prep_row_major_w(
            np.ascontiguousarray(inp["L0_ffn_gate_w"].T), NEXP),
    }


L0FFN_WEIGHT_DECLS = [
    ("f_gw", (128, NDT * NEXP), F32),
]


def emit_l0ffn_gating_row(p, bufs, tabs, W, xrow_in, r):
    xs = bufs["xs"]
    xnb = bufs["xnb"]
    sqb, xcb = bufs["sqb"], bufs["xcb"]
    tmf = bufs["tmf"]
    rsum, rw, rowf, rowbf = bufs["rsum"], bufs["rw"], bufs["rowf"], bufs["rowbf"]
    rawT = bufs["rawT"]
    bis = bufs["bis"]
    crow = bufs["crow"]
    wtb = bufs["wtb"]
    wtok = bufs["wtok"][r]
    ps_a, ps_b, ps_c, ps_t = bufs["ps_a"], bufs["ps_b"], bufs["ps_c"], bufs["ps_t"]
    ones_col = tabs["c_ones_col"]
    ones_row = tabs["c_ones_row"]
    onesf = tabs["c_onesf"]
    gw = W["f_gw"]
    c1f = p.constcol(1.0)

    def xs_t(j, s0=0, s1=CH):
        return xs[:, j * CH + s0:j * CH + s1]

    for ci in range(NCH):
        h0, h1 = ci * CH, (ci + 1) * CH
        eld = p.dma("sync",
                    xs[:].rearrange("p (j t) -> p j t", j=NDT),
                    xrow_in.rearrange("(j p) t -> p j t", p=128)[:, :, h0:h1])
        sevs = []
        for s0 in range(0, CH, 512):
            sevs += ln_stats_chunk(p, lambda j, s0=s0: xs_t(j, s0, s0 + 512),
                                   sqb, xcb, ps_b[0:1, 0:512],
                                   ps_b[0:1, 512:1024], rsum, h0 + s0,
                                   ones_col, [eld])
            mm = None
            for j in range(NDT):
                mm = p.emit("tensor", lambda t, j=j, s0=s0: t.matmul(
                    ps_a[0:8, 0:512], gw[:, j * NEXP:(j + 1) * NEXP],
                    xs_t(j, s0, s0 + 512), start=(j == 0), stop=(j == NDT - 1),
                    skip_group_check=True), waits=[eld], sig=(j == NDT - 1))
            p.emit("vector", lambda v, c0=h0 + s0: v.tensor_copy(
                rawT[0:8, c0:c0 + 512], ps_a[0:8, 0:512]), waits=[mm], sig=True)
        revs = ln_rows(p, rsum, rw, rowbf, sevs, c0=h0, c1=h1, rowf=rowf)
        eR = bcast_to_psum(p, ps_a[:, 0:CH], rowbf[0:1, h0:h1], ones_row, revs)
        eA = bcast_to_psum(p, ps_b[:, 0:CH], rowbf[0:1, T + h0:T + h1],
                           ones_row, revs)
        for j in range(NDT):
            tmp = tmf[j % 2]
            e1 = p.emit("vector", lambda v, j=j, tmp=tmp: v.tensor_mul(
                tmp[:], xs_t(j), ps_a[:, 0:CH]), waits=[eR], sig=True)
            p.emit("vector", lambda v, j=j, h0=h0, h1=h1, tmp=tmp: v.tensor_add(
                xnb[:, j * T + h0:j * T + h1], tmp[:], ps_b[:, 0:CH]),
                waits=[e1, eA], sig=True)

    tmp8 = xs[0:8, 0:T]
    expT = xs[0:8, T:2 * T]
    probsT = xs[0:8, 2 * T:3 * T]
    cmpT = xs[0:8, 3 * T:4 * T]
    mm = None
    for j in range(NDT):
        mm = p.emit("tensor", lambda t, j=j: t.matmul(
            ps_c[0:8, 0:1], gw[:, j * NEXP:(j + 1) * NEXP], c1f,
            start=(j == 0), stop=(j == NDT - 1), skip_group_check=True),
            sig=(j == NDT - 1))
    e_g8 = p.emit("vector", lambda v: v.tensor_copy(
        bis[0:8, 7:8], ps_c[0:8, 0:1]), waits=[mm], sig=True)
    evR = None
    for s0 in range(0, T, 512):
        evR = p.emit("tensor", lambda t, s0=s0: t.matmul(
            ps_a[0:8, s0:s0 + 512], onesf[0:1, 0:8], rowf[0:1, s0:s0 + 512],
            start=True, stop=True), sig=(s0 + 512 == T))
    e1 = p.emit("vector", lambda v: v.tensor_mul(
        tmp8, rawT[0:8, :], ps_a[0:8, 0:T]), waits=[evR], sig=True)
    evA = None
    for s0 in range(0, T, 512):
        evA = p.emit("tensor", lambda t, s0=s0: t.matmul(
            ps_a[0:8, s0:s0 + 512], onesf[0:1, 0:8],
            rowf[0:1, T + s0:T + s0 + 512],
            start=True, stop=True), waits=[e1], sig=(s0 + 512 == T))
    e2 = p.emit("vector", lambda v: v.scalar_tensor_tensor(
        rawT[0:8, :], ps_a[0:8, 0:T], bis[0:8, 7:8], tmp8,
        ALU.mult, ALU.add), waits=[evA, e_g8], sig=True)
    e3 = p.emit("vector", lambda v: v.reduce_max(
        bis[0:8, 5:6], rawT[0:8, :], axis=mybir.AxisListType.X),
        waits=[e2], sig=True)
    e4 = p.dma("gpsimd", crow[0:1, 0:8], bis[0:8, 5:6], waits=[e3])
    e5 = p.emit("vector", lambda v: v.reduce_max(
        crow[0:1, 16:17], crow[0:1, 0:8], axis=mybir.AxisListType.X),
        waits=[e4], sig=True)
    e6 = p.emit("tensor", lambda t: t.matmul(
        ps_c[0:8, 0:1], onesf[0:1, 0:8], crow[0:1, 16:17],
        start=True, stop=True), waits=[e5], sig=True)
    e7 = p.emit("vector", lambda v: v.tensor_scalar_mul(
        bis[0:8, 6:7], ps_c[0:8, 0:1], -1.0), waits=[e6], sig=True)
    e8 = p.emit("scalar", lambda s: s.activation(
        expT, rawT[0:8, :], AF.Exp, bias=bis[0:8, 6:7]), waits=[e7], sig=True)
    evS = None
    for s0 in range(0, T, 512):
        evS = p.emit("tensor", lambda t, s0=s0: t.matmul(
            ps_b[0:1, 0:512] if s0 % 1024 == 0 else ps_b[0:1, 512:1024],
            c1f[0:8, 0:1], expT[:, s0:s0 + 512],
            start=True, stop=True), waits=[e8], sig=True)
        p.emit("vector", lambda v, s0=s0, evS=evS: v.tensor_copy(
            rw[0:1, s0:s0 + 512],
            ps_b[0:1, 0:512] if s0 % 1024 == 0 else ps_b[0:1, 512:1024]),
            waits=[evS], sig=True)
    if getattr(p, "dbg", False) and r == 0:
        p.debug_dump("rawT", rawT[0:8, :])
        p.debug_dump("expT", expT)
        p.debug_dump("sums", rw[0:1, 0:T])
    cc_m1g = p.constcol(-1.0)[0:1, 0:1]
    e9 = p.emit("scalar", lambda s: s.activation(
        rowf[0:1, 0:T], rw[0:1, 0:T], AF.Ln), sig=True)
    e10 = p.emit("scalar", lambda s: s.activation(
        rw[0:1, 0:T], rowf[0:1, 0:T], AF.Exp, scale=cc_m1g),
        waits=[e9], sig=True)
    evR2 = None
    for s0 in range(0, T, 512):
        evR2 = p.emit("tensor", lambda t, s0=s0: t.matmul(
            ps_a[0:8, s0:s0 + 512], onesf[0:1, 0:8], rw[0:1, s0:s0 + 512],
            start=True, stop=True), waits=[e10], sig=(s0 + 512 == T))
    e11 = p.emit("vector", lambda v: v.tensor_mul(
        probsT, expT, ps_a[0:8, 0:T]), waits=[evR2], sig=True)
    if getattr(p, "dbg", False) and r == 0:
        p.debug_dump("recipb", rw[0:1, 0:T])
    cc_half = p.constcol(0.5)
    cc_nh = p.constcol(-511.5)
    p.emit("gpsimd", lambda g: g.memset(bis[0:8, 0:1], 0.0), sig=True)
    ez = p.emit("gpsimd", lambda g: g.memset(bis[0:8, 1:2], 2.0), sig=True)
    for it in range(30):
        em = p.emit("gpsimd", lambda g: g.tensor_add(
            bis[0:8, 2:3], bis[0:8, 0:1], bis[0:8, 1:2]), waits=[ez], sig=True)
        em2 = p.emit("gpsimd", lambda g: g.tensor_mul(
            bis[0:8, 2:3], bis[0:8, 2:3], cc_half[0:8, 0:1]), waits=[em],
            sig=True)
        ec = p.emit("vector", lambda v: v.tensor_scalar(
            cmpT, probsT, bis[0:8, 2:3], None, ALU.is_ge),
            waits=[em2, e11], sig=True)
        er = p.emit("vector", lambda v: v.reduce_sum(
            bis[0:8, 3:4], cmpT, axis=mybir.AxisListType.X), waits=[ec], sig=True)
        eg1 = p.emit("scalar", lambda s: s.activation(
            bis[0:8, 4:5], bis[0:8, 3:4], AF.Sign, bias=cc_nh[0:8, 0:1]),
            waits=[er], sig=True)
        eg2 = p.emit("gpsimd", lambda g: g.tensor_mul(
            bis[0:8, 4:5], bis[0:8, 4:5], cc_half[0:8, 0:1]), waits=[eg1],
            sig=True)
        es_ = p.emit("gpsimd", lambda g: g.tensor_add(
            bis[0:8, 4:5], bis[0:8, 4:5], cc_half[0:8, 0:1]), waits=[eg2],
            sig=True)
        ed1 = p.emit("gpsimd", lambda g: g.tensor_sub(
            bis[0:8, 5:6], bis[0:8, 2:3], bis[0:8, 0:1]), waits=[es_], sig=True)
        ed2 = p.emit("gpsimd", lambda g: g.tensor_sub(
            bis[0:8, 6:7], bis[0:8, 1:2], bis[0:8, 2:3]), waits=[es_], sig=True)
        em3 = p.emit("gpsimd", lambda g: g.tensor_mul(
            bis[0:8, 5:6], bis[0:8, 5:6], bis[0:8, 4:5]), waits=[ed1], sig=True)
        em4 = p.emit("gpsimd", lambda g: g.tensor_mul(
            bis[0:8, 6:7], bis[0:8, 6:7], bis[0:8, 4:5]), waits=[ed2], sig=True)
        en1 = p.emit("gpsimd", lambda g: g.tensor_add(
            bis[0:8, 0:1], bis[0:8, 0:1], bis[0:8, 5:6]), waits=[em3], sig=True)
        ez = p.emit("gpsimd", lambda g: g.tensor_add(
            bis[0:8, 1:2], bis[0:8, 2:3], bis[0:8, 6:7]), waits=[em4, en1],
            sig=True)
    ew1 = p.emit("vector", lambda v: v.tensor_scalar(
        cmpT, probsT, bis[0:8, 0:1], None, ALU.is_ge), waits=[ez], sig=True)
    ew2 = p.emit("vector", lambda v: v.tensor_mul(
        tmp8, probsT, cmpT), waits=[ew1], sig=True)
    ew3 = p.emit("vector", lambda v: v.tensor_copy(wtb[0:8, :], tmp8),
                 waits=[ew2], sig=True)
    for tt in range(16):
        et = p.emit("tensor", lambda t, tt=tt: t.transpose(
            ps_t[:, 0:8], wtb[0:8, tt * 128:(tt + 1) * 128],
            tabs["c_ident"][0:8, 0:8]), waits=[ew3], sig=True)
        p.emit("vector", lambda v, tt=tt: v.tensor_copy(
            wtok[:, tt * 8:(tt + 1) * 8], ps_t[:, 0:8]), waits=[et], sig=True)
    if getattr(p, "dbg", False) and r == 0:
        p.debug_dump("probsT", probsT)
        p.debug_dump("w8", tmp8)
        p.debug_dump("thr", bis[0:8, 0:2])
        p.debug_dump("wtok", wtok[:, :])
        p.debug_dump("xnb", xnb[:, :])


def emit_l0ffn_experts_row(p, bufs, tabs, W, w1_ap, w2_ap, yscr_row, r):
    xnb = bufs["xnb"]
    hgel = bufs["hgel"]
    outacc = bufs["outacc"]
    ocb = bufs["ocb"]
    ytc = bufs["ytc"]
    w1c, w2c = bufs["w1c"], bufs["w2c"]
    wtok = bufs["wtok"][r]
    ps_a, ps_c, ps_t = bufs["ps_a"], bufs["ps_c"], bufs["ps_t"]
    ident = tabs["c_ident"]
    NTT_C = TCW // 128
    import os
    NCH_ = not os.environ.get("NODMAPRE")
    ps_h2 = [ps_c, bufs["ps_b"]]
    w1_readers = [None, None]
    w2_readers = [None, None]
    gelu_evs = [None, None]
    for tc in range(NTC):
        for e in range(NEXP):
            for dh in range(2):
                with p.relaxed('l0f'):
                    for f in range(NFT):
                        buf = f % 2
                        if dh == 0:
                            ew1 = p.dma("gpsimd",
                                        w1c[buf][:].rearrange(
                                            "p (j f) -> p j f", j=NDT),
                                        w1_ap[e].rearrange(
                                            "(j p) f -> p j f", p=128)[
                                            :, :, f * 128:(f + 1) * 128],
                                        waits=[w1_readers[buf]], nochain=NCH_)
                        else:
                            ew1 = None
                        ew2 = p.dma("gpsimd", w2c[buf][:],
                                    w2_ap[e][f * 128:(f + 1) * 128, :],
                                    waits=[w2_readers[buf]], nochain=NCH_)
                        if dh == 0:
                            psh = ps_h2[buf][0:128, 0:512]
                            mm = None
                            for j in range(NDT):
                                mm = p.emit("tensor", lambda t, j=j, f=f,
                                            tc=tc, buf=buf, psh=psh: t.matmul(
                                                psh,
                                                w1c[buf][:, j * 128:(j + 1) * 128],
                                                xnb[:, j * T + tc * TCW:
                                                    j * T + (tc + 1) * TCW],
                                                start=(j == 0),
                                                stop=(j == NDT - 1)),
                                            waits=[ew1, gelu_evs[buf]],
                                            sig=(j == NDT - 1))
                            w1_readers[buf] = mm
                            eg = p.emit("scalar", lambda s, f=f, psh=psh:
                                        s.activation(
                                            hgel[:, f * TCW:(f + 1) * TCW],
                                            psh, AF.Gelu), waits=[mm],
                                        sig=True)
                            gelu_evs[buf] = eg
                        else:
                            eg = None
                        mm2 = None
                        for tt in range(NTT_C):
                            mm2 = p.emit("tensor", lambda t, f=f, tt=tt,
                                         dh=dh, buf=buf: t.matmul(
                                             ps_a[:, tt * 512:(tt + 1) * 512],
                                             hgel[:, f * TCW + tt * 128:
                                                  f * TCW + (tt + 1) * 128],
                                             w2c[buf][:, dh * 512:
                                                      (dh + 1) * 512],
                                             start=(f == 0), stop=(f == NFT - 1),
                                             skip_group_check=True),
                                         waits=[eg, ew2],
                                         sig=(tt == NTT_C - 1))
                        w2_readers[buf] = mm2
                    for tt in range(NTT_C):
                        wcol = wtok[:, (tc * NTT_C + tt) * 8 + e:
                                    (tc * NTT_C + tt) * 8 + e + 1]
                        dst = outacc[:, tt * 1024 + dh * 512:
                                     tt * 1024 + (dh + 1) * 512]
                        if e == 0:
                            p.emit("vector", lambda v, tt=tt, dst=dst,
                                   wcol=wcol: v.tensor_scalar_mul(
                                       dst, ps_a[:, tt * 512:(tt + 1) * 512],
                                       wcol), waits=[mm2], sig=True)
                        else:
                            p.emit("vector", lambda v, tt=tt, dst=dst,
                                   wcol=wcol: v.scalar_tensor_tensor(
                                       dst, ps_a[:, tt * 512:(tt + 1) * 512],
                                       wcol, dst, ALU.mult, ALU.add),
                                   waits=[mm2], sig=True)
        for hh in range(2):
            p.emit("vector", lambda v, hh=hh: v.tensor_copy(
                ocb[:, hh * 2048:(hh + 1) * 2048],
                outacc[:, hh * 2048:(hh + 1) * 2048]), sig=True)
        for tt in range(NTT_C):
            for dj in range(NDT):
                et = p.emit("tensor", lambda t, tt=tt, dj=dj: t.transpose(
                    ps_t[:, 0:128],
                    ocb[:, tt * 1024 + dj * 128:tt * 1024 + (dj + 1) * 128],
                    ident[:]), sig=True)
                p.emit("vector", lambda v, tt=tt, dj=dj: v.tensor_copy(
                    ytc[:, dj * TCW + tt * 128:dj * TCW + (tt + 1) * 128],
                    ps_t[:, 0:128]), waits=[et], sig=True)
        p.dma("scalar",
              yscr_row.rearrange("(j p) t -> p j t", p=128)[
                  :, :, tc * TCW:(tc + 1) * TCW],
              ytc[:].rearrange("p (j t) -> p j t", j=NDT))
        if getattr(p, "dbg", False) and r == 0 and tc == 0:
            p.debug_dump("outacc", outacc[:, :])


def emit_l0ffn_lnout_row(p, bufs, tabs, xrow_in, xrow_out, yscr_row):
    xs = bufs["xs"]
    ybt = bufs["xnb"][:, 0:NDT * CH]
    sqb, xcb = bufs["sqb"], bufs["xcb"]
    tmf = bufs["tmf"]
    rsum, rw, rowbf = bufs["rsum"], bufs["rw"], bufs["rowbf"]
    ps_a, ps_b = bufs["ps_a"], bufs["ps_b"]
    ones_col = tabs["c_ones_col"]
    ones_row = tabs["c_ones_row"]
    st_evs = []
    for ci in range(NCH):
        h0, h1 = ci * CH, (ci + 1) * CH
        eldy = p.dma("sync",
                     ybt[:].rearrange("p (j t) -> p j t", j=NDT),
                     yscr_row.rearrange("(j p) t -> p j t", p=128)[:, :, h0:h1])
        eldx = p.dma("gpsimd",
                     xs[:].rearrange("p (j t) -> p j t", j=NDT),
                     xrow_in.rearrange("(j p) t -> p j t", p=128)[:, :, h0:h1])
        sevs = []
        for s0 in range(0, CH, 512):
            sevs += ln_stats_chunk(
                p, lambda j, s0=s0: ybt[:, j * CH + s0:j * CH + s0 + 512],
                sqb, xcb, ps_b[0:1, 0:512], ps_b[0:1, 512:1024], rsum,
                h0 + s0, ones_col, [eldy], is_bf16=True)
        revs = ln_rows(p, rsum, rw, rowbf, sevs, c0=h0, c1=h1)
        eR = bcast_to_psum(p, ps_b[:, 0:CH], rowbf[0:1, h0:h1], ones_row, revs)
        eA = bcast_to_psum(p, ps_a[:, 0:CH], rowbf[0:1, T + h0:T + h1],
                           ones_row, revs)
        for j in range(NDT):
            tmp = tmf[j % 2]
            e1 = p.emit("vector", lambda v, j=j, tmp=tmp: v.tensor_mul(
                tmp[:], ybt[:, j * CH:(j + 1) * CH], ps_b[:, 0:CH]),
                waits=[eR], sig=True)
            e2 = p.emit("vector", lambda v, tmp=tmp: v.tensor_add(
                tmp[:], tmp[:], ps_a[:, 0:CH]), waits=[e1, eA], sig=True)
            e3 = p.emit("vector", lambda v, j=j, tmp=tmp: v.tensor_add(
                xs[:, j * CH:(j + 1) * CH], xs[:, j * CH:(j + 1) * CH], tmp[:]),
                waits=[e2, eldx], sig=True)
            st_evs.append(p.dma(
                "scalar",
                xrow_out.rearrange("(j p) t -> p j t", p=128)[:, j, h0:h1],
                xs[:, j * CH:(j + 1) * CH], waits=[e3]))
    return st_evs


def alloc_l0ffn_bufs(p):
    bufs = {}
    bufs["xs"] = p.sbuf("xs", [128, NDT * CH], F32)
    bufs["xnb"] = p.sbuf("xnb", [128, NDT * T], BF16)
    bufs["hgel"] = p.sbuf("hgel", [128, NFT * TCW], BF16)
    bufs["ocb"] = bufs["hgel"][:, 0:4 * 1024]
    bufs["outacc"] = p.sbuf("outacc", [128, 4 * 1024], F32)
    bufs["ytc"] = p.sbuf("ytc", [128, NDT * TCW], BF16)
    bufs["w1c"] = [p.sbuf(f"w1c{i}", [128, NDT * 128], BF16) for i in range(2)]
    bufs["w2c"] = [p.sbuf(f"w2c{i}", [128, 1024], BF16) for i in range(2)]
    bufs["sqb"] = [p.sbuf(f"sqb{i}", [128, 512], BF16) for i in range(2)]
    bufs["xcb"] = [p.sbuf(f"xcb{i}", [128, 512], BF16) for i in range(2)]
    bufs["tmf"] = [p.sbuf(f"tmf{i}", [128, CH], F32) for i in range(2)]
    bufs["rawT"] = p.sbuf("rawT", [8, T], F32)
    bufs["wtb"] = p.sbuf("wtb", [8, T], BF16)
    bufs["wtok"] = [p.sbuf(f"wtok{i}", [128, 128], F32) for i in range(2)]
    bufs["bis"] = p.sbuf("bis", [8, 8], F32)
    bufs["crow"] = p.sbuf("crow", [1, 32], F32)
    bufs["rsum"] = p.sbuf("rsum", [1, 2 * T], F32)
    bufs["rowf"] = bufs["rsum"]
    bufs["rw"] = p.sbuf("rw", [1, 2048], F32)
    bufs["rowbf"] = p.sbuf("rowbf", [1, 2 * T], BF16)
    bufs["ps_a"] = p.psum("ps_a", [128, 2048])
    bufs["ps_b"] = p.psum("ps_b", [128, 1024])
    bufs["ps_c"] = p.psum("ps_c", [128, 512])
    bufs["ps_t"] = p.psum("ps_t", [128, 512], BF16)
    return bufs


def emit_block_l0ffn(p, xin, xout):
    w1 = p.dram_in("f_w1", [NEXP, D, 4096], F32)
    w2 = p.dram_in("f_w2", [NEXP, 4096, D], F32)
    yscr = p.nc.dram_tensor("yscr", [ROWS_PER_CORE, D, T], BF16,
                            kind="Internal").ap()
    tabs, tev = declare_tables(p, host_tables())
    Wsb, wev = declare_weights(p, L0FFN_WEIGHT_DECLS)
    bufs = alloc_l0ffn_bufs(p)
    st_evs = []
    for r in range(ROWS_PER_CORE):
        emit_l0ffn_gating_row(p, bufs, tabs, Wsb, xin[r], r)
        emit_l0ffn_experts_row(p, bufs, tabs, Wsb, w1, w2, yscr[r], r)
        st_evs += emit_l0ffn_lnout_row(p, bufs, tabs, xin[r], xout[r], yscr[r])
    return st_evs


def build_l0ffn_program(serial=True, dbg=False):
    p = Prog(serial=serial)
    p.dbg = dbg
    xin = p.dram_in("xT", [ROWS_PER_CORE, D, T], F32)
    xout = p.dram_out("yT", [ROWS_PER_CORE, D, T], F32)
    st_evs = emit_block_l0ffn(p, xin, xout)
    nc = p.finalize(final_waits=st_evs)
    return p, nc


def run_l0ffn(x, inputs, serial=True):
    shards = _shard_x(x)
    tabs = host_tables()
    wprep = prep_l0ffn(inputs)
    p, nc = build_l0ffn_program(serial=serial)
    in_maps = []
    for c in range(NCORES):
        m = {"xT": shards[c],
             "f_w1": inputs["L0_ffn_w1"], "f_w2": inputs["L0_ffn_w2"]}
        m.update(tabs)
        m.update({k: np.ascontiguousarray(v, dtype=np.float32)
                  for k, v in wprep.items()})
        in_maps.append(m)
    res = run_bass_kernel_spmd(nc, in_maps, core_ids=list(range(NCORES)))
    p.close()
    out = np.concatenate([r["yT"] for r in res.results], axis=0)
    return np.ascontiguousarray(np.transpose(out, (0, 2, 1)))


NH = 16
DK = 64
QC = 512
NQC = T // QC


def _l1_perm():
    perm = []
    for g in range(4):
        for k in range(4):
            h = 4 * g + k
            perm += list(range(h * 64, h * 64 + 32))
    for g in range(4):
        for k in range(4):
            h = 4 * g + k
            perm += list(range(h * 64 + 32, h * 64 + 64))
    return np.array(perm)


def _prep2(w):
    return np.ascontiguousarray(
        w.reshape(8, 128, 8, 128).transpose(1, 2, 0, 3).reshape(128, 8192))


def prep_l1attn(inp):
    Wq = inp["L1_attn_Wq"]
    perm = _l1_perm()
    u = inp["L1_attn_u"].reshape(NH, DK)
    v = inp["L1_attn_v"].reshape(NH, DK)
    uprep = np.zeros((128, 8), np.float32)
    for m in range(8):
        for pp in range(128):
            uprep[pp, m] = u[2 * m + pp // 64, pp % 64]
    vflat = v.reshape(1024)[perm]
    vperm = np.ascontiguousarray(vflat.reshape(8, 128).T)
    return {
        "t_wq": _prep2(Wq),
        "t_wqp": _prep2(np.ascontiguousarray(Wq[:, perm])),
        "t_wk": _prep2(inp["L1_attn_Wk"]),
        "t_wv": np.ascontiguousarray(inp["L1_attn_Wv"]),
        "t_wo": _prep2(inp["L1_attn_Wo"]),
        "t_u": uprep,
        "t_v": vperm,
        "t_bo": _prep_dcol(inp["L1_attn_bo"]),
    }


L1ATTN_WEIGHT_DECLS = [
    ("t_u", (128, 8), F32),
    ("t_v", (128, 8), F32),
    ("t_bo", (128, 8), F32),
]


def l1attn_tables():
    inv = (1.0 / 10000.0 ** (np.arange(0, DK, 2, dtype=np.float64) / DK))
    q = np.arange(T, dtype=np.float64)
    k = np.arange(T, dtype=np.float64)
    sinq = np.sin(q[None, :] * inv[:, None])
    cosq = np.cos(q[None, :] * inv[:, None])
    sA = np.sin((2049.0 + q[None, :]) * inv[:, None])
    cA = np.cos((2049.0 + q[None, :]) * inv[:, None])
    cosk = np.cos(k[None, :] * inv[:, None])
    sink = np.sin(k[None, :] * inv[:, None])
    t32 = lambda a: np.ascontiguousarray(np.tile(a, (4, 1)).astype(BF16NP))
    htab = np.zeros((128, T), np.float64)
    htab[0:32] = cosk
    htab[32:64] = sink
    htab[64:96] = cosk
    htab[96:128] = sink
    kk = np.arange(128)[:, None]
    qq = np.arange(128)[None, :]
    tabs = {
        "c_tsin": t32(sinq), "c_tcos": t32(cosq),
        "c_tsa": t32(sA), "c_tca": t32(cA),
        "c_htab": np.ascontiguousarray(htab.astype(BF16NP)),
        "c_mlow": np.ascontiguousarray((kk <= qq).astype(BF16NP)),
        "c_md1": np.ascontiguousarray((kk == qq + 1).astype(BF16NP)),
        "c_mup": np.ascontiguousarray((kk >= qq + 2).astype(BF16NP)),
    }
    return tabs


def emit_l1attn_row(p, bufs, tabs, W, wq_ap, wqp_ap, wk_ap, wv_ap, wo_ap,
                    xrow_in, xrow_out):
    import os
    PH = int(os.environ.get("L1_PHASES", "9"))
    xs = bufs["xs"]
    xnb = bufs["xnb"]
    ktb = bufs["ktb"]
    qvpc = bufs["qvpc"]
    vaug = bufs["vaug"]
    scr16 = bufs["scr16"]
    wbf = bufs["wbf"]
    qg = bufs["qg"]
    prods = bufs["prods"]
    pbuf = bufs["pbuf"]
    pwb = bufs["pwb"]
    pk128 = bufs["pk128"]
    pk2 = bufs["pk2"]
    pcorn = bufs["pcorn"]
    recb = bufs["recb"]
    rrow = bufs["rrow"]
    sqb, xcb = bufs["sqb"], bufs["xcb"]
    tmf = bufs["tmf"]
    rsum, rw, rowbf = bufs["rsum"], bufs["rw"], bufs["rowbf"]
    ps_a, ps_b = bufs["ps_a"], bufs["ps_b"]
    ps_w, ps_d = bufs["ps_w"], bufs["ps_d"]
    ps_av = bufs["ps_av"]
    ps_m = bufs["ps_m"]
    ones_col = tabs["c_ones_col"]
    ones_row = tabs["c_ones_row"]

    quc = xs[:, 4096:6144].bitcast(BF16)
    ut = xs[:, 6144:8192].bitcast(BF16)
    ych = xs[:, 0:4096]
    xrel = xs[:, 4096:8192]

    def xnb_t(j, c0=0, c1=T):
        return xnb[:, j * T + c0:j * T + c1]

    for ci in range(NCH):
        h0, h1 = ci * CH, (ci + 1) * CH
        eld = p.dma("sync",
                    xs[:].rearrange("p (j t) -> p j t", j=NDT),
                    xrow_in.rearrange("(j p) t -> p j t", p=128)[:, :, h0:h1])
        revs = []
        for s0 in range(0, CH, 512):
            sevs = ln_stats_chunk(
                p, lambda j, s0=s0: xs[:, j * CH + s0:j * CH + s0 + 512],
                sqb, xcb, ps_a[0:1, 0:512], ps_b[0:1, 0:512], rsum, 0,
                ones_col, [eld], rsT=512)
            revs += ln_rows(p, rsum, rw, rowbf, sevs, c0=0, c1=512, rsT=512,
                            rbc0=s0, rbT=1024)
        eR = bcast_to_psum(p, ps_a[:, 0:512], rowbf[0:1, 0:512],
                           ones_row, revs)
        eR2 = bcast_to_psum(p, ps_w[:, 0:512], rowbf[0:1, 512:1024],
                            ones_row, revs)
        eA = bcast_to_psum(p, ps_b[:, 0:512], rowbf[0:1, 1024:1536],
                           ones_row, revs)
        eA2 = bcast_to_psum(p, ps_m[:, 0:512], rowbf[0:1, 1536:2048],
                            ones_row, revs)
        for j in range(NDT):
            for (sh, eRx, eAx, psR, psA) in ((0, eR, eA, ps_a, ps_b),
                                             (512, eR2, eA2, ps_w, ps_m)):
                tmp = tmf[j % 2]
                e1 = p.emit("vector", lambda v, j=j, sh=sh, tmp=tmp, psR=psR:
                            v.tensor_mul(tmp[:, 0:512],
                                         xs[:, j * CH + sh:j * CH + sh + 512],
                                         psR[:, 0:512]), waits=[eRx], sig=True)
                p.emit("vector", lambda v, j=j, sh=sh, tmp=tmp, psA=psA,
                       h0=h0: v.tensor_add(
                           xnb_t(j, h0 + sh, h0 + sh + 512), tmp[:, 0:512],
                           psA[:, 0:512]), waits=[e1, eAx], sig=True)

    if PH < 2:
        ev = p.dma("scalar", xrow_out.rearrange("(j p) t -> p j t", p=128)[:, :, 0:CH],
                   xs[:].rearrange("p (j t) -> p j t", j=NDT))
        return [ev]
    for m in range(8):
        ewk = p.dma("gpsimd", wbf[m % 2][:],
                    wk_ap[:, m * 1024:(m + 1) * 1024])
        for c0 in range(0, T, 512):
            mm = None
            for j in range(NDT):
                mm = p.emit("tensor", lambda t, j=j, m=m, c0=c0: t.matmul(
                    ps_a[:, 0:512], wbf[m % 2][:, j * 128:(j + 1) * 128],
                    xnb_t(j, c0, c0 + 512), start=(j == 0), stop=(j == NDT - 1)),
                    waits=[ewk], sig=(j == NDT - 1))
            p.emit("scalar", lambda s, m=m, c0=c0: s.copy(
                ktb[:, m * T + c0:m * T + c0 + 512], ps_a[:, 0:512]),
                waits=[mm], sig=True)
    vr = vaug[:].rearrange("p (tt h c) -> p tt h c", tt=16, h=16, c=65)
    p.emit("vector", lambda v: v.memset(vr[:, :, :, 64:65], 1.0), sig=True)
    ewv = p.dma("gpsimd", scr16[:].rearrange("p (j c) -> p j c", j=NDT),
                wv_ap.rearrange("(j p) c -> p j c", p=128))
    for tt in range(16):
        for half in range(2):
            mm = None
            for j in range(NDT):
                mm = p.emit("tensor", lambda t, j=j, tt=tt, half=half: t.matmul(
                    ps_a[:, 0:512],
                    xnb[:, j * T + tt * 128:j * T + (tt + 1) * 128],
                    scr16[:, j * 1024 + half * 512:j * 1024 + (half + 1) * 512],
                    start=(j == 0), stop=(j == NDT - 1)),
                    waits=[ewv], sig=(j == NDT - 1))
            p.emit("vector", lambda v, tt=tt, half=half: v.tensor_copy(
                vr[:, tt, half * 8:(half + 1) * 8, 0:64], ps_a[:, 0:512]),
                waits=[mm], sig=True)

    if PH < 3:
        ev = p.dma("scalar", xrow_out.rearrange("(j p) t -> p j t", p=128)[:, :, 0:CH],
                   xs[:].rearrange("p (j t) -> p j t", j=NDT))
        return [ev]
    st_evs = []
    for qc in range(NQC):
        q0 = qc * QC
        for m in range(8):
            ewq = p.dma("gpsimd", wbf[m % 2][:],
                        wqp_ap[:, m * 1024:(m + 1) * 1024])
            mm = None
            for j in range(NDT):
                mm = p.emit("tensor", lambda t, j=j, m=m, q0=q0: t.matmul(
                    ps_w[:, 0:512], wbf[m % 2][:, j * 128:(j + 1) * 128],
                    xnb_t(j, q0, q0 + QC), start=(j == 0), stop=(j == NDT - 1)),
                    waits=[ewq], sig=(j == NDT - 1))
            p.emit("vector", lambda v, m=m: v.tensor_scalar_add(
                qvpc[:, m * 514:m * 514 + 512], ps_w[:, 0:512],
                W["t_v"][:, m:m + 1]), waits=[mm], sig=True)
            if q0 + QC < T:
                mm2 = None
                for j in range(NDT):
                    mm2 = p.emit("tensor", lambda t, j=j, m=m, q0=q0: t.matmul(
                        ps_d[:, 0:1], wbf[m % 2][:, j * 128:(j + 1) * 128],
                        xnb_t(j, q0 + QC, q0 + QC + 1),
                        start=(j == 0), stop=(j == NDT - 1)),
                        waits=[ewq], sig=(j == NDT - 1))
                p.emit("vector", lambda v, m=m: v.tensor_scalar_add(
                    qvpc[:, m * 514 + 512:m * 514 + 513], ps_d[:, 0:1],
                    W["t_v"][:, m:m + 1]), waits=[mm2], sig=True)
            else:
                p.emit("vector", lambda v, m=m: v.memset(
                    qvpc[:, m * 514 + 512:m * 514 + 513], 0.0), sig=True)
        for m in range(8):
            ewq = p.dma("gpsimd", wbf[m % 2][:],
                        wq_ap[:, m * 1024:(m + 1) * 1024])
            mm = None
            for j in range(NDT):
                mm = p.emit("tensor", lambda t, j=j, m=m, q0=q0: t.matmul(
                    ps_a[:, 0:512], wbf[m % 2][:, j * 128:(j + 1) * 128],
                    xnb_t(j, q0, q0 + QC), start=(j == 0), stop=(j == NDT - 1)),
                    waits=[ewq], sig=(j == NDT - 1))
            p.emit("vector", lambda v, m=m: v.tensor_scalar_add(
                quc[:, m * QC:(m + 1) * QC], ps_a[:, 0:512],
                W["t_u"][:, m:m + 1]), waits=[mm], sig=True)
        for g in range(4):
            sft = qvpc[:, g * 514:g * 514 + 512]
            cft = qvpc[:, (4 + g) * 514:(4 + g) * 514 + 512]
            sfs = qvpc[:, g * 514 + 1:g * 514 + 513]
            cfs = qvpc[:, (4 + g) * 514 + 1:(4 + g) * 514 + 513]
            e1 = p.emit("vector", lambda v, sft=sft, q0=q0: v.tensor_mul(
                prods[0][:], sft, tabs["c_tsin"][:, q0:q0 + QC]), sig=True)
            e2 = p.emit("vector", lambda v, cft=cft, q0=q0: v.tensor_mul(
                prods[1][:], cft, tabs["c_tcos"][:, q0:q0 + QC]), sig=True)
            e3 = p.emit("vector", lambda v, cft=cft, q0=q0: v.tensor_mul(
                prods[2][:], cft, tabs["c_tsin"][:, q0:q0 + QC]), sig=True)
            e4 = p.emit("vector", lambda v, sft=sft, q0=q0: v.tensor_mul(
                prods[3][:], sft, tabs["c_tcos"][:, q0:q0 + QC]), sig=True)
            for k in range(4):
                h = 4 * g + k
                b1 = 64 if h % 2 == 0 else 0
                b2 = 96 if h % 2 == 0 else 32
                sl = slice(32 * k, 32 * k + 32)
                p.emit("vector", lambda v, h=h, b1=b1, sl=sl: v.tensor_add(
                    qg[h % 4][b1:b1 + 32, :], prods[0][sl, :], prods[1][sl, :]),
                    waits=[e1, e2], sig=True)
                p.emit("vector", lambda v, h=h, b2=b2, sl=sl: v.tensor_sub(
                    qg[h % 4][b2:b2 + 32, :], prods[2][sl, :], prods[3][sl, :]),
                    waits=[e3, e4], sig=True)
            e5 = p.emit("vector", lambda v, sfs=sfs, q0=q0: v.tensor_mul(
                prods[0][:, 0:sfs.shape[-1]], sfs,
                tabs["c_tsa"][:, q0:q0 + sfs.shape[-1]]), sig=True)
            e6 = p.emit("vector", lambda v, cfs=cfs, q0=q0: v.tensor_mul(
                prods[1][:, 0:cfs.shape[-1]], cfs,
                tabs["c_tca"][:, q0:q0 + cfs.shape[-1]]), sig=True)
            e7 = p.emit("vector", lambda v, cfs=cfs, q0=q0: v.tensor_mul(
                prods[2][:, 0:cfs.shape[-1]], cfs,
                tabs["c_tsa"][:, q0:q0 + cfs.shape[-1]]), sig=True)
            e8 = p.emit("vector", lambda v, sfs=sfs, q0=q0: v.tensor_mul(
                prods[3][:, 0:sfs.shape[-1]], sfs,
                tabs["c_tca"][:, q0:q0 + sfs.shape[-1]]), sig=True)
            wq_ = sfs.shape[-1]
            for k in range(4):
                h = 4 * g + k
                b1 = 64 if h % 2 == 0 else 0
                b2 = 96 if h % 2 == 0 else 32
                sl = slice(32 * k, 32 * k + 32)
                p.emit("vector", lambda v, h=h, b1=b1, sl=sl, wq_=wq_:
                       v.tensor_add(qg[4 + h % 4][b1:b1 + 32, 0:wq_],
                                    prods[0][sl, 0:wq_], prods[1][sl, 0:wq_]),
                       waits=[e5, e6], sig=True)
                p.emit("vector", lambda v, h=h, b2=b2, sl=sl, wq_=wq_:
                       v.tensor_sub(qg[4 + h % 4][b2:b2 + 32, 0:wq_],
                                    prods[2][sl, 0:wq_], prods[3][sl, 0:wq_]),
                       waits=[e7, e8], sig=True)
            if PH >= 5:
                for k in range(4):
                    h = 4 * g + k
                    emit_l1attn_head(p, bufs, tabs, W, h, qc, qg[h % 4],
                                     qg[4 + h % 4], quc, ut)
        if PH < 6:
            continue
        for ct in range(8):
            ewo = p.dma("gpsimd", wbf[ct % 2][:],
                        wo_ap[:, ct * 1024:(ct + 1) * 1024])
            mm = None
            for hp in range(8):
                mm = p.emit("tensor", lambda t, hp=hp, ct=ct: t.matmul(
                    ps_a[:, 0:512], wbf[ct % 2][:, hp * 128:(hp + 1) * 128],
                    ut[:, hp * QC:(hp + 1) * QC], start=(hp == 0),
                    stop=(hp == 7)), waits=[ewo], sig=(hp == 7))
            p.emit("vector", lambda v, ct=ct: v.tensor_scalar_add(
                ych[:, ct * 512:(ct + 1) * 512], ps_a[:, 0:512],
                W["t_bo"][:, ct:ct + 1]), waits=[mm], sig=True)
        sevs = ln_stats_chunk(
            p, lambda j: ych[:, j * 512:(j + 1) * 512],
            sqb, xcb, ps_a[0:1, 0:512], ps_b[0:1, 0:512], rsum, 0,
            ones_col, [], rsT=512)
        revs = ln_rows(p, rsum, rw, rowbf, sevs, c0=0, c1=QC, rsT=512,
                       rbc0=0, rbT=1024)
        eldx = p.dma("sync", xrel[:].rearrange("p (j t) -> p j t", j=NDT),
                     xrow_in.rearrange("(j p) t -> p j t", p=128)[
                         :, :, q0:q0 + QC])
        eR = bcast_to_psum(p, ps_a[:, 0:512], rowbf[0:1, 0:QC],
                           ones_row, revs)
        eA = bcast_to_psum(p, ps_b[:, 0:512], rowbf[0:1, 1024:1024 + QC],
                           ones_row, revs)
        for j in range(NDT):
            tmp = tmf[j % 2]
            e1 = p.emit("vector", lambda v, j=j, tmp=tmp: v.tensor_mul(
                tmp[:, 0:512], ych[:, j * 512:(j + 1) * 512], ps_a[:, 0:512]),
                waits=[eR], sig=True)
            e2 = p.emit("vector", lambda v, tmp=tmp: v.tensor_add(
                tmp[:, 0:512], tmp[:, 0:512], ps_b[:, 0:512]),
                waits=[e1, eA], sig=True)
            e3 = p.emit("vector", lambda v, j=j, tmp=tmp: v.tensor_add(
                xrel[:, j * 512:(j + 1) * 512], xrel[:, j * 512:(j + 1) * 512],
                tmp[:, 0:512]), waits=[e2, eldx], sig=True)
            st_evs.append(p.dma(
                "scalar",
                xrow_out.rearrange("(j p) t -> p j t", p=128)[:, j, q0:q0 + QC],
                xrel[:, j * 512:(j + 1) * 512], waits=[e3]))
    return st_evs


def emit_l1attn_head(p, bufs, tabs, W, h, qc, qgh, qgph, quc, ut):
    import os
    HD = int(os.environ.get("L1_HEAD", "9"))
    ktb = bufs["ktb"]
    vaug = bufs["vaug"]
    pbuf = bufs["pbuf"]
    pwb = bufs["pwb"]
    pk128 = bufs["pk128"]
    pk2 = bufs["pk2"]
    pcorn_s = bufs["pcorn"]
    recb = bufs["recb"]
    rrow = bufs["rrow"]
    ps_S = bufs["ps_a"] if h % 2 == 0 else bufs["ps_b"]
    ps_w, ps_d = bufs["ps_w"], bufs["ps_d"]
    ps_av = bufs["ps_av"][h % 2]
    ps_m = bufs["ps_m"]
    htab = tabs["c_htab"]
    q0 = qc * QC
    hp = h // 2
    bA = 0 if h % 2 == 0 else 64
    b1 = 64 if h % 2 == 0 else 0
    b2 = 96 if h % 2 == 0 else 32
    cc8 = p.constcol(0.125)
    ktr2 = ktb[:].rearrange("p (m kt c) -> p m kt c", m=8, kt=16, c=128)
    qur2 = quc[:].rearrange("p (m qt c) -> p m qt c", m=8, qt=4, c=128)
    vav = vaug[:].rearrange("p (tt hh c) -> p tt hh c", tt=16, hh=16, c=65)

    kts = [kt for kt in range(qc * 4 + 1, qc * 4 + 5) if kt < 16]
    ncor = len(kts)
    ec2 = None
    if ncor and HD >= 3:
        ec1 = None
        for i, kt_ in enumerate(kts):
            kap = ktr2[bA:bA + 64, hp, kt_, 0:1]
            qap = qur2[bA:bA + 64, hp, kt_ - qc * 4 - 1, 127:128]
            ec1 = p.emit("tensor", lambda t, i=i, kap=kap, qap=qap: t.matmul(
                ps_d[0:1, i:i + 1], kap, qap, start=True, stop=True),
                sig=(i == ncor - 1))
        ec2 = p.emit("scalar", lambda s: s.activation(
            pcorn_s[0:1, 0:ncor], ps_d[0:1, 0:ncor], AF.Exp,
            scale=cc8[0:1, 0:1]), waits=[ec1], sig=True)
    av_last = None
    exp_evs = [None, None]
    av_evs = [None, None]
    prev_mix = None
    with p.relaxed():
      for kt in range(16):
        buf = pbuf[kt % 2]
        ps_S = bufs["ps_a"] if kt % 2 == 0 else bufs["ps_b"]
        lo_t, hi_t = qc * 4, qc * 4 + 3
        kind = "main" if kt < lo_t else ("wrap" if kt > hi_t else "mix")
        if kind in ("main", "wrap"):
            qgx = qgh if kind == "main" else qgph
            eS = emit_score_matmuls(p, ps_S, ktb, htab, qgx, quc, hp, kt,
                                    bA, b1, b2, pre=[exp_evs[kt % 2]])
            eP = p.emit("scalar", lambda s, buf=buf, ps_S=ps_S: s.activation(
                buf[:], ps_S[:, 0:512], AF.Exp, scale=cc8),
                waits=[eS, av_evs[kt % 2]], sig=True)
        else:
            p.serial = True
            eS = emit_score_matmuls(p, ps_S, ktb, htab, qgh, quc, hp, kt,
                                    bA, b1, b2, pre=[exp_evs[kt % 2],
                                                     prev_mix])
            eP = p.emit("scalar", lambda s, buf=buf, ps_S=ps_S: s.activation(
                buf[:], ps_S[:, 0:512], AF.Exp, scale=cc8),
                waits=[eS, av_evs[kt % 2]], sig=True)
            eSw = emit_score_matmuls(p, ps_w, ktb, htab, qgph, quc, hp, kt,
                                     bA, b1, b2)
            dcol = (kt - lo_t) * 128
            if dcol > 0:
                eP = p.emit("scalar", lambda s, buf=buf, dcol=dcol:
                            s.activation(buf[:, 0:dcol], ps_w[:, 0:dcol],
                                         AF.Exp, scale=cc8),
                            waits=[eSw, eP], sig=True)
            eK = p.emit("tensor", lambda t, kt=kt, dcol=dcol: t.matmul(
                ps_d[:, 0:128],
                ktr2[bA:bA + 64, hp, kt, :],
                qur2[bA:bA + 64, hp, (q0 + dcol) // 128 - qc * 4, :],
                start=True, stop=True), waits=[eP, ec2], sig=True)
            eK2 = p.emit("scalar", lambda s: s.activation(
                pk128[:], ps_d[:, 0:128], AF.Exp, scale=cc8),
                waits=[eK], sig=True)
            e1 = p.emit("vector", lambda v: v.tensor_mul(
                pk2[:], pk128[:], tabs["c_md1"][:]), waits=[eK2], sig=True)
            eW1 = p.emit("scalar", lambda s, dcol=dcol: s.activation(
                pwb[:, 0:128], ps_w[:, dcol:dcol + 128], AF.Exp, scale=cc8),
                waits=[eSw], sig=True)
            eW2 = p.emit("vector", lambda v: v.tensor_mul(
                pwb[:, 128:256], pwb[:, 0:128], tabs["c_mup"][:]),
                waits=[eW1], sig=True)
            e2 = p.emit("vector", lambda v, buf=buf, dcol=dcol: v.tensor_mul(
                buf[:, dcol:dcol + 128], buf[:, dcol:dcol + 128],
                tabs["c_mlow"][:]), waits=[eP], sig=True)
            e4 = p.emit("vector", lambda v, buf=buf, dcol=dcol: v.tensor_add(
                buf[:, dcol:dcol + 128], buf[:, dcol:dcol + 128], pk2[:]),
                waits=[e1, e2], sig=True)
            eP = p.emit("vector", lambda v, buf=buf, dcol=dcol: v.tensor_add(
                buf[:, dcol:dcol + 128], buf[:, dcol:dcol + 128],
                pwb[:, 128:256]), waits=[e4, eW2], sig=True)
            prev_mix = eP
            p.serial = False
        if kt in kts:
            ci = kt - (qc * 4 + 1)
            cloc = (kt - 1 - qc * 4) * 128 + 127
            eP = p.emit("vector", lambda v, buf=buf, ci=ci, cloc=cloc:
                        v.tensor_copy(buf[0:1, cloc:cloc + 1],
                                      pcorn_s[0:1, ci:ci + 1]),
                        waits=[eP, ec2], sig=True)
        exp_evs[kt % 2] = eP
        av_last = p.emit("tensor", lambda t, kt=kt, buf=buf: t.matmul(
            ps_av[0:65, 0:512], vav[:, kt, h, :], buf[:],
            start=(kt == 0), stop=(kt == 15)), waits=[eP], sig=True)
        av_evs[kt % 2] = av_last
    e1 = p.emit("vector", lambda v: v.tensor_copy(
        rrow[0:1, 0:512], ps_av[64:65, 0:512]), waits=[av_last], sig=True)
    e1 = p.emit("vector", lambda v: v.tensor_copy(
        rrow[0:1, 0:512], ps_av[64:65, 0:512]), sig=True)
    e2 = p.emit("scalar", lambda s: s.activation(
        rrow[0:1, 512:1024], rrow[0:1, 0:512], AF.Ln), waits=[e1], sig=True)
    cc_m1h = p.constcol(-1.0)[0:1, 0:1]
    e3 = p.emit("scalar", lambda s: s.activation(
        rrow[0:1, 0:512], rrow[0:1, 512:1024], AF.Exp,
        scale=cc_m1h), waits=[e2], sig=True)
    e4 = p.emit("vector", lambda v: v.tensor_copy(
        rrow[0:1, 512:1024].bitcast(BF16)[0:1, 0:512], rrow[0:1, 0:512]),
        waits=[e3], sig=True)
    e5 = bcast_to_psum(p, ps_m[:, 0:512],
                       rrow[0:1, 512:1024].bitcast(BF16)[0:1, 0:512],
                       tabs["c_ones_row"], [e4])
    e6 = p.emit("scalar", lambda s: s.copy(recb[:], ps_m[:, 0:512]),
                waits=[e5], sig=True)
    p.emit("vector", lambda v: v.tensor_mul(
        ut[bA:bA + 64, hp * QC:(hp + 1) * QC], ps_av[0:64, 0:512],
        recb[bA:bA + 64, :]), waits=[e6], sig=True)


def emit_score_matmuls(p, ps, ktb, htab, qgx, quc, hp, kt, bA, b1, b2,
                       pre=()):
    bG = 64 - bA
    mmA = lambda t, start, stop: t.matmul(
        ps[:, 0:512], ktb[bA:bA + 64, hp * T + kt * 128:hp * T + (kt + 1) * 128],
        quc[bA:bA + 64, hp * QC:(hp + 1) * QC], start=start, stop=stop,
        skip_group_check=True)
    mmG = lambda t, start, stop: t.matmul(
        ps[:, 0:512], htab[bG:bG + 64, kt * 128:(kt + 1) * 128],
        qgx[bG:bG + 64, :], start=start, stop=stop, skip_group_check=True)
    if bA == 0:
        ei = p.emit("tensor", lambda t: mmA(t, True, False), waits=list(pre),
                    sig=True)
        return p.emit("tensor", lambda t: mmG(t, False, True), sig=True,
                      force_waits=[ei])
    else:
        ei = p.emit("tensor", lambda t: mmG(t, True, False), waits=list(pre),
                    sig=True)
        return p.emit("tensor", lambda t: mmA(t, False, True), sig=True,
                      force_waits=[ei])


def alloc_l1attn_bufs(p):
    bufs = {}
    bufs["xs"] = p.sbuf("xs", [128, NDT * CH], F32)
    bufs["xnb"] = p.sbuf("xnb", [128, NDT * T], BF16)
    bufs["ktb"] = p.sbuf("ktb", [128, NDT * T], BF16)
    bufs["qvpc"] = p.sbuf("qvpc", [128, 8 * 514], BF16)
    bufs["vaug"] = p.sbuf("vaug", [128, 16 * 16 * 65], BF16)
    bufs["scr16"] = p.sbuf("scr16", [128, 8192], BF16)
    s16 = bufs["scr16"]
    bufs["qg"] = [s16[:, i * 512:(i + 1) * 512] for i in range(8)]
    bufs["prods"] = [s16[:, 4096 + i * 512:4096 + (i + 1) * 512]
                     for i in range(4)]
    bufs["pbuf"] = [s16[:, 6144 + i * 512:6144 + (i + 1) * 512]
                    for i in range(2)]
    bufs["pwb"] = s16[:, 7168:7424]
    bufs["pk128"] = s16[:, 7424:7552]
    bufs["pk2"] = s16[:, 7552:7680]
    bufs["recb"] = s16[:, 7680:8192]
    bufs["pcorn"] = p.sbuf("pcorn", [1, 4], F32)
    _wbf = p.sbuf("wbf", [128, 1024], BF16)
    bufs["wbf"] = [_wbf, _wbf]
    bufs["sqb"] = [s16[:, 2048 + i * 512:2048 + (i + 1) * 512]
                   for i in range(2)]
    bufs["xcb"] = [s16[:, 3072 + i * 512:3072 + (i + 1) * 512]
                   for i in range(2)]
    _t32 = s16[:, 0:2048].bitcast(F32)
    bufs["tmf"] = [_t32[:, 0:512], _t32[:, 512:1024]]
    bufs["rsum"] = p.sbuf("rsum", [1, 1024], F32)
    bufs["rw"] = s16[:, 4096:7168].bitcast(F32)[0:1, 0:1536]
    bufs["rrow"] = p.sbuf("rrow", [1, 1024], F32)
    bufs["rowbf"] = p.sbuf("rowbf", [1, 2048], BF16)
    bufs["ps_a"] = p.psum("ps_a", [128, 512])
    bufs["ps_b"] = p.psum("ps_b", [128, 512])
    bufs["ps_w"] = p.psum("ps_w", [128, 512])
    bufs["ps_d"] = p.psum("ps_d", [128, 128])
    bufs["ps_av"] = [p.psum(f"ps_av{i}", [128, 512]) for i in range(2)]
    bufs["ps_m"] = p.psum("ps_m", [128, 512])
    return bufs


def emit_block_l1attn(p, xin, xout):
    wq = p.dram_in("t_wq", [128, 8192], F32)
    wqp = p.dram_in("t_wqp", [128, 8192], F32)
    wk = p.dram_in("t_wk", [128, 8192], F32)
    wv = p.dram_in("t_wv", [1024, 1024], F32)
    wo = p.dram_in("t_wo", [128, 8192], F32)
    tabs_np = host_tables()
    del tabs_np["c_ident"], tabs_np["c_onesf"]
    tabs_np.update(l1attn_tables())
    tabs, tev = declare_tables(p, tabs_np)
    Wsb, wev = declare_weights(p, L1ATTN_WEIGHT_DECLS)
    bufs = alloc_l1attn_bufs(p)
    st_evs = []
    for r in range(ROWS_PER_CORE):
        st_evs += emit_l1attn_row(p, bufs, tabs, Wsb, wq, wqp, wk, wv, wo,
                                  xin[r], xout[r])
    return st_evs


def build_l1attn_program(serial=True, dbg=False):
    p = Prog(serial=serial)
    p.dbg = dbg
    xin = p.dram_in("xT", [ROWS_PER_CORE, D, T], F32)
    xout = p.dram_out("yT", [ROWS_PER_CORE, D, T], F32)
    st_evs = emit_block_l1attn(p, xin, xout)
    nc = p.finalize(final_waits=st_evs)
    return p, nc


def run_l1attn(x, inputs, serial=True):
    shards = _shard_x(x)
    tabs = host_tables()
    del tabs["c_ident"], tabs["c_onesf"]
    tabs.update(l1attn_tables())
    wprep = prep_l1attn(inputs)
    p, nc = build_l1attn_program(serial=serial)
    in_maps = []
    for c in range(NCORES):
        m = {"xT": shards[c]}
        m.update(tabs)
        m.update({k: np.ascontiguousarray(v, dtype=np.float32)
                  for k, v in wprep.items()})
        in_maps.append(m)
    res = run_bass_kernel_spmd(nc, in_maps, core_ids=list(range(NCORES)))
    p.close()
    out = np.concatenate([r["yT"] for r in res.results], axis=0)
    return np.ascontiguousarray(np.transpose(out, (0, 2, 1)))


DC = 256
HID = 2048
FEPS = 1e-6


def prep_l1ffn(inp):
    perm66 = None
    We = inp["L1_ffn_We_gate"]
    Wg = inp["L1_ffn_Wg_gate"]
    gate = np.zeros((1024, 66), np.float32)
    gate[:, 0:4] = We[:, 0:4]
    gate[:, 32:36] = We[:, 4:8]
    gate[:, 64:66] = Wg
    gbias = np.zeros((66, 1), np.float32)
    gbias[0:4, 0] = inp["L1_ffn_expert_bias"][0:4]
    gbias[32:36, 0] = inp["L1_ffn_expert_bias"][4:8]
    gbias[64:66, 0] = inp["L1_ffn_group_bias"]
    return {
        "m_wq": _prep2(inp["L1_ffn_Wq"]),
        "m_wkc": np.ascontiguousarray(
            inp["L1_ffn_Wk_c"].reshape(8, 128, 2, 128).transpose(1, 2, 0, 3)
            .reshape(128, 2048)),
        "m_wvc": np.ascontiguousarray(
            inp["L1_ffn_Wv_c"].reshape(8, 128, 2, 128).transpose(1, 2, 0, 3)
            .reshape(128, 2048)),
        "m_wk": np.ascontiguousarray(
            inp["L1_ffn_Wk"].reshape(2, 128, 8, 128).transpose(1, 2, 0, 3)
            .reshape(128, 2048)),
        "m_wv": np.ascontiguousarray(inp["L1_ffn_Wv"]),
        "m_wo": _prep2(inp["L1_ffn_Wo"]),
        "m_wsi": np.ascontiguousarray(
            inp["L1_ffn_W_shared_in"].reshape(8, 128, 32, 128)
            .transpose(1, 2, 0, 3).reshape(128, 32 * 1024)),
        "m_w1s": np.ascontiguousarray(
            inp["L1_ffn_W1_shared"].reshape(8, 128, 32, 128)
            .transpose(1, 2, 0, 3).reshape(128, 32 * 1024)),
        "m_wso": np.ascontiguousarray(inp["L1_ffn_W_shared_out"]),
        "m_gate": np.ascontiguousarray(
            gate.reshape(8, 128, 66).transpose(1, 0, 2).reshape(128, 528)),
        "m_gbias": gbias,
        "m_n1w": _prep_dcol(inp["L1_ffn_norm1_w"]),
        "m_n2w": _prep_dcol(inp["L1_ffn_norm2_w"]),
    }


L1FFN_WEIGHT_DECLS = [
    ("m_wk", (128, 2048), BF16),
    ("m_gate", (128, 528), BF16),
    ("m_gbias", (66, 1), F32),
    ("m_n1w", (128, 8), F32),
    ("m_n2w", (128, 8), F32),
]


def l1ffn_tables():
    inv = (1.0 / 10000.0 ** (np.arange(0, 64, 2, dtype=np.float64) / 64))
    t = np.arange(T, dtype=np.float64)
    sin = np.sin(t[None, :] * inv[:, None])
    cos = np.cos(t[None, :] * inv[:, None])
    T1 = np.zeros((64, T)); T2 = np.zeros((64, T))
    T1[0::2] = cos; T1[1::2] = cos
    T2[0::2] = -sin; T2[1::2] = sin
    return {
        "c_rt1": np.ascontiguousarray(np.tile(T1, (2, 1)).astype(BF16NP)),
        "c_rt2": np.ascontiguousarray(np.tile(T2, (2, 1)).astype(BF16NP)),
        "c_identf": np.eye(128, dtype=np.float32),
    }


def emit_rope(p, dst, srcap, tabs, shuf, waits=()):
    pass


def emit_l1ffn_row(p, bufs, tabs, W, waps, xrow_in, xrow_out):
    xs = bufs["xs"]
    xnb = bufs["xnb"]
    x1b = bufs["x1b"]
    ktb = bufs["ktb"]
    vaug = bufs["vaug"]
    cb = bufs["cb"]
    scr16 = bufs["scr16"]
    pbuf = bufs["pbuf"]
    recb = bufs["recb"]
    rrow = bufs["rrow"]
    wtok = bufs["wtok"]
    lgsb = bufs["lgsb"]
    gtt = bufs["gtt"]
    wbf = bufs["wbf"]
    sqb, xcb = bufs["sqb"], bufs["xcb"]
    tmf = bufs["tmf"]
    rsum, rw, rowbf = bufs["rsum"], bufs["rw"], bufs["rowbf"]
    ps_a, ps_b = bufs["ps_a"], bufs["ps_b"]
    ps_w, ps_d = bufs["ps_w"], bufs["ps_d"]
    ps_av = bufs["ps_av"]
    ps_m = bufs["ps_m"]
    ones_col = tabs["c_ones_col"]
    ones_row = tabs["c_ones_row"]
    rt1, rt2 = tabs["c_rt1"], tabs["c_rt2"]
    swapmask = []
    for i in range(16):
        swapmask += [2 * i + 1, 2 * i]
    quc = xs[:, 4096:6144].bitcast(BF16)
    ut = xs[:, 6144:8192].bitcast(BF16)
    ych = xs[:, 0:4096]
    xrel = xs[:, 4096:8192]

    def xnb_t(j, c0=0, c1=T):
        return xnb[:, j * T + c0:j * T + c1]

    for ci in range(NCH):
        h0, h1 = ci * CH, (ci + 1) * CH
        eld = p.dma("sync",
                    xs[:].rearrange("p (j t) -> p j t", j=NDT),
                    xrow_in.rearrange("(j p) t -> p j t", p=128)[:, :, h0:h1])
        revs = []
        for s0 in range(0, CH, 512):
            sevs = ln_stats_chunk(
                p, lambda j, s0=s0: xs[:, j * CH + s0:j * CH + s0 + 512],
                sqb, xcb, ps_a[0:1, 0:512], ps_b[0:1, 0:512], rsum, 0,
                ones_col, [eld], rsT=512)
            revs += ln_rows(p, rsum, rw, rowbf, sevs, c0=0, c1=512, rsT=512,
                            rbc0=s0, rbT=1024)
        eR = bcast_to_psum(p, ps_a[:, 0:512], rowbf[0:1, 0:512],
                           ones_row, revs)
        eR2 = bcast_to_psum(p, ps_w[:, 0:512], rowbf[0:1, 512:1024],
                            ones_row, revs)
        eA = bcast_to_psum(p, ps_b[:, 0:512], rowbf[0:1, 1024:1536],
                           ones_row, revs)
        eA2 = bcast_to_psum(p, ps_m[:, 0:512], rowbf[0:1, 1536:2048],
                            ones_row, revs)
        for j in range(NDT):
            for (sh, eRx, eAx, psR, psA) in ((0, eR, eA, ps_a, ps_b),
                                             (512, eR2, eA2, ps_w, ps_m)):
                tmp = tmf[j % 2]
                e1 = p.emit("vector", lambda v, j=j, sh=sh, tmp=tmp, psR=psR,
                            ci=ci: v.tensor_mul(
                                tmp[:, 0:512],
                                xs[:, j * CH + sh:j * CH + sh + 512],
                                psR[:, 0:512]), waits=[eRx], sig=True)
                p.emit("vector", lambda v, j=j, sh=sh, tmp=tmp, psA=psA,
                       h0=h0: v.tensor_add(
                           xnb_t(j, h0 + sh, h0 + sh + 512), tmp[:, 0:512],
                           psA[:, 0:512]), waits=[e1, eAx], sig=True)

    for lowrank, wname in ((0, "m_wvc"), (1, "m_wkc")):
        for m in range(2):
            ewc = p.dma("gpsimd", bufs["wbf2"][:],
                        waps[wname][:, m * 1024:(m + 1) * 1024])
            for c0 in range(0, T, 512):
                mm = None
                for j in range(NDT):
                    mm = p.emit("tensor", lambda t, j=j, m=m, c0=c0:
                                t.matmul(
                                    ps_a[:, 0:512],
                                    bufs["wbf2"][:, j * 128:(j + 1) * 128],
                                    xnb_t(j, c0, c0 + 512),
                                    start=(j == 0), stop=(j == NDT - 1)),
                                waits=[ewc], sig=(j == NDT - 1))
                p.emit("scalar", lambda s, m=m, c0=c0: s.copy(
                    cb[:, m * T + c0:m * T + c0 + 512], ps_a[:, 0:512]),
                    waits=[mm], sig=True)
        if lowrank == 0:
            vr = vaug[:].rearrange("p (tt hh c) -> p tt hh c", tt=16, hh=16,
                                   c=65)
            p.emit("vector", lambda v: v.memset(vr[:, :, :, 64:65], 1.0),
                   sig=True)
            ewv = p.dma("gpsimd", scr16[:, 0:2048].rearrange(
                "p (m c) -> p m c", m=2),
                waps["m_wv"].rearrange("(m p) c -> p m c", p=128))
            for tt in range(16):
                for half in range(2):
                    mm = None
                    for m in range(2):
                        mm = p.emit("tensor", lambda t, m=m, tt=tt, half=half:
                                    t.matmul(
                                        ps_a[:, 0:512],
                                        cb[:, m * T + tt * 128:
                                           m * T + (tt + 1) * 128],
                                        scr16[:, m * 1024 + half * 512:
                                              m * 1024 + (half + 1) * 512],
                                        start=(m == 0), stop=(m == 1)),
                                    waits=[ewv], sig=(m == 1))
                    p.emit("vector", lambda v, tt=tt, half=half: v.tensor_copy(
                        vr[:, tt, half * 8:(half + 1) * 8, 0:64],
                        ps_a[:, 0:512]), waits=[mm], sig=True)
    for m in range(8):
        for c0 in range(0, T, 512):
            mm = None
            for dc in range(2):
                mm = p.emit("tensor", lambda t, dc=dc, m=m, c0=c0: t.matmul(
                    ps_a[:, 0:512],
                    W["m_wk"][:, (m * 2 + dc) * 128:(m * 2 + dc) * 128 + 128],
                    cb[:, dc * T + c0:dc * T + c0 + 512],
                    start=(dc == 0), stop=(dc == 1)), sig=(dc == 1))
            ek = p.emit("scalar", lambda s, m=m, c0=c0: s.copy(
                ktb[:, m * T + c0:m * T + c0 + 512], ps_a[:, 0:512]),
                waits=[mm], sig=True)
    for m in range(8):
        for c0 in range(0, T, 512):
            kt_ = ktb[:, m * T + c0:m * T + c0 + 512]
            e1 = p.emit("vector", lambda v, kt_=kt_: v.stream_shuffle(
                bufs["rsc"][:], kt_, swapmask), sig=True)
            e2 = p.emit("vector", lambda v, kt_=kt_, c0=c0: v.tensor_mul(
                bufs["rsc2"][:], kt_, rt1[:, c0:c0 + 512]), sig=True)
            e3 = p.emit("vector", lambda v, c0=c0: v.tensor_mul(
                bufs["rsc"][:], bufs["rsc"][:], rt2[:, c0:c0 + 512]),
                waits=[e1], sig=True)
            p.emit("vector", lambda v, kt_=kt_: v.tensor_add(
                kt_, bufs["rsc2"][:], bufs["rsc"][:]), waits=[e2, e3],
                sig=True)

    for qc in range(NQC):
        q0 = qc * QC
        for m in range(8):
            ewq = p.dma("gpsimd", wbf[0][:],
                        waps["m_wq"][:, m * 1024:(m + 1) * 1024])
            mm = None
            for j in range(NDT):
                mm = p.emit("tensor", lambda t, j=j, m=m, q0=q0: t.matmul(
                    ps_a[:, 0:512], wbf[0][:, j * 128:(j + 1) * 128],
                    xnb_t(j, q0, q0 + QC), start=(j == 0), stop=(j == NDT - 1)),
                    waits=[ewq], sig=(j == NDT - 1))
            eq = p.emit("scalar", lambda s, m=m: s.copy(
                quc[:, m * QC:(m + 1) * QC], ps_a[:, 0:512]), waits=[mm],
                sig=True)
            qt_ = quc[:, m * QC:(m + 1) * QC]
            e1 = p.emit("vector", lambda v, qt_=qt_: v.stream_shuffle(
                bufs["rsc"][:], qt_, swapmask), waits=[eq], sig=True)
            e2 = p.emit("vector", lambda v, qt_=qt_, q0=q0: v.tensor_mul(
                bufs["rsc2"][:], qt_, rt1[:, q0:q0 + QC]), waits=[eq], sig=True)
            e3 = p.emit("vector", lambda v, q0=q0: v.tensor_mul(
                bufs["rsc"][:], bufs["rsc"][:], rt2[:, q0:q0 + QC]),
                waits=[e1], sig=True)
            p.emit("vector", lambda v, qt_=qt_: v.tensor_add(
                qt_, bufs["rsc2"][:], bufs["rsc"][:]), waits=[e2, e3],
                sig=True)
        cc8 = p.constcol(0.125)
        cc_m1h = p.constcol(-1.0)[0:1, 0:1]
        vav = vaug[:].rearrange("p (tt hh c) -> p tt hh c", tt=16, hh=16, c=65)
        for h in range(NH):
            hp = h // 2
            bA = 0 if h % 2 == 0 else 64
            ps_S = ps_a if h % 2 == 0 else ps_b
            ps_avh = ps_av[h % 2]
            for kt in range(16):
                buf = pbuf[kt % 2]
                eS = p.emit("tensor", lambda t, hp=hp, kt=kt, bA=bA, ps_S=ps_S:
                            t.matmul(
                                ps_S[:, 0:512],
                                ktb[bA:bA + 64, hp * T + kt * 128:
                                    hp * T + (kt + 1) * 128],
                                quc[bA:bA + 64, hp * QC:(hp + 1) * QC],
                                start=True, stop=True), sig=True)
                eP = p.emit("scalar", lambda s, buf=buf, ps_S=ps_S:
                            s.activation(buf[:], ps_S[:, 0:512], AF.Exp,
                                         scale=cc8), waits=[eS], sig=True)
                p.emit("tensor", lambda t, kt=kt, buf=buf, h=h, ps_avh=ps_avh:
                       t.matmul(ps_avh[0:65, 0:512], vav[:, kt, h, :], buf[:],
                                start=(kt == 0), stop=(kt == 15)),
                       waits=[eP], sig=True)
            e1 = p.emit("vector", lambda v, ps_avh=ps_avh: v.tensor_copy(
                rrow[0:1, 0:512], ps_avh[64:65, 0:512]), sig=True)
            e2 = p.emit("scalar", lambda s: s.activation(
                rrow[0:1, 512:1024], rrow[0:1, 0:512], AF.Ln), waits=[e1],
                sig=True)
            e3 = p.emit("scalar", lambda s: s.activation(
                rrow[0:1, 0:512], rrow[0:1, 512:1024], AF.Exp,
                scale=cc_m1h), waits=[e2], sig=True)
            e4 = p.emit("vector", lambda v: v.tensor_copy(
                rrow[0:1, 512:1024].bitcast(BF16)[0:1, 0:512],
                rrow[0:1, 0:512]), waits=[e3], sig=True)
            e5 = bcast_to_psum(p, ps_m[:, 0:512],
                               rrow[0:1, 512:1024].bitcast(BF16)[0:1, 0:512],
                               ones_row, [e4])
            e6 = p.emit("scalar", lambda s: s.copy(recb[:], ps_m[:, 0:512]),
                        waits=[e5], sig=True)
            p.emit("vector", lambda v, bA=bA, hp=hp, ps_avh=ps_avh:
                   v.tensor_mul(ut[bA:bA + 64, hp * QC:(hp + 1) * QC],
                                ps_avh[0:64, 0:512], recb[bA:bA + 64, :]),
                   waits=[e6], sig=True)
        for ct in range(8):
            ewo = p.dma("gpsimd", wbf[0][:],
                        waps["m_wo"][:, ct * 1024:(ct + 1) * 1024])
            mm = None
            for hp in range(8):
                mm = p.emit("tensor", lambda t, hp=hp, ct=ct: t.matmul(
                    ps_a[:, 0:512], wbf[0][:, hp * 128:(hp + 1) * 128],
                    ut[:, hp * QC:(hp + 1) * QC], start=(hp == 0),
                    stop=(hp == 7)), waits=[ewo], sig=(hp == 7))
            p.emit("vector", lambda v, ct=ct, q0=q0: v.tensor_add(
                ych[:, ct * 512:(ct + 1) * 512], ps_a[:, 0:512],
                xnb_t(ct, q0, q0 + QC)), waits=[mm], sig=True)
        sevs = ln_stats_chunk(
            p, lambda j: ych[:, j * 512:(j + 1) * 512],
            sqb, xcb, ps_a[0:1, 0:512], ps_b[0:1, 0:512], rsum, 0,
            ones_col, [], rsT=512)
        cc_eps = p.constcol(FEPS)[0:1, 0:1]
        cc_mh = p.constcol(-0.5)[0:1, 0:1]
        e1 = p.emit("vector", lambda v: v.tensor_scalar_mul(
            rw[0:1, 0:512], rsum[0:1, 512:1024], 1.0 / D), waits=sevs,
            sig=True)
        e2 = p.emit("scalar", lambda s: s.activation(
            rw[0:1, 512:1024], rw[0:1, 0:512], AF.Ln, bias=cc_eps),
            waits=[e1], sig=True)
        e3 = p.emit("scalar", lambda s: s.activation(
            rw[0:1, 0:512], rw[0:1, 512:1024], AF.Exp, scale=cc_mh),
            waits=[e2], sig=True)
        e4 = p.emit("vector", lambda v: v.tensor_copy(
            rowbf[0:1, 0:512], rw[0:1, 0:512]), waits=[e3], sig=True)
        eR = bcast_to_psum(p, ps_b[:, 0:512], rowbf[0:1, 0:512], ones_row,
                           [e4])
        for j in range(NDT):
            p.emit("vector", lambda v, j=j, q0=q0: v.scalar_tensor_tensor(
                x1b[:, j * T + q0:j * T + q0 + QC],
                ych[:, j * 512:(j + 1) * 512], W["m_n1w"][:, j:j + 1],
                ps_b[:, 0:512], ALU.mult, ALU.mult), waits=[eR], sig=True)
    return emit_l1ffn_moe(p, bufs, tabs, W, waps, xrow_in, xrow_out)
def emit_l1ffn_moe(p, bufs, tabs, W, waps, xrow_in, xrow_out):
    xs = bufs["xs"]
    xnb = bufs["xnb"]
    x1b = bufs["x1b"]
    scr16 = bufs["scr16"]
    wtok = bufs["wtok"]
    lgsb = bufs["lgsb"]
    gtt = bufs["gtt"]
    wbf = bufs["wbf"]
    sqb, xcb = bufs["sqb"], bufs["xcb"]
    tmf = bufs["tmf"]
    rsum, rw, rowbf = bufs["rsum"], bufs["rw"], bufs["rowbf"]
    ps_a, ps_b = bufs["ps_a"], bufs["ps_b"]
    ps_w, ps_d = bufs["ps_w"], bufs["ps_d"]
    ps_av = bufs["ps_av"]
    ps_m = bufs["ps_m"]
    ones_col = tabs["c_ones_col"]
    ones_row = tabs["c_ones_row"]
    identf = tabs["c_identf"]
    rsc, rsc2 = bufs["rsc"], bufs["rsc2"]
    swb = xnb[:, 0:8192]
    hb = xnb[:, 8192:16384]
    ych = xs[:, 0:4096]
    xrel = xs[:, 4096:8192]
    c1f = p.constcol(1.0)
    cc_half = p.constcol(0.5)
    cc_epsf = p.constcol(FEPS)[0:1, 0:1]
    cc_eps5 = p.constcol(1e-5)[0:1, 0:1]
    cc_mh = p.constcol(-0.5)[0:1, 0:1]

    for c0 in range(0, T, 512):
        mm = None
        for j in range(NDT):
            for (r0, r1) in ((0, 4), (32, 36), (64, 66)):
                gs = {0: 0, 32: 4, 64: 8}[r0]
                mm = p.emit("tensor", lambda t, j=j, c0=c0, r0=r0, r1=r1,
                            gs=gs: t.matmul(
                                ps_d[r0:r1, 0:512] if False else
                                ps_a[r0:r1, 0:512],
                                W["m_gate"][:, j * 66 + gs:j * 66 + gs +
                                            (r1 - r0)],
                                x1b[:, j * T + c0:j * T + c0 + 512],
                                start=(j == 0), stop=(j == NDT - 1),
                                skip_group_check=True),
                            sig=(j == NDT - 1 and r0 == 64))
        p.emit("vector", lambda v, c0=c0: v.tensor_scalar_add(
            lgsb[0:66, c0:c0 + 512], ps_a[0:66, 0:512],
            W["m_gbias"][0:66, 0:1]), waits=[mm], sig=True)
    for tt in range(16):
        et = p.emit("tensor", lambda t, tt=tt: t.transpose(
            ps_d[:, 0:66], lgsb[0:66, tt * 128:(tt + 1) * 128],
            identf[0:66, 0:66]), sig=True)
        ec1 = p.emit("vector", lambda v: v.tensor_copy(
            gtt[:, 0:4], ps_d[:, 0:4]), waits=[et], sig=True)
        ec2 = p.emit("vector", lambda v: v.tensor_copy(
            gtt[:, 4:8], ps_d[:, 32:36]), waits=[et], sig=True)
        ec3 = p.emit("vector", lambda v: v.tensor_copy(
            gtt[:, 8:10], ps_d[:, 64:66]), waits=[et], sig=True)
        ee = p.emit("scalar", lambda s: s.activation(
            gtt[:, 10:20], gtt[:, 0:10], AF.Exp), waits=[ec1, ec2, ec3],
            sig=True)
        ex = lambda i: gtt[:, 10 + i:11 + i]
        g = "gpsimd"
        e1 = p.emit(g, lambda gg: gg.tensor_add(gtt[:, 20:21], ex(0), ex(1)),
                    waits=[ee], sig=True)
        e1 = p.emit(g, lambda gg: gg.tensor_add(gtt[:, 20:21], gtt[:, 20:21],
                                                ex(2)), waits=[e1], sig=True)
        e1 = p.emit(g, lambda gg: gg.tensor_add(gtt[:, 20:21], gtt[:, 20:21],
                                                ex(3)), waits=[e1], sig=True)
        e2 = p.emit(g, lambda gg: gg.tensor_add(gtt[:, 21:22], ex(4), ex(5)),
                    waits=[ee], sig=True)
        e2 = p.emit(g, lambda gg: gg.tensor_add(gtt[:, 21:22], gtt[:, 21:22],
                                                ex(6)), waits=[e2], sig=True)
        e2 = p.emit(g, lambda gg: gg.tensor_add(gtt[:, 21:22], gtt[:, 21:22],
                                                ex(7)), waits=[e2], sig=True)
        e3 = p.emit(g, lambda gg: gg.tensor_sub(gtt[:, 22:23], gtt[:, 9:10],
                                                gtt[:, 8:9]), sig=True)
        e4 = p.emit("scalar", lambda s: s.sign(gtt[:, 23:24], gtt[:, 22:23]),
                    waits=[e3], sig=True)
        e5 = p.emit(g, lambda gg: gg.tensor_mul(gtt[:, 23:24], gtt[:, 23:24],
                                                cc_half[:, 0:1]), waits=[e4],
                    sig=True)
        e5 = p.emit(g, lambda gg: gg.tensor_add(gtt[:, 23:24], gtt[:, 23:24],
                                                cc_half[:, 0:1]), waits=[e5],
                    sig=True)
        e6 = p.emit(g, lambda gg: gg.tensor_sub(gtt[:, 24:25], gtt[:, 21:22],
                                                gtt[:, 20:21]), waits=[e1, e2],
                    sig=True)
        e6 = p.emit(g, lambda gg: gg.tensor_mul(gtt[:, 24:25], gtt[:, 24:25],
                                                gtt[:, 23:24]), waits=[e6, e5],
                    sig=True)
        e6 = p.emit(g, lambda gg: gg.tensor_add(gtt[:, 24:25], gtt[:, 24:25],
                                                gtt[:, 20:21]), waits=[e6],
                    sig=True)
        e7 = p.emit(g, lambda gg: gg.tensor_sub(gtt[:, 25:26], ex(9 - 1),
                                                ex(8 - 0)), waits=[ee],
                    sig=True)
        e7 = p.emit(g, lambda gg: gg.tensor_mul(gtt[:, 25:26], gtt[:, 25:26],
                                                gtt[:, 23:24]), waits=[e7, e5],
                    sig=True)
        e7 = p.emit(g, lambda gg: gg.tensor_add(gtt[:, 25:26], gtt[:, 25:26],
                                                ex(8)), waits=[e7], sig=True)
        e8 = p.emit(g, lambda gg: gg.tensor_add(gtt[:, 26:27], ex(8), ex(9)),
                    waits=[ee], sig=True)
        e9 = p.emit(g, lambda gg: gg.tensor_mul(gtt[:, 27:28], gtt[:, 24:25],
                                                gtt[:, 26:27]), waits=[e6, e8],
                    sig=True)
        e10 = p.emit("vector", lambda v: v.reciprocal(
            gtt[:, 28:29], gtt[:, 27:28]), waits=[e9], sig=True)
        e11 = p.emit(g, lambda gg: gg.tensor_mul(gtt[:, 29:30], gtt[:, 25:26],
                                                 gtt[:, 28:29]),
                     waits=[e10, e7], sig=True)
        e12 = p.emit(g, lambda gg: gg.tensor_sub(gtt[:, 30:31], c1f[:, 0:1],
                                                 gtt[:, 23:24]), waits=[e5],
                     sig=True)
        e12 = p.emit(g, lambda gg: gg.tensor_mul(gtt[:, 30:31], gtt[:, 30:31],
                                                 gtt[:, 29:30]),
                     waits=[e12, e11], sig=True)
        e13 = p.emit(g, lambda gg: gg.tensor_mul(gtt[:, 31:32], gtt[:, 23:24],
                                                 gtt[:, 29:30]),
                     waits=[e11], sig=True)
        e14 = p.emit("vector", lambda v: v.tensor_scalar_mul(
            gtt[:, 0:4], gtt[:, 10:14], gtt[:, 30:31]), waits=[e12], sig=True)
        e15 = p.emit("vector", lambda v: v.tensor_scalar_mul(
            gtt[:, 4:8], gtt[:, 14:18], gtt[:, 31:32]), waits=[e13], sig=True)
        e16 = p.emit("vector", lambda v: v.max(
            rsc[:, 0:8].bitcast(F32)[:, 0:4] if False else gtt[:, 10:18],
            gtt[:, 0:8]), waits=[e14, e15], sig=True)
        e17 = p.emit(g, lambda gg: gg.tensor_copy(gtt[:, 18:19],
                                                  gtt[:, 11:12]),
                     waits=[e16], sig=True)
        e18 = p.emit("vector", lambda v: v.tensor_scalar(
            gtt[:, 20:28], gtt[:, 0:8], gtt[:, 18:19], None, ALU.is_ge),
            waits=[e17], sig=True)
        e19 = p.emit(g, lambda gg: gg.tensor_mul(gtt[:, 0:8], gtt[:, 0:8],
                                                 gtt[:, 20:28]), waits=[e18],
                     sig=True)
        p.emit(g, lambda gg, tt=tt: gg.tensor_copy(
            wtok[:, tt * 8:(tt + 1) * 8], gtt[:, 0:8]), waits=[e19], sig=True)

    st_evs = []
    for tc in range(4):
        t0 = tc * 512
        for wname, dest in (("m_wsi", swb), ("m_w1s", hb)):
            with p.relaxed():
                ra = [None, None]
                rb = [None, None]
                rsil = [None, None]
                rmul = [None, None]
                for i in range(16):
                    bi = i % 2
                    ew1 = p.dma("gpsimd", wbf[0][:] if bi == 0 else
                                bufs["wbf3"][:],
                                waps[wname][:, i * 1024:(i + 1) * 1024],
                                waits=[ra[bi]])
                    ew2 = p.dma("gpsimd", bufs["wbf2"][:] if bi == 0 else
                                bufs["wbf4"][:],
                                waps[wname][:, (i + 16) * 1024:(i + 17) * 1024],
                                waits=[rb[bi]])
                    wA = wbf[0] if bi == 0 else bufs["wbf3"]
                    wB = bufs["wbf2"] if bi == 0 else bufs["wbf4"]
                    psA = ps_a if bi == 0 else ps_av[0]
                    psB = ps_w if bi == 0 else ps_av[1]
                    mma = None
                    for j in range(NDT):
                        mma = p.emit("tensor", lambda t, j=j, t0=t0, wA=wA,
                                     psA=psA: t.matmul(
                                         psA[:, 0:512],
                                         wA[:, j * 128:(j + 1) * 128],
                                         x1b[:, j * T + t0:j * T + t0 + 512],
                                         start=(j == 0), stop=(j == NDT - 1)),
                                     waits=[ew1, rsil[bi]],
                                     sig=(j == NDT - 1))
                    ra[bi] = mma
                    mmb = None
                    for j in range(NDT):
                        mmb = p.emit("tensor", lambda t, j=j, t0=t0, wB=wB,
                                     psB=psB: t.matmul(
                                         psB[:, 0:512],
                                         wB[:, j * 128:(j + 1) * 128],
                                         x1b[:, j * T + t0:j * T + t0 + 512],
                                         start=(j == 0), stop=(j == NDT - 1)),
                                     waits=[ew2, rmul[bi]],
                                     sig=(j == NDT - 1))
                    rb[bi] = mmb
                    rs = rsc if bi == 0 else rsc2
                    es = p.emit("scalar", lambda s, rs=rs, psA=psA:
                                s.activation(rs[:], psA[:, 0:512], AF.Silu),
                                waits=[mma, rmul[bi]], sig=True)
                    rsil[bi] = es
                    em = p.emit("vector", lambda v, i=i, dest=dest, rs=rs,
                                psB=psB: v.tensor_mul(
                                    dest[:, i * 512:(i + 1) * 512], rs[:],
                                    psB[:, 0:512]), waits=[es, mmb], sig=True)
                    rmul[bi] = em
        pso = [ps_a, ps_b, ps_w, ps_m]
        for ei in range(9):
            hsrc = swb if ei == 0 else hb
            for dh in range(2):
                with p.relaxed():
                    wrd = [None, None]
                    mm = None
                    for f in range(16):
                        bi = f % 2
                        wb_ = wbf[0] if bi == 0 else bufs["wbf2"]
                        if ei == 0:
                            ewo = p.dma("gpsimd", wb_[:, 0:1024],
                                        waps["m_wso"][f * 128:(f + 1) * 128, :],
                                        waits=[wrd[bi]])
                        else:
                            ewo = p.dma("gpsimd", wb_[:, 0:1024],
                                        waps["m_w2e"][ei - 1,
                                                      f * 128:(f + 1) * 128, :],
                                        waits=[wrd[bi]])
                        for ttl in range(4):
                            mm = p.emit("tensor", lambda t, f=f, ttl=ttl,
                                        dh=dh, hsrc=hsrc, wb_=wb_: t.matmul(
                                            pso[ttl][:, 0:512],
                                            hsrc[:, f * 512 + ttl * 128:
                                                 f * 512 + (ttl + 1) * 128],
                                            wb_[:, dh * 512:(dh + 1) * 512],
                                            start=(f == 0), stop=(f == 15),
                                            skip_group_check=True),
                                        waits=[ewo],
                                        sig=(ttl == 3))
                        wrd[bi] = mm
                    for ttl in range(4):
                        dst = ych[:, ttl * 1024 + dh * 512:
                                  ttl * 1024 + (dh + 1) * 512]
                        if ei == 0:
                            p.emit("vector", lambda v, ttl=ttl, dst=dst:
                                   v.tensor_copy(dst, pso[ttl][:, 0:512]),
                                   waits=[mm], sig=True)
                        else:
                            wcol = wtok[:, (tc * 4 + ttl) * 8 + ei - 1:
                                        (tc * 4 + ttl) * 8 + ei]
                            p.emit("vector", lambda v, ttl=ttl, dst=dst,
                                   wcol=wcol: v.scalar_tensor_tensor(
                                       dst, pso[ttl][:, 0:512], wcol, dst,
                                       ALU.mult, ALU.add), waits=[mm],
                                   sig=True)
        ffnT = bufs["ffnT"]
        for ttl in range(4):
            for dj in range(NDT):
                et = p.emit("tensor", lambda t, ttl=ttl, dj=dj: t.transpose(
                    ps_d[:, 0:128],
                    ych[:, ttl * 1024 + dj * 128:ttl * 1024 + (dj + 1) * 128],
                    tabs["c_identf"][:]), sig=True)
                e1 = p.emit("vector", lambda v, ttl=ttl, dj=dj: v.tensor_copy(
                    ffnT[:, dj * 512 + ttl * 128:dj * 512 + (ttl + 1) * 128],
                    ps_d[:, 0:128]), waits=[et], sig=True)
        for j in range(NDT):
            p.emit("vector", lambda v, j=j, t0=t0: v.tensor_add(
                xrel[:, j * 512:(j + 1) * 512],
                ffnT[:, j * 512:(j + 1) * 512],
                x1b[:, j * T + t0:j * T + t0 + 512]), sig=True)
        sevs = ln_stats_chunk(
            p, lambda j: xrel[:, j * 512:(j + 1) * 512],
            sqb, xcb, ps_a[0:1, 0:512], ps_b[0:1, 0:512], rsum, 0,
            ones_col, [], rsT=512)
        e1 = p.emit("vector", lambda v: v.tensor_scalar_mul(
            rw[0:1, 0:512], rsum[0:1, 512:1024], 1.0 / D), waits=sevs,
            sig=True)
        e2 = p.emit("scalar", lambda s: s.activation(
            rw[0:1, 512:1024], rw[0:1, 0:512], AF.Ln, bias=cc_epsf),
            waits=[e1], sig=True)
        e3 = p.emit("scalar", lambda s: s.activation(
            rw[0:1, 0:512], rw[0:1, 512:1024], AF.Exp, scale=cc_mh),
            waits=[e2], sig=True)
        e4 = p.emit("vector", lambda v: v.tensor_copy(
            rowbf[0:1, 0:512], rw[0:1, 0:512]), waits=[e3], sig=True)
        eR = bcast_to_psum(p, ps_b[:, 0:512], rowbf[0:1, 0:512], ones_row,
                           [e4])
        for j in range(NDT):
            p.emit("vector", lambda v, j=j: v.scalar_tensor_tensor(
                ych[:, j * 512:(j + 1) * 512], xrel[:, j * 512:(j + 1) * 512],
                W["m_n2w"][:, j:j + 1], ps_b[:, 0:512], ALU.mult, ALU.mult),
                waits=[eR], sig=True)
        sevs = ln_stats_chunk(
            p, lambda j: ych[:, j * 512:(j + 1) * 512],
            sqb, xcb, ps_a[0:1, 0:512], ps_b[0:1, 0:512], rsum, 0,
            ones_col, [], rsT=512)
        revs = ln_rows(p, rsum, rw, rowbf, sevs, c0=0, c1=512, rsT=512,
                       rbc0=0, rbT=1024)
        eldx = p.dma("sync", xrel[:].rearrange("p (j t) -> p j t", j=NDT),
                     xrow_in.rearrange("(j p) t -> p j t", p=128)[
                         :, :, t0:t0 + 512])
        eR = bcast_to_psum(p, ps_a[:, 0:512], rowbf[0:1, 0:512],
                           ones_row, revs)
        eA = bcast_to_psum(p, ps_b[:, 0:512], rowbf[0:1, 1024:1536],
                           ones_row, revs)
        for j in range(NDT):
            tmp = tmf[j % 2]
            e1 = p.emit("vector", lambda v, j=j, tmp=tmp: v.tensor_mul(
                tmp[:, 0:512], ych[:, j * 512:(j + 1) * 512], ps_a[:, 0:512]),
                waits=[eR], sig=True)
            e2 = p.emit("vector", lambda v, tmp=tmp: v.tensor_add(
                tmp[:, 0:512], tmp[:, 0:512], ps_b[:, 0:512]),
                waits=[e1, eA], sig=True)
            e3 = p.emit("vector", lambda v, j=j, tmp=tmp: v.tensor_add(
                xrel[:, j * 512:(j + 1) * 512], xrel[:, j * 512:(j + 1) * 512],
                tmp[:, 0:512]), waits=[e2, eldx], sig=True)
            st_evs.append(p.dma(
                "scalar",
                xrow_out.rearrange("(j p) t -> p j t", p=128)[
                    :, j, t0:t0 + 512],
                xrel[:, j * 512:(j + 1) * 512], waits=[e3]))
    return st_evs


def alloc_l1ffn_bufs(p):
    bufs = {}
    bufs["xs"] = p.sbuf("xs", [128, NDT * CH], F32)
    bufs["xnb"] = p.sbuf("xnb", [128, NDT * T], BF16)
    bufs["x1b"] = p.sbuf("x1b", [128, NDT * T], BF16)
    bufs["ktb"] = p.sbuf("ktb", [128, NDT * T], BF16)
    bufs["vaug"] = p.sbuf("vaug", [128, 16 * 16 * 65], BF16)
    bufs["cb"] = p.sbuf("cb", [128, 2 * T], BF16)
    bufs["lgsb"] = bufs["cb"][:].bitcast(F32)[0:66, 0:2048]
    bufs["scr16"] = p.sbuf("scr16", [128, 8192], BF16)
    s16 = bufs["scr16"]
    bufs["rsc"] = s16[:, 0:512]
    bufs["rsc2"] = s16[:, 512:1024]
    bufs["pbuf"] = [s16[:, 1024 + i * 512:1024 + (i + 1) * 512]
                    for i in range(2)]
    bufs["recb"] = s16[:, 2048:2560]
    bufs["ffnT"] = s16[:, 0:4096]
    bufs["sqb"] = [s16[:, 4096 + i * 512:4096 + (i + 1) * 512]
                   for i in range(2)]
    bufs["xcb"] = [s16[:, 5120 + i * 512:5120 + (i + 1) * 512]
                   for i in range(2)]
    _t32 = s16[:, 6144:8192].bitcast(F32)
    bufs["tmf"] = [_t32[:, 0:512], _t32[:, 512:1024]]
    bufs["wtok"] = p.sbuf("wtok", [128, 128], F32)
    bufs["gtt"] = p.sbuf("gtt", [128, 32], F32)
    _wbf = p.sbuf("wbf", [128, 1024], BF16)
    bufs["wbf"] = [_wbf, _wbf]
    bufs["wbf2"] = p.sbuf("wbf2", [128, 1024], BF16)
    bufs["wbf3"] = s16[:, 1024:2048]
    bufs["wbf4"] = s16[:, 2560:3584]
    bufs["rsum"] = p.sbuf("rsum", [1, 1024], F32)
    _cbf = bufs["cb"][:].bitcast(F32)
    bufs["rw"] = _cbf[0:1, 0:1536]
    bufs["rrow"] = _cbf[32:33, 0:1024]
    bufs["rowbf"] = bufs["cb"][64:65, 0:2048]
    bufs["ps_a"] = p.psum("ps_a", [128, 512])
    bufs["ps_b"] = p.psum("ps_b", [128, 512])
    bufs["ps_w"] = p.psum("ps_w", [128, 512])
    bufs["ps_d"] = p.psum("ps_d", [128, 128])
    bufs["ps_av"] = [p.psum(f"ps_av{i}", [128, 512]) for i in range(2)]
    bufs["ps_m"] = p.psum("ps_m", [128, 512])
    return bufs


def emit_block_l1ffn(p, xin, xout):
    waps = {
        "m_wq": p.dram_in("m_wq", [128, 8192], F32),
        "m_wkc": p.dram_in("m_wkc", [128, 2048], F32),
        "m_wvc": p.dram_in("m_wvc", [128, 2048], F32),
        "m_wo": p.dram_in("m_wo", [128, 8192], F32),
        "m_wv": p.dram_in("m_wv", [256, 1024], F32),
        "m_wsi": p.dram_in("m_wsi", [128, 32 * 1024], F32),
        "m_w1s": p.dram_in("m_w1s", [128, 32 * 1024], F32),
        "m_wso": p.dram_in("m_wso", [2048, 1024], F32),
        "m_w2e": p.dram_in("m_w2e", [8, 2048, 1024], F32),
    }
    tabs_np = host_tables()
    del tabs_np["c_onesf"]
    tabs_np.update(l1ffn_tables())
    tabs, tev = declare_tables(p, tabs_np)
    Wsb, wev = declare_weights(p, L1FFN_WEIGHT_DECLS)
    bufs = alloc_l1ffn_bufs(p)
    st_evs = []
    for r in range(ROWS_PER_CORE):
        st_evs += emit_l1ffn_row(p, bufs, tabs, Wsb, waps, xin[r], xout[r])
    return st_evs


def build_l1ffn_program(serial=True, dbg=False):
    p = Prog(serial=serial)
    p.dbg = dbg
    xin = p.dram_in("xT", [ROWS_PER_CORE, D, T], F32)
    xout = p.dram_out("yT", [ROWS_PER_CORE, D, T], F32)
    st_evs = emit_block_l1ffn(p, xin, xout)
    nc = p.finalize(final_waits=st_evs)
    return p, nc


def run_l1ffn(x, inputs, serial=True):
    shards = _shard_x(x)
    tabs = host_tables()
    del tabs["c_onesf"]
    tabs.update(l1ffn_tables())
    wprep = prep_l1ffn(inputs)
    p, nc = build_l1ffn_program(serial=serial)
    in_maps = []
    for c in range(NCORES):
        m = {"xT": shards[c], "m_w2e": inputs["L1_ffn_W2_experts"]}
        m.update(tabs)
        m.update({k: np.ascontiguousarray(v, dtype=np.float32)
                  for k, v in wprep.items()})
        in_maps.append(m)
    res = run_bass_kernel_spmd(nc, in_maps, core_ids=list(range(NCORES)))
    p.close()
    out = np.concatenate([r["yT"] for r in res.results], axis=0)
    return np.ascontiguousarray(np.transpose(out, (0, 2, 1)))



def _shard_x(x):
    xT = np.ascontiguousarray(np.transpose(x, (0, 2, 1)))
    return [xT[c * ROWS_PER_CORE:(c + 1) * ROWS_PER_CORE] for c in range(NCORES)]


def run_l0attn(x, inputs, serial=True):
    shards = _shard_x(x)
    tabs = host_tables()
    wprep = prep_l0attn(inputs)
    p, nc = build_l0attn_program(serial=serial)
    in_maps = []
    for c in range(NCORES):
        m = {"xT": shards[c]}
        m.update(tabs)
        m.update({k: np.ascontiguousarray(v, dtype=np.float32)
                  for k, v in wprep.items()})
        in_maps.append(m)
    res = run_bass_kernel_spmd(nc, in_maps, core_ids=list(range(NCORES)))
    p.close()
    out = np.concatenate([r["yT"] for r in res.results], axis=0)
    return np.ascontiguousarray(np.transpose(out, (0, 2, 1)))


def build_fused_program(serial=True):
    p = Prog(serial=serial)
    xin = p.dram_in("xT", [ROWS_PER_CORE, D, T], F32)
    xout = p.dram_out("yT", [ROWS_PER_CORE, D, T], F32)
    xm = [p.nc.dram_tensor(f"xmid{i}", [ROWS_PER_CORE, D, T], F32,
                           kind="Internal").ap() for i in range(3)]
    p._ensure_consts()
    plan = [(emit_block_l0attn, xin, xm[0]),
            (emit_block_l0ffn, xm[0], xm[1]),
            (emit_block_l1attn, xm[1], xm[2]),
            (emit_block_l1ffn, xm[2], xout)]
    st_evs = []
    for fn, a, b in plan:
        p.push_scope()
        st_evs = fn(p, a, b)
        p.pop_scope()
    nc = p.finalize(final_waits=st_evs)
    return p, nc


def fused_in_map(inputs, shard):
    m = {"xT": shard,
         "f_w1": inputs["L0_ffn_w1"], "f_w2": inputs["L0_ffn_w2"],
         "m_w2e": inputs["L1_ffn_W2_experts"]}
    tabs = host_tables()
    tabs.update(l1attn_tables())
    tabs.update(l1ffn_tables())
    m.update(tabs)
    for prep in (prep_l0attn, prep_l0ffn, prep_l1attn, prep_l1ffn):
        m.update({k: np.ascontiguousarray(v, dtype=np.float32)
                  for k, v in prep(inputs).items()})
    return m


def kernel(**inputs):
    x = inputs["L0_attn_x"]
    shards = _shard_x(x)
    p, nc = build_fused_program(serial=True)
    in_maps = [fused_in_map(inputs, shards[c]) for c in range(NCORES)]
    res = run_bass_kernel_spmd(nc, in_maps, core_ids=list(range(NCORES)))
    p.close()
    out = np.concatenate([r["yT"] for r in res.results], axis=0)
    return np.ascontiguousarray(np.transpose(out, (0, 2, 1)))
